# version 17
# baseline (speedup 1.0000x reference)
import numpy as np
import ml_dtypes
import concourse.bass as bass
import concourse.mybir as mybir
from concourse.bass_utils import run_bass_kernel_spmd
from concourse.mybir import AluOpType as ALU, ActivationFunctionType as AF
from concourse.bass_core_rust import IndirectOffsetOnAxis

F32 = mybir.dt.float32
BF16 = mybir.dt.bfloat16
I32 = mybir.dt.int32
FP8 = mybir.dt.float8e4

P = 128
D = 1024
NCH = D // P
TC = 512
T = 2048
NTC = T // TC
S = 2 * T
MAGIC = 12582912.0
NBF = ml_dtypes.bfloat16
X = mybir.AxisListType.X
KB = 1024
GW = 65
VB = 16 * GW
AGW = (T // P) * VB


def _dft_consts():
    e = np.arange(D)[:, None].astype(np.float64)
    C = np.zeros((D, D), np.float64)
    f_re = np.arange(512)[None, :].astype(np.float64)
    C[:, 0:512] = np.cos(2 * np.pi * e * f_re / D)
    C[:, 512] = np.cos(np.pi * e[:, 0])
    f_im = np.arange(1, 512)[None, :].astype(np.float64)
    C[:, 513:] = -np.sin(2 * np.pi * e * f_im / D)
    d = np.arange(D)[None, :].astype(np.float64)
    IR = np.zeros((D, D), np.float64)
    IR[0, :] = 1.0 / D
    IR[512, :] = np.cos(np.pi * d[0]) / D
    fr = np.arange(1, 512)[:, None].astype(np.float64)
    IR[1:512, :] = 2 * np.cos(2 * np.pi * d * fr / D) / D
    IR[513:, :] = -2 * np.sin(2 * np.pi * d * fr / D) / D
    return C.astype(np.float32), IR.astype(np.float32)


_CONSTS = None


def get_consts():
    global _CONSTS
    if _CONSTS is None:
        C, IR = _dft_consts()
        _CONSTS = (C.astype(NBF), IR.astype(NBF))
    return _CONSTS


def fm(a):
    Dd, Tt = a.shape
    n = Dd // P
    return np.ascontiguousarray(a.reshape(n, P, Tt).transpose(1, 0, 2).reshape(P, n * Tt))


def fm_inv(a, Tt):
    n = a.shape[1] // Tt
    return np.ascontiguousarray(a.reshape(P, n, Tt).transpose(1, 0, 2).reshape(n * P, Tt))


def vcol(a):
    return np.ascontiguousarray(np.asarray(a).reshape(NCH, P).T)


def tmaj(a):
    Tt, Dd = a.shape
    n = Tt // P
    return np.ascontiguousarray(a.reshape(n, P, Dd).transpose(1, 0, 2).reshape(P, n * Dd))


class Cnt:
    __slots__ = ("h", "n")

    def __init__(self, h):
        self.h = h
        self.n = 0


class Bld:
    def __init__(self, nc):
        self.nc = nc
        self.q = {e: [] for e in ("tensor", "vector", "scalar", "sync", "gpsimd")}
        self._si = 0
        self.ch = {e: self.sem("ch_" + e) for e in ("vector", "scalar", "gpsimd")}

    pfx = ""

    def sem(self, name=None):
        self._si += 1
        return Cnt(self.nc.alloc_semaphore(self.pfx + (name or ("s%d" % self._si))))

    def sb(self, name, shape, dtype, off):
        return self.nc.alloc_sbuf_tensor_at(self.pfx + name, shape, dtype,
                                            offset=off + 17408)

    def op(self, eng, fn, waits=(), incs=()):
        waits = tuple(waits) + tuple((c, c.n) for c, k in incs if k == 16)
        for c, k in incs:
            c.n += k
        W = tuple((c.h, v) for c, v in waits)
        I = tuple((c.h, k) for c, k in incs)

        def run(e):
            for h, v in W:
                e.wait_ge(h, v)
            ins = fn(e)
            if ins is not None:
                for h, k in I:
                    ins = ins.then_inc(h, k)
        self.q[eng].append(run)

    def chain(self, eng, fn, waits=()):
        c = self.ch[eng]
        self.op(eng, fn, waits=tuple(waits) + ((c, c.n),), incs=((c, 1),))
        return c.n

    def dma(self, eng, fn, sem, waits=()):
        self.op(eng, fn, waits=tuple(waits) + ((sem, sem.n),), incs=((sem, 16),))
        return sem.n

    def vw(self):
        return (self.ch["vector"], self.ch["vector"].n)

    def finalize(self):
        nc = self.nc
        q = self.q
        with nc.Block() as block:
            @block.tensor
            def _(e):
                for f in q["tensor"]:
                    f(e)

            @block.vector
            def _(e):
                for f in q["vector"]:
                    f(e)

            @block.scalar
            def _(e):
                for f in q["scalar"]:
                    f(e)

            @block.sync
            def _(e):
                for f in q["sync"]:
                    f(e)

            @block.gpsimd
            def _(e):
                for f in q["gpsimd"]:
                    f(e)
        return nc


def r3(t, inner):
    return t[:, :].rearrange("p (c t) -> p c t", t=inner)


def slc3(t, total, tc):
    return r3(t, total)[:, :, tc * TC:(tc + 1) * TC]


def bc3(ps):
    return ps[:, :].rearrange("p (u t) -> p u t", u=1).broadcast_to([P, NCH, TC])


class Fx:

    def __init__(self, fused=False):
        self.nc = bass.Bass("TRN2", target_bir_lowering=False)
        self.b = Bld(self.nc)
        self.fused = fused
        self._dram = {}
        nc = self.nc
        self.PA = nc.alloc_psum_tensor("PA", [P, 2 * TC], F32)[:, :]
        self.PB = nc.alloc_psum_tensor("PB", [P, 2 * TC], F32)[:, :]
        self.PC = nc.alloc_psum_tensor("PC", [P, 2 * TC], F32)[:, :]
        self.PD = nc.alloc_psum_tensor("PD", [P, 2 * TC], F32)[:, :]
        self.ext_spref = None
        self.ag = None
        self.force_out = set()

    def dram(self, name, shape, dtype, role):
        if name in self._dram:
            return self._dram[name]
        if name in self.force_out:
            kind = "ExternalOutput"
        elif role == "win" or (role == "in" and not self.fused):
            kind = "ExternalInput"
        elif role == "result" or (role == "out" and not self.fused):
            kind = "ExternalOutput"
        else:
            kind = "Internal"
        ap = self.nc.dram_tensor(name, shape, dtype, kind=kind).ap()
        self._dram[name] = ap
        return ap


_BARRIER_N = [0]


def emit_barrier(fx, dma_sems=()):
    b = fx.b
    _BARRIER_N[0] += 1
    ph = b.sem("ph%d" % _BARRIER_N[0])
    engs = ("tensor", "vector", "scalar", "sync", "gpsimd")
    for eng in engs:
        w = tuple((s, s.n) for s in dma_sems) if eng == "sync" else ()
        b.op(eng, lambda e: e.nop(), waits=w, incs=((ph, 1),))
    for eng in engs:
        b.op(eng, lambda e, v=len(engs): e.wait_ge(ph.h, v))


def std_env(b, nc, o, fx, pre=None):
    env = {}
    env["sxb"] = b.sb("sxb", [P, NCH * TC], BF16, o); o += NCH * TC * 2
    env["sxx"] = b.sb("sxx", [P, NCH * TC], BF16, o); o += NCH * TC * 2
    env["m_row"] = b.sb("m_row", [1, TC], F32, o); o += TC * 4
    env["r_row"] = b.sb("r_row", [1, TC], F32, o); o += TC * 4
    env["ones_bf"] = b.sb("ones_bf", [P, 1], BF16, o); o += 32
    env["ones_row"] = b.sb("ones_row", [1, P], F32, o); o += P * 4
    env["eps_c"] = b.sb("eps_c", [1, 1], F32, o); o += 32
    env["end"] = o
    env["psum_sum"] = fx.PA[:, 0:TC]
    env["psum_sq"] = fx.PA[:, TC:2 * TC]
    env["pbc_m"] = fx.PB[:, 0:TC]
    env["pbc_r"] = fx.PB[:, TC:2 * TC]
    env["pk0"] = fx.PC[:, 0:TC]
    env["pk1"] = fx.PC[:, TC:2 * TC]
    env["tq"] = b.sem("tq")
    env["lds"] = (pre["lds"] if pre else b.sem("lds"))
    env["ldg"] = b.sem("ldg")
    env["stq"] = b.sem("stq")
    b.chain("vector", lambda e: e.memset(env["ones_bf"][:, :], 1.0))
    b.chain("vector", lambda e: e.memset(env["ones_row"][:, :], 1.0))
    b.chain("vector", lambda e: e.memset(env["eps_c"][:, :], 1e-5))
    return env


def emit_ln_tc(b, env, src3, scale_ap=None):
    tq = env["tq"]
    sxb, sxx = env["sxb"], env["sxx"]
    vb = b.chain("vector", lambda e: e.tensor_copy(r3(sxb, TC), src3))
    vx = b.chain("vector", lambda e: e.tensor_mul(r3(sxx, TC), r3(sxb, TC), r3(sxb, TC)))
    for ch in range(NCH):
        b.op("tensor",
             (lambda e, ch=ch: e.matmul(env["psum_sum"][0:1, :], env["ones_bf"][:, :],
                                        sxb[:, ch * TC:(ch + 1) * TC],
                                        start=(ch == 0), stop=(ch == NCH - 1))),
             waits=((b.ch["vector"], vb),) if ch == 0 else ())
    for ch in range(NCH):
        b.op("tensor",
             (lambda e, ch=ch: e.matmul(env["psum_sq"][0:1, :], env["ones_bf"][:, :],
                                        sxx[:, ch * TC:(ch + 1) * TC],
                                        start=(ch == 0), stop=(ch == NCH - 1))),
             waits=((b.ch["vector"], vx),) if ch == 0 else (),
             incs=((tq, 1),) if ch == NCH - 1 else ())
    td = tq.n
    m, r = env["m_row"], env["r_row"]
    b.chain("vector", lambda e: e.tensor_scalar(m[0:1, :], env["psum_sum"][0:1, :],
                                                1.0 / 1024.0, None, ALU.mult),
            waits=((tq, td),))
    b.chain("vector", lambda e: e.tensor_mul(r[0:1, :], m[0:1, :], m[0:1, :]))
    vv = b.chain("vector", lambda e: e.scalar_tensor_tensor(
        r[0:1, :], env["psum_sq"][0:1, :], 1.0 / 1024.0, r[0:1, :],
        ALU.mult, ALU.subtract))
    sv = b.chain("scalar", lambda e: e.activation(r[0:1, :], r[0:1, :], AF.Sqrt,
                                                  bias=env["eps_c"][0:1, :]),
                 waits=((b.ch["vector"], vv),))
    vr = b.chain("vector", lambda e: e.reciprocal(r[0:1, :], r[0:1, :]),
                 waits=((b.ch["scalar"], sv),))
    if scale_ap is not None:
        vr = b.chain("vector", lambda e: e.tensor_scalar(
            r[0:1, :], r[0:1, :], scale_ap, None, ALU.mult))
    b.op("tensor", lambda e: e.matmul(env["pbc_m"][:, :], env["ones_row"][0:1, :],
                                      m[0:1, :], start=True, stop=True),
         waits=((b.ch["vector"], vr),))
    b.op("tensor", lambda e: e.matmul(env["pbc_r"][:, :], env["ones_row"][0:1, :],
                                      r[0:1, :], start=True, stop=True),
         incs=((tq, 1),))
    return tq.n


def emit_spectrum_norm(b, env, seva, sp0, sp1, stq2, stq3):
    H = 4 * TC
    re = seva[:, 0:H]
    im = seva[:, H:2 * H]
    for (blk, sp) in ((0, sp0), (1, sp1)):
        srcrow = seva[0:1, blk * H: blk * H + TC]
        vc = b.chain("vector", lambda e, srcrow=srcrow, sp=sp: e.tensor_copy(sp[0:1, 0:TC], srcrow))
        sa = b.chain("scalar", lambda e, srcrow=srcrow, sp=sp: e.activation(
            sp[0:1, TC:2 * TC], srcrow, AF.Abs), waits=((b.ch["vector"], vc),))
        b.chain("vector", lambda e, sp=sp: e.tensor_scalar(
            sp[0:1, TC:2 * TC], sp[0:1, TC:2 * TC], 1e-8, None, ALU.add),
            waits=((b.ch["scalar"], sa),))
        b.chain("vector", lambda e, sp=sp: e.reciprocal(sp[0:1, TC:2 * TC], sp[0:1, TC:2 * TC]))
        b.chain("vector", lambda e, sp=sp: e.tensor_mul(
            sp[0:1, 0:TC], sp[0:1, 0:TC], sp[0:1, TC:2 * TC]))
    s1 = b.chain("scalar", lambda e: e.activation(stq2[:, :], re, AF.Square),
                 waits=(b.vw(),))
    s2 = b.chain("scalar", lambda e: e.activation(stq3[:, :], im, AF.Square))
    b.chain("vector", lambda e: e.tensor_add(stq2[:, :], stq2[:, :], stq3[:, :]),
            waits=((b.ch["scalar"], s2),))
    s3 = b.chain("scalar", lambda e: e.activation(stq2[:, :], stq2[:, :], AF.Sqrt),
                 waits=(b.vw(),))
    b.chain("vector", lambda e: e.tensor_scalar(stq2[:, :], stq2[:, :], 1e-8, None, ALU.add),
            waits=((b.ch["scalar"], s3),))
    b.chain("vector", lambda e: e.reciprocal(stq2[:, :], stq2[:, :]))
    b.chain("vector", lambda e: e.tensor_mul(re, re, stq2[:, :]))
    b.chain("vector", lambda e: e.tensor_mul(im, im, stq2[:, :]))
    b.chain("vector", lambda e: e.tensor_copy(seva[0:1, 0:TC], sp0[0:1, 0:TC]))
    return b.chain("vector", lambda e: e.tensor_copy(seva[0:1, H:H + TC], sp1[0:1, 0:TC]))


def emit_cmul(b, are, aim, bre, bim, dre, dim, t1, conj_b, spec):
    b.chain("vector", lambda e: e.tensor_mul(t1, are, bre))
    b.chain("vector", lambda e: e.tensor_mul(dim, aim, bim))
    b.chain("vector", lambda e: e.tensor_tensor(dre, t1, dim,
                                                ALU.add if conj_b else ALU.subtract))
    b.chain("vector", lambda e: e.tensor_mul(t1, aim, bre))
    b.chain("vector", lambda e: e.tensor_mul(dim, are, bim))
    b.chain("vector", lambda e: e.tensor_tensor(dim, t1, dim,
                                                ALU.subtract if conj_b else ALU.add))
    a0, a5, b0, b5, d0, d5 = spec
    b.chain("vector", lambda e: e.tensor_mul(d0, a0, b0))
    return b.chain("vector", lambda e: e.tensor_mul(d5, a5, b5))


def emit_fuse(b, env, w_in, dst, war_tensor_cnt, swbuf, scb):
    tq, ldg = env["tq"], env["ldg"]
    b.op("gpsimd", lambda e: e.dma_start(
        out=r3(swbuf, D), in_=w_in.rearrange("(c p) t -> p c t", p=P)),
        waits=((tq, war_tensor_cnt), (b.ch["gpsimd"], b.ch["gpsimd"].n)),
        incs=((ldg, 16),))
    ld_target = ldg.n
    evac = {}
    for m in range(NCH):
        for half in range(2):
            key = (m * 2 + half) % 2
            ps = env["pk0"] if key == 0 else env["pk1"]
            for ech in range(NCH):
                w = []
                if m == 0 and half == 0 and ech == 0:
                    w += [(ldg, ld_target), (env["lds"], env["lds_need"]), b.vw()]
                if ech == 0 and key in evac:
                    w.append((b.ch["vector"], evac[key]))
                b.op("tensor",
                     (lambda e, m=m, half=half, ech=ech, ps=ps:
                      e.matmul(ps[:, :],
                               swbuf[:, ech * D + m * P: ech * D + (m + 1) * P],
                               scb[:, ech * D + half * TC: ech * D + half * TC + TC],
                               start=(ech == 0), stop=(ech == NCH - 1))),
                     waits=tuple(w),
                     incs=((tq, 1),) if ech == NCH - 1 else ())
            td = tq.n
            evac[key] = b.chain(
                "vector",
                (lambda e, m=m, half=half, ps=ps:
                 e.tensor_copy(dst[:, m * D + half * TC: m * D + half * TC + TC], ps[:, :])),
                waits=((tq, td),))
    return tq.n, b.ch["vector"].n


def emit_a1(fx):
    nc, b = fx.nc, fx.b
    b.pfx = "a1_"
    xin = fx.dram("xT", [D, T], F32, "win")
    wk_in = fx.dram("Wk", [D, D], F32, "win")
    wv_in = fx.dram("Wv", [D, D], F32, "win")
    c_in = fx.dram("Cb", [D, D], BF16, "win")

    o_fkv = fx.dram("o_fkv", [P, NCH * T], BF16, "out")
    o_ln1 = fx.dram("o_ln1", [P, NCH * T], BF16, "out")
    o_xmid = fx.dram("o_xmid", [P, NCH * T], F32, "out")
    o_tot = fx.dram("o_tot", [P, NCH], F32, "out")

    sfkv = b.sb("sfkv", [P, NCH * T], BF16, 0)
    sfusedA = b.sb("sfusedA", [P, NCH * D], BF16, 32 * KB)
    sfusedB = b.sb("sfusedB", [P, NCH * D], BF16, 48 * KB)
    scb = b.sb("scb", [P, NCH * D], BF16, 64 * KB)
    swbuf = b.sb("swbuf", [P, NCH * D], BF16, 80 * KB)
    sxt = [b.sb("sxt0", [P, NCH * TC], F32, 96 * KB),
           b.sb("sxt1", [P, NCH * TC], F32, 112 * KB)]
    slnt = b.sb("slnt", [P, NCH * TC], BF16, 128 * KB)
    seva = b.sb("seva", [P, NCH * TC], BF16, 136 * KB)
    sevb = b.sb("sevb", [P, NCH * TC], BF16, 144 * KB)
    stmp1 = b.sb("stmp1", [P, 4 * TC], F32, 152 * KB)
    stot = b.sb("stot", [P, NCH], F32, 160 * KB)
    sp0 = b.sb("sp0", [1, 2 * TC], F32, 161 * KB)
    sp1 = b.sb("sp1", [1, 2 * TC], F32, 165 * KB)
    stq2 = b.sb("stq2", [P, 4 * TC], F32, 169 * KB)
    stq3 = b.sb("stq3", [P, 4 * TC], F32, 177 * KB)
    env = std_env(b, nc, 185 * KB, fx)

    tq, lds, ldg, stq = env["tq"], env["lds"], env["ldg"], env["stq"]
    b.op("sync", lambda e: e.dma_start(out=r3(scb, D),
                                       in_=c_in.rearrange("(c p) t -> p c t", p=P)),
         incs=((lds, 16),))
    env["lds_need"] = 16
    tfK, vfK = emit_fuse(b, env, wk_in, sfusedA, 0, swbuf, scb)
    tfV, vfV = emit_fuse(b, env, wv_in, sfusedB, tfK, swbuf, scb)

    evac = {0: vfV, 1: vfV}
    xload = [0] * NTC
    tile_free_v = [0, 0]
    tile_free_s = [0, 0]
    for tc in range(min(2, NTC)):
        b.op("sync", lambda e, tc=tc: e.dma_start(
            out=r3(sxt[tc % 2], TC),
            in_=xin.rearrange("(c p) t -> p c t", p=P)[:, :, tc * TC:(tc + 1) * TC]),
            incs=((lds, 16),))
        xload[tc] = lds.n
    st_ln = 0
    for tc in range(NTC):
        c0 = tc * TC
        cur = sxt[tc % 2]
        b.op("vector", lambda e, v=xload[tc]: e.wait_ge(lds.h, v))
        b.op("vector", lambda e, v=st_ln: e.wait_ge(stq.h, v))
        tbc = emit_ln_tc(b, env, r3(cur, TC))
        b.chain("vector", lambda e, cur=cur: e.tensor_tensor(
            r3(slnt, TC), r3(cur, TC), bc3(env["pbc_m"]), ALU.subtract),
            waits=((tq, tbc),))
        vln = b.chain("vector", lambda e: e.tensor_tensor(
            r3(slnt, TC), r3(slnt, TC), bc3(env["pbc_r"]), ALU.mult))
        vres = b.chain("vector", lambda e, cur=cur: e.tensor_tensor(
            r3(cur, TC), r3(cur, TC), r3(slnt, TC), ALU.add))
        tile_free_v[tc % 2] = vres
        b.op("gpsimd", lambda e, tc=tc, cur=cur: e.dma_start(
            out=r3(o_xmid, T)[:, :, tc * TC:(tc + 1) * TC], in_=r3(cur, TC)),
            waits=((b.ch["vector"], vres), (b.ch["gpsimd"], b.ch["gpsimd"].n)),
            incs=((stq, 16),))
        tile_free_s[tc % 2] = stq.n
        b.op("gpsimd", lambda e, tc=tc: e.dma_start(
            out=r3(o_ln1, T)[:, :, tc * TC:(tc + 1) * TC], in_=r3(slnt, TC)),
            incs=((stq, 16),))
        st_ln = stq.n
        if tc + 2 < NTC:
            b.op("sync", lambda e, tc=tc: e.dma_start(
                out=r3(sxt[tc % 2], TC),
                in_=xin.rearrange("(c p) t -> p c t", p=P)[:, :, (tc + 2) * TC:(tc + 3) * TC]),
                waits=((b.ch["vector"], tile_free_v[tc % 2]), (stq, tile_free_s[tc % 2])),
                incs=((lds, 16),))
            xload[tc + 2] = lds.n
        for (dst, fmx) in ((seva, sfusedA), (sevb, sfusedB)):
            for fch in range(NCH):
                key = fch % 2
                ps = env["pk0"] if key == 0 else env["pk1"]
                for dch in range(NCH):
                    w = []
                    if dch == 0:
                        w.append((b.ch["vector"], max(vln, evac[key])))
                    b.op("tensor",
                         (lambda e, fch=fch, dch=dch, ps=ps, fmx=fmx:
                          e.matmul(ps[:, :],
                                   fmx[:, dch * D + fch * P: dch * D + (fch + 1) * P],
                                   slnt[:, dch * TC: (dch + 1) * TC],
                                   start=(dch == 0), stop=(dch == NCH - 1))),
                         waits=tuple(w),
                         incs=((tq, 1),) if dch == NCH - 1 else ())
                td = tq.n
                evac[key] = b.chain(
                    "vector",
                    (lambda e, dst=dst, fch=fch, ps=ps:
                     e.tensor_copy(dst[:, fch * TC:(fch + 1) * TC], ps[:, :])),
                    waits=((tq, td),))
        b.op("vector", lambda e, v=tq.n: e.wait_ge(tq.h, v))
        emit_spectrum_norm(b, env, seva, sp0, sp1, stq2, stq3)
        H = 4 * TC
        dre = r3(sfkv, T)[:, 0:4, c0:c0 + TC]
        dim = r3(sfkv, T)[:, 4:8, c0:c0 + TC]
        spec = (seva[0:1, 0:TC], seva[0:1, H:H + TC],
                sevb[0:1, 0:TC], sevb[0:1, H:H + TC],
                sfkv[0:1, c0:c0 + TC],
                sfkv[0:1, 4 * T + c0: 4 * T + c0 + TC])
        emit_cmul(b, r3(seva[:, 0:H], TC), r3(seva[:, H:2 * H], TC),
                  r3(sevb[:, 0:H], TC), r3(sevb[:, H:2 * H], TC),
                  dre, dim, r3(stmp1, TC), False, spec)
    for ch in range(NCH):
        b.chain("vector", lambda e, ch=ch: e.reduce_sum(
            stot[:, ch:ch + 1], sfkv[:, ch * T:(ch + 1) * T], X))
    vend = b.ch["vector"].n
    b.op("gpsimd", lambda e: e.dma_start(out=o_fkv, in_=sfkv[:, :]),
         waits=((b.ch["vector"], vend), (b.ch["gpsimd"], b.ch["gpsimd"].n)),
         incs=((stq, 16),))
    b.dma("gpsimd", lambda e: e.dma_start(out=o_tot, in_=stot[:, :]), stq)
    b.op("gpsimd", lambda e, v=stq.n: e.wait_ge(stq.h, v))
    return dict(stot=stot, env=env, stq=stq, lds=lds, ldg=ldg)


def build_a1():
    fx = Fx()
    emit_a1(fx)
    return fx.b.finalize()


def emit_a2(fx):
    nc, b = fx.nc, fx.b
    b.pfx = "a2_"
    fkv_in = fx.dram("o_fkv", [P, NCH * T], BF16, "in")
    ln1_in = fx.dram("o_ln1", [P, NCH * T], BF16, "in")
    xmid_in = fx.dram("o_xmid", [P, NCH * T], F32, "in")
    if not fx.fused:
        pref_in = fx.dram("prefix", [P, NCH], F32, "in")
    wq_in = fx.dram("Wq", [D, D], F32, "win")
    c_in = fx.dram("Cb", [D, D], BF16, "win")
    ir_in = fx.dram("IRb", [D, D], BF16, "win")
    tern_in = fx.dram("ternT", [D, D], I32, "win")
    l0b_in = fx.dram("l0bias", [P, NCH], F32, "win")
    asc_in = fx.dram("ascale", [1, 1], F32, "win")
    wvt_in = fx.dram("wvT", [D, D], F32, "win")
    wvb_in = fx.dram("wvb", [1, D], F32, "win")

    o_x2 = fx.dram("o_x2", [P, NCH * T], F32, "out")
    if fx.fused:
        o_ln3 = fx.ag_in[0:P, 0:NCH * T]
        o_vx = fx.ag_in[P:2 * P, :]
    else:
        o_ln3 = fx.dram("o_ln3", [P, NCH * T], BF16, "out")
        o_vx = fx.dram("o_vx", [P, (T // P) * D], BF16, "out")

    pre = getattr(fx, "pre", None)
    sx = pre["sx"] if pre else b.sb("sx", [P, NCH * T], F32, 0)
    scum = b.sb("scum", [P, NCH * T], BF16, 64 * KB)
    sxdq = b.sb("sxdq", [P, NCH * T], BF16, 64 * KB)
    sfkv = pre["sfkv"] if pre else b.sb("sfkv", [P, NCH * T], BF16, 96 * KB)
    stq2 = b.sb("stq2", [P, 4 * TC], F32, 96 * KB)
    stmp1 = b.sb("stmp1", [P, 4 * TC], F32, 104 * KB)
    stq3 = b.sb("stq3", [P, 4 * TC], F32, 112 * KB)
    sp0 = b.sb("sp0", [1, 2 * TC], F32, 120 * KB)
    sp1 = b.sb("sp1", [1, 2 * TC], F32, 122 * KB)
    sqt = b.sb("sqt", [P, NCH * TC], F32, 96 * KB)
    sint = b.sb("sint", [P, NCH * TC], I32, 112 * KB)
    sfusedQ = b.sb("sfusedQ", [P, NCH * D], BF16, 128 * KB)
    swt = b.sb("swt", [P, NCH * D], BF16, 128 * KB)
    sln3 = b.sb("sln3", [P, NCH * T], BF16, 128 * KB)
    sir = b.sb("sir", [P, NCH * D], BF16, 144 * KB)
    scb = b.sb("scb", [P, NCH * D], BF16, 160 * KB)
    slnt = b.sb("slnt", [P, NCH * TC], BF16, 160 * KB)
    sfev = b.sb("sfev", [P, TC], F32, 160 * KB)
    swvt = b.sb("swvt", [P, NCH * D], BF16, 160 * KB)
    seva = b.sb("seva", [P, NCH * TC], BF16, 168 * KB)
    swbuf = b.sb("swbuf", [P, NCH * D], BF16, 176 * KB)
    sh = b.sb("sh", [P, NCH * TC], BF16, 176 * KB)
    svxt = b.sb("svxt", [P, VB], BF16, 197 * KB)
    swvbb = b.sb("swvbb", [1, D], BF16, 199 * KB + 64)
    onesr_bf = b.sb("onesr_bf", [1, P], BF16, 201 * KB + 128)
    spref = b.sb("spref", [P, NCH], F32, 201 * KB + 512)
    sl0b = b.sb("sl0b", [P, NCH], F32, 202 * KB)
    sasc = b.sb("sasc", [1, 4], F32, 202 * KB + 512)
    sscol = b.sb("sscol", [P, 1], F32, 203 * KB)
    env = std_env(b, nc, 176 * KB, fx, pre=pre)
    tq, lds, ldg, stq = env["tq"], env["lds"], env["ldg"], env["stq"]
    env["lds_need"] = 0

    if not pre:
        b.dma("sync", lambda e: e.dma_start(out=sx[:, :], in_=xmid_in), lds)
        b.dma("sync", lambda e: e.dma_start(out=sfkv[:, :], in_=fkv_in), lds)
    b.op("sync", lambda e: e.dma_start(out=r3(scb, D),
                                       in_=c_in.rearrange("(c p) t -> p c t", p=P)),
         incs=((lds, 16),))
    b.op("sync", lambda e: e.dma_start(out=r3(sir, D),
                                       in_=ir_in.rearrange("(c p) t -> p c t", p=P)),
         incs=((lds, 16),))
    if fx.fused:
        spref = fx.ext_spref
    else:
        b.dma("sync", lambda e: e.dma_start(out=spref[:, :], in_=pref_in), lds)
    b.dma("sync", lambda e: e.dma_start(out=sl0b[:, :], in_=l0b_in), lds)
    b.dma("sync", lambda e: e.dma_start(out=sasc[0:1, 0:1], in_=asc_in), lds)
    b.op("gpsimd", lambda e: e.dma_start(out=swvbb[:, :], in_=wvb_in),
         incs=((ldg, 16),))
    ldwvb = ldg.n
    ld_base = lds.n

    env["lds_need"] = 48
    tfQ, vfQ = emit_fuse(b, env, wq_in, sfusedQ, 0, swbuf, scb)

    b.op("vector", lambda e: e.wait_ge(lds.h, ld_base))
    scan_done = 0
    for ch in range(NCH):
        scan_done = b.chain("vector", lambda e, ch=ch: e.tensor_tensor_scan(
            scum[:, ch * T:(ch + 1) * T],
            env["ones_bf"][:, :].broadcast_to([P, T]),
            sfkv[:, ch * T:(ch + 1) * T],
            spref[:, ch:ch + 1], ALU.mult, ALU.add))

    b.chain("vector", lambda e: e.tensor_scalar(sasc[0:1, 0:1], sasc[0:1, 0:1],
                                                1e-5, None, ALU.max))
    va = b.chain("vector", lambda e: e.reciprocal(sasc[0:1, 1:2], sasc[0:1, 0:1]))
    b.op("tensor", lambda e: e.matmul(env["pk0"][:, 0:1], env["ones_row"][0:1, :],
                                      sasc[0:1, 0:1], start=True, stop=True),
         waits=((b.ch["vector"], va),), incs=((tq, 1),))
    tscol = tq.n
    b.chain("vector", lambda e: e.tensor_copy(sscol[:, :], env["pk0"][:, 0:1]),
            waits=((tq, tscol),))
    vscol = b.ch["vector"].n
    b.op("vector", lambda e, v=ldwvb: e.wait_ge(ldg.h, v))
    b.chain("vector", lambda e: e.memset(onesr_bf[:, :], 1.0))

    evac = {0: b.ch["vector"].n, 1: b.ch["vector"].n}
    slnt_ld = [0] * NTC
    b.op("sync", lambda e: e.dma_start(
        out=r3(slnt, TC), in_=r3(ln1_in, T)[:, :, 0:TC]),
        waits=((tq, tfQ),), incs=((lds, 16),))
    slnt_ld[0] = lds.n
    qproj_done = 0
    for tc in range(NTC):
        c0 = tc * TC
        b.op("vector", lambda e, v=slnt_ld[tc]: e.wait_ge(lds.h, v))
        vgate = b.chain("vector", lambda e: e.engine_nop())
        for fch in range(NCH):
            key = fch % 2
            ps = env["pk0"] if key == 0 else env["pk1"]
            for dch in range(NCH):
                w = []
                if dch == 0:
                    w.append((b.ch["vector"], max(vgate, evac[key], vfQ)))
                b.op("tensor",
                     (lambda e, fch=fch, dch=dch, ps=ps:
                      e.matmul(ps[:, :],
                               sfusedQ[:, dch * D + fch * P: dch * D + (fch + 1) * P],
                               slnt[:, dch * TC: (dch + 1) * TC],
                               start=(dch == 0), stop=(dch == NCH - 1))),
                     waits=tuple(w),
                     incs=((tq, 1),) if dch == NCH - 1 else ())
            td = tq.n
            evac[key] = b.chain(
                "vector",
                (lambda e, fch=fch, ps=ps:
                 e.tensor_copy(seva[:, fch * TC:(fch + 1) * TC], ps[:, :])),
                waits=((tq, td),))
        qproj_done = tq.n
        if tc + 1 < NTC:
            b.op("sync", lambda e, tc=tc: e.dma_start(
                out=r3(slnt, TC), in_=r3(ln1_in, T)[:, :, (tc + 1) * TC:(tc + 2) * TC]),
                waits=((tq, qproj_done),), incs=((lds, 16),))
            slnt_ld[tc + 1] = lds.n
        emit_spectrum_norm(b, env, seva, sp0, sp1, stq2, stq3)
        H = 4 * TC
        dre = r3(sh, TC)[:, 0:4, :]
        dim = r3(sh, TC)[:, 4:8, :]
        spec = (scum[0:1, c0:c0 + TC], scum[0:1, 4 * T + c0: 4 * T + c0 + TC],
                seva[0:1, 0:TC], seva[0:1, H:H + TC],
                sh[0:1, 0:TC], sh[0:1, 4 * TC: 5 * TC])
        cum_re = r3(scum, T)[:, 0:4, c0:c0 + TC]
        cum_im = r3(scum, T)[:, 4:8, c0:c0 + TC]
        emit_cmul(b, cum_re, cum_im,
                  r3(seva[:, 0:H], TC), r3(seva[:, H:2 * H], TC),
                  dre, dim, r3(stmp1, TC), True, spec)
        vh_done = b.ch["vector"].n
        iev = {0: 0, 1: 0}
        for dch in range(NCH):
            key = dch % 2
            ps = env["pk0"] if key == 0 else env["pk1"]
            for fc in range(NCH):
                w = []
                if fc == 0:
                    w.append((b.ch["vector"], max(vh_done, iev[key])))
                b.op("tensor",
                     (lambda e, dch=dch, fc=fc, ps=ps:
                      e.matmul(ps[:, :],
                               sir[:, fc * D + dch * P: fc * D + (dch + 1) * P],
                               sh[:, fc * TC:(fc + 1) * TC],
                               start=(fc == 0), stop=(fc == NCH - 1))),
                     waits=tuple(w),
                     incs=((tq, 1),) if fc == NCH - 1 else ())
            td = tq.n
            iev[key] = b.chain(
                "vector",
                (lambda e, dch=dch, ps=ps, c0=c0: e.tensor_tensor(
                    sx[:, dch * T + c0: dch * T + c0 + TC],
                    sx[:, dch * T + c0: dch * T + c0 + TC], ps[:, :], ALU.add)),
                waits=((tq, td),))
        evac[0] = max(evac[0], iev[0])
        evac[1] = max(evac[1], iev[1])

    for tc in range(NTC):
        c0 = tc * TC
        tbc = emit_ln_tc(b, env, slc3(sx, T, tc), scale_ap=sasc[0:1, 1:2])
        b.chain("vector", lambda e, tc=tc: e.tensor_tensor(
            r3(sqt, TC), slc3(sx, T, tc), bc3(env["pbc_m"]), ALU.subtract),
            waits=((tq, tbc),))
        b.chain("vector", lambda e: e.tensor_tensor(
            r3(sqt, TC), r3(sqt, TC), bc3(env["pbc_r"]), ALU.mult))
        b.chain("vector", lambda e: e.tensor_scalar(
            sqt[:, :], sqt[:, :], 127.0, -127.0, ALU.min, ALU.max))
        b.chain("vector", lambda e: e.tensor_scalar(
            sqt[:, :], sqt[:, :], MAGIC, MAGIC, ALU.add, ALU.subtract))
        b.chain("vector", lambda e: e.tensor_copy(sint[:, :], sqt[:, :]))
        b.chain("vector", lambda e, tc=tc: e.tensor_copy(
            slc3(sxdq, T, tc), r3(sint, TC)))

    b.op("gpsimd", lambda e: e.dma_start(
        out=r3(swt, D), in_=tern_in.rearrange("(c p) t -> p c t", p=P)),
        waits=((tq, qproj_done), (b.ch["gpsimd"], b.ch["gpsimd"].n)),
        incs=((ldg, 16),))
    vt = b.chain("vector", lambda e: e.tensor_scalar(
        swt[:, :], swt[:, :], 1.0, None, ALU.subtract),
        waits=((ldg, ldg.n),))
    ffn_ev = b.ch["vector"].n
    sc_ev = b.ch["vector"].n
    for och in range(NCH):
        for tcq in range(NTC):
            key = (och * NTC + tcq) % 2
            ps = env["pk0"] if key == 0 else env["pk1"]
            for ich in range(NCH):
                w = []
                if ich == 0:
                    w.append((b.ch["vector"], max(vt, ffn_ev, sc_ev)))
                b.op("tensor",
                     (lambda e, och=och, tcq=tcq, ich=ich, ps=ps:
                      e.matmul(ps[:, :],
                               swt[:, ich * D + och * P: ich * D + (och + 1) * P],
                               sxdq[:, ich * T + tcq * TC: ich * T + tcq * TC + TC],
                               start=(ich == 0), stop=(ich == NCH - 1))),
                     waits=tuple(w),
                     incs=((tq, 1),) if ich == NCH - 1 else ())
            td = tq.n
            sc_ev = b.chain("vector", lambda e, och=och, ps=ps: e.scalar_tensor_tensor(
                sfev[:, :], ps[:, :], sscol[:, 0:1],
                sl0b[:, och:och + 1].broadcast_to([P, TC]),
                ALU.mult, ALU.add), waits=((tq, td), (b.ch["vector"], vscol)))
            ffn_ev = b.chain("vector", lambda e, och=och, tcq=tcq: e.tensor_tensor(
                sx[:, och * T + tcq * TC: och * T + tcq * TC + TC],
                sx[:, och * T + tcq * TC: och * T + tcq * TC + TC],
                sfev[:, :], ALU.add))

    b.op("gpsimd", lambda e: e.dma_start(out=o_x2, in_=sx[:, :]),
         waits=((b.ch["vector"], b.ch["vector"].n), (b.ch["gpsimd"], b.ch["gpsimd"].n)),
         incs=((stq, 16),))

    b.op("gpsimd", lambda e: e.dma_start(
        out=r3(swvt, D), in_=wvt_in.rearrange("(c p) t -> p c t", p=P)),
        waits=((b.ch["scalar"], b.ch["scalar"].n), (b.ch["vector"], b.ch["vector"].n),
               (tq, tq.n), (b.ch["gpsimd"], b.ch["gpsimd"].n)),
        incs=((ldg, 16),))
    ldwv = ldg.n

    for tc in range(NTC):
        tbc = emit_ln_tc(b, env, slc3(sx, T, tc))
        b.chain("vector", lambda e, tc=tc: e.tensor_tensor(
            r3(sqt, TC), slc3(sx, T, tc), bc3(env["pbc_m"]), ALU.subtract),
            waits=((tq, tbc),))
        b.chain("vector", lambda e, tc=tc: e.tensor_tensor(
            slc3(sln3, T, tc), r3(sqt, TC), bc3(env["pbc_r"]), ALU.mult))
    vln3 = b.ch["vector"].n
    b.op("gpsimd", lambda e: e.dma_start(out=o_ln3, in_=sln3[:, :]),
         waits=((b.ch["vector"], vln3), (b.ch["gpsimd"], b.ch["gpsimd"].n)),
         incs=((stq, 16),))

    vx_ev = b.ch["vector"].n
    for tk in range(T // P):
        for half in range(2):
            key = half
            ps = env["pk0"] if key == 0 else env["pk1"]
            for dch in range(NCH):
                w = []
                if dch == 0:
                    w.append((b.ch["vector"], max(vln3, vx_ev)))
                    w.append((ldg, ldwv))
                b.op("tensor",
                     (lambda e, tk=tk, half=half, dch=dch, ps=ps:
                      e.matmul(ps[:, :],
                               sln3[:, dch * T + tk * P: dch * T + (tk + 1) * P],
                               swvt[:, dch * D + half * TC: dch * D + half * TC + TC],
                               start=(dch == 0), stop=False)),
                     waits=tuple(w))
            b.op("tensor",
                 (lambda e, half=half, ps=ps:
                  e.matmul(ps[:, :], onesr_bf[0:1, :],
                           swvbb[0:1, half * TC: half * TC + TC],
                           start=False, stop=True)),
                 incs=((tq, 1),))
            td = tq.n
            vx_ev = b.chain(
                "vector",
                (lambda e, half=half, ps=ps:
                 e.tensor_copy(
                     svxt[:, half * 8 * GW:(half + 1) * 8 * GW].rearrange(
                         "p (g f) -> p g f", f=GW)[:, :, 0:64],
                     ps[:, :].rearrange("p (g f) -> p g f", f=64))
                 if fx.fused else
                 e.tensor_copy(svxt[:, half * TC: half * TC + TC], ps[:, :])),
                waits=((tq, td),))
        b.op("gpsimd", (lambda e, tk=tk: e.dma_start(
            out=(o_vx[:, tk * VB:(tk + 1) * VB] if fx.fused
                else o_vx[:, tk * D:(tk + 1) * D]),
            in_=(svxt[:, 0:VB] if fx.fused else svxt[:, 0:D]))),
            waits=((b.ch["vector"], vx_ev), (b.ch["gpsimd"], b.ch["gpsimd"].n)),
            incs=((stq, 16),))
        b.op("vector", lambda e, v=stq.n: e.wait_ge(stq.h, v))

    b.op("gpsimd", lambda e, v=stq.n: e.wait_ge(stq.h, v))
    return dict(env=env, stq=stq, lds=lds, ldg=ldg)


def build_a2():
    fx = Fx()
    emit_a2(fx)
    return fx.b.finalize()


def emit_b(fx):
    nc, b = fx.nc, fx.b
    b.pfx = "b_"
    n_pairs = getattr(fx, "b_pairs", NCH)
    do_outproj = getattr(fx, "b_outproj", True)
    if not fx.fused:
        k_in = fx.dram("ln3full", [P, NCH * S], BF16, "in")
        v_in = fx.dram("vxfull", [P, 2 * AGW], BF16, "in")
    x2_in = fx.dram("o_x2", [P, NCH * T], F32, "in")
    watt_in = fx.dram("watt", [P, NCH * 64], F32, "win")
    ouw_in = fx.dram("outwT", [D, D], F32, "win")
    oub_in = fx.dram("outb", [P, NCH], F32, "win")

    o_x3 = fx.dram("o_x3", [P, NCH * T], F32, "out")

    sk = b.sb("sk", [P, NCH * T + AGW], BF16, 0)
    sv = b.sb("sv", [P, 2 * AGW], BF16, 66560)
    so = b.sb("so", [P, NCH * T], BF16, 133120)
    sxw = [b.sb("sxw0", [P, T], BF16, 165888),
           b.sb("sxw1", [P, T], BF16, 203264)]
    sp = [b.sb("sp_0", [P, 2 * TC], BF16, 169984),
          b.sb("sp_1", [P, 2 * TC], BF16, 172032)]
    swatt = b.sb("swatt", [P, NCH * 64], BF16, 174080)
    swattf = b.sb("swattf", [P, NCH * 64], F32, 174336)
    souw = b.sb("souw", [P, NCH * D], BF16, 174848)
    sx2t = [b.sb("sx2t0", [P, TC], F32, 191232),
            b.sb("sx2t1", [P, TC], F32, 193280),
            b.sb("sx2t2", [P, TC], F32, 195328)]
    ssr = b.sb("ssr", [1, 2 * TC], F32, 195328)
    sot = b.sb("sot", [P, TC], BF16, 199424)
    soutb = b.sb("soutb", [P, NCH], F32, 200448)
    sneg20 = b.sb("sneg20", [P, 1], F32, 200576)
    ones_row = b.sb("ones_rowB", [1, P], F32, 200640)
    sfin = [b.sb("sfin0", [P, TC], F32, 201216),
            b.sb("sfin1", [P, TC], F32, 207360)]

    p_att = [fx.PA, fx.PB]
    p_o = [(fx.PC[:, 0:TC], fx.PC[:, TC:2 * TC]),
           (fx.PD[:, 0:TC], fx.PD[:, TC:2 * TC])]

    tq = b.sem("tq")
    lds = b.sem("lds")
    ldg = b.sem("ldg")
    stq = b.sem("stq")
    ldig = b.sem("ldig")

    if fx.fused:
        nbl = getattr(fx, "b_loads", 5)
        agq, ag_out, idx_ln, idx_vx, a2_stq = fx.ag
        ln3own = fx.ag_in[0:P, 0:NCH * T]
        vxown = fx.ag_in[P:2 * P, :]
        own_sv_ld = 0
        if nbl >= 1:
            b.dma("sync", lambda e: e.dma_start(
                out=sk[:, 0:NCH * T], in_=ln3own),
                sem=lds, waits=((a2_stq, a2_stq.n),))
        if nbl >= 2:
            b.dma("sync", lambda e: e.dma_start(
                out=sv[:, 0:AGW], in_=vxown), sem=lds)
            own_sv_ld = lds.n
        if nbl >= 3:
            b.op("gpsimd", lambda e: e.indirect_dma_start(
                sk[:, NCH * T:NCH * T + AGW],
                None, ag_out, IndirectOffsetOnAxis(idx_ln[:, :], 0)),
                waits=((agq, agq.n), (ldig, ldig.n), (b.ch["gpsimd"], b.ch["gpsimd"].n)),
                incs=((ldig, 16),))
        if nbl >= 4:
            b.op("gpsimd", lambda e: e.indirect_dma_start(
                sv[:, AGW:2 * AGW],
                None, ag_out, IndirectOffsetOnAxis(idx_vx[:, :], 0)),
                waits=((ldig, ldig.n),), incs=((ldig, 16),))
        gath_done = ldig.n
        if nbl >= 5:
            b.dma("sync", lambda e: e.dma_start(out=swattf[:, :], in_=watt_in), lds)
            b.dma("sync", lambda e: e.dma_start(out=soutb[:, :], in_=oub_in), lds)
        b.chain("vector", lambda e: e.memset(
            sv[:, :].rearrange("p (g f) -> p g f", f=GW)[:, :, 64:65], 1.0),
            waits=((lds, own_sv_ld), (ldig, gath_done)))
    else:
        gath_done = 0
        b.dma("sync", lambda e: e.dma_start(out=sk[:, 0:NCH * S], in_=k_in), lds)
        b.dma("sync", lambda e: e.dma_start(out=sv[:, :], in_=v_in), lds)
        b.dma("sync", lambda e: e.dma_start(out=swattf[:, :], in_=watt_in), lds)
        b.dma("sync", lambda e: e.dma_start(out=soutb[:, :], in_=oub_in), lds)
    b.op("gpsimd", lambda e: e.dma_start(
        out=r3(souw, D), in_=ouw_in.rearrange("(c p) t -> p c t", p=P)),
        incs=((ldg, 16),))
    b.chain("vector", lambda e: e.memset(sneg20[:, :], -20.0))
    b.chain("vector", lambda e: e.memset(ones_row[:, :], 1.0))
    vw0 = b.chain("vector", lambda e: e.tensor_scalar(
        swatt[:, :], swattf[:, :], 0.125, None, ALU.mult),
        waits=((lds, lds.n),))

    ev_xw = 0
    ev_fin_par = {0: b.ch["vector"].n, 1: b.ch["vector"].n}
    exp_cnt = {}
    av_done = {}
    pending = None

    def emit_deferred():
        nonlocal pending
        pj, pqc, ppar, pev_s = pending
        pA, pB = p_o[ppar]
        b.op("tensor", (lambda e, pA=pA: e.matmul(
            pA[64:128, :], ones_row[0:1, 0:64], ssr[0:1, 0:TC],
            start=True, stop=True, tile_position=(0, 64))),
            waits=((b.ch["vector"], pev_s),))
        b.op("tensor", (lambda e, pB=pB: e.matmul(
            pB[64:128, :], ones_row[0:1, 0:64], ssr[0:1, TC:2 * TC],
            start=True, stop=True, tile_position=(0, 64))),
            incs=((tq, 1),))
        t_rs = tq.n
        b.chain("vector", (lambda e, pA=pA: e.tensor_copy(
            sot[0:64, :], pA[0:64, :])), waits=((tq, t_rs),))
        b.chain("vector", (lambda e, pB=pB: e.tensor_copy(
            sot[64:128, :], pB[0:64, :])))
        b.chain("vector", (lambda e, pj=pj, pqc=pqc, pA=pA: e.tensor_tensor(
            so[0:64, pj * T + pqc * TC: pj * T + pqc * TC + TC],
            sot[0:64, :], pA[64:128, :], ALU.mult)))
        ev_fin_par[ppar] = b.chain("vector",
            (lambda e, pj=pj, pqc=pqc, pB=pB: e.tensor_tensor(
                so[64:128, pj * T + pqc * TC: pj * T + pqc * TC + TC],
                sot[64:128, :], pB[64:128, :], ALU.mult)))
        pending = None

    def emit_xw_qb(jj, qb):
        nonlocal ev_xw
        tgt = p_att[0][:, 0:TC] if jj == 0 else p_o[0][0]
        w = []
        if jj == 0:
            w.append((lds, lds.n))
            w.append((b.ch["vector"], max(vw0, ev_xw)))
        elif qb == 0:
            w.append((b.ch["vector"], ev_fin_par[0]))
        else:
            w.append((b.ch["vector"], ev_xw))
        b.op("tensor", (lambda e, jj=jj, qb=qb, tgt=tgt: e.matmul(
            tgt[0:64, :], swatt[0:64, jj * 64:(jj + 1) * 64],
            sk[0:64, jj * T + qb * TC: jj * T + qb * TC + TC],
            start=True, stop=True, tile_position=(0, 0))),
            waits=tuple(w))
        b.op("tensor", (lambda e, jj=jj, qb=qb, tgt=tgt: e.matmul(
            tgt[64:128, :], swatt[64:128, jj * 64:(jj + 1) * 64],
            sk[64:128, jj * T + qb * TC: jj * T + qb * TC + TC],
            start=True, stop=True, tile_position=(64, 64))),
            incs=((tq, 1),))
        td = tq.n
        ev_xw = b.chain("vector", (lambda e, jj=jj, qb=qb, tgt=tgt: e.tensor_copy(
            sxw[jj % 2][:, qb * TC:(qb + 1) * TC], tgt[:, :])),
            waits=((tq, td),))
        if jj > 0:
            ev_fin_par[0] = ev_xw

    for j in range(n_pairs):
        if j == 0:
            for qb in range(NTC):
                emit_xw_qb(0, qb)
        for qc in range(NTC):
            NKB = S // P
            par = (j * NTC + qc) % 2
            pA, pB = p_o[par]

            def kcol(j, kb):
                if kb < NKB // 2:
                    return j * T + kb * P
                return NCH * T + j * T + (kb - NKB // 2) * P

            def svcol(j, kb, h):
                base = kb * VB if kb < NKB // 2 else AGW + (kb - NKB // 2) * VB
                return base + (2 * j + h) * GW

            att_done = {}

            def emit_att(kb, j=j, qc=qc):
                buf = kb % 2
                w = []
                if kb < 2:
                    w.append((b.ch["vector"], ev_xw))
                if fx.fused and kb == NKB // 2:
                    w.append((ldig, gath_done))
                if (j, qc, kb - 2) in exp_cnt:
                    w.append((b.ch["scalar"], exp_cnt[(j, qc, kb - 2)]))
                elif kb < 2 and (qc > 0 or j > 0):
                    prev = exp_cnt.get((j, qc - 1, NKB - 2 + buf)) if qc > 0 \
                        else exp_cnt.get((j - 1, NTC - 1, NKB - 2 + buf))
                    if prev is not None:
                        w.append((b.ch["scalar"], prev))
                kc = kcol(j, kb)
                b.op("tensor", (lambda e, kc=kc, qc=qc, buf=buf, j=j: e.matmul(
                    p_att[buf][:, 0:TC],
                    sk[0:64, kc: kc + P],
                    sxw[j % 2][0:64, qc * TC:(qc + 1) * TC],
                    start=True, stop=True, tile_position=(0, 0))),
                    waits=tuple(w))
                b.op("tensor", (lambda e, kc=kc, qc=qc, buf=buf, j=j: e.matmul(
                    p_att[buf][:, TC:2 * TC],
                    sk[64:128, kc: kc + P],
                    sxw[j % 2][64:128, qc * TC:(qc + 1) * TC],
                    start=True, stop=True, tile_position=(64, 0))),
                    incs=((tq, 1),))
                att_done[kb] = tq.n

            emit_att(0)
            emit_att(1)
            for kb in range(NKB):
                buf = kb % 2
                sw = [(tq, att_done[kb])]
                if (j, qc, kb - 2) in av_done:
                    sw.append((tq, av_done[(j, qc, kb - 2)]))
                exp_cnt[(j, qc, kb)] = b.chain("scalar", (lambda e, buf=buf: e.activation(
                    sp[buf][:, :], p_att[buf][:, :], AF.Exp, bias=sneg20[:, 0:1])),
                    waits=tuple(sw))
                if kb + 2 < NKB:
                    emit_att(kb + 2)
                if kb == 9 and pending is not None:
                    emit_deferred()
                if (qc == NTC - 1 and j + 1 < n_pairs
                        and kb in (13, 16, 19, 22)):
                    emit_xw_qb(j + 1, (kb - 13) // 3)
                aw = [(b.ch["scalar"], exp_cnt[(j, qc, kb)])]
                if kb == 0:
                    aw.append((b.ch["vector"], ev_fin_par[par]))
                cA = svcol(j, kb, 0)
                cB = svcol(j, kb, 1)
                b.op("tensor", (lambda e, cA=cA, kb=kb, buf=buf, pA=pA: e.matmul(
                    pA[0:GW, :], sv[:, cA:cA + GW],
                    sp[buf][:, 0:TC],
                    start=(kb == 0), stop=(kb == NKB - 1), tile_position=(0, 0))),
                    waits=tuple(aw))
                b.op("tensor", (lambda e, cB=cB, kb=kb, buf=buf, pB=pB: e.matmul(
                    pB[0:GW, :], sv[:, cB:cB + GW],
                    sp[buf][:, TC:2 * TC],
                    start=(kb == 0), stop=(kb == NKB - 1), tile_position=(0, 0))),
                    incs=((tq, 1),))
                av_done[(j, qc, kb)] = tq.n
            tdav = av_done[(j, qc, NKB - 1)]
            b.chain("vector", (lambda e, pA=pA: e.tensor_copy(
                ssr[0:1, 0:TC], pA[64:65, :])), waits=((tq, tdav),))
            b.chain("vector", (lambda e, pB=pB: e.tensor_copy(
                ssr[0:1, TC:2 * TC], pB[64:65, :])))
            ev_s = b.chain("vector", lambda e: e.reciprocal(ssr[0:1, :], ssr[0:1, :]))
            pending = (j, qc, par, ev_s)
    emit_deferred()
    avn = tq.n

    ld_all = lds.n
    ldg_all = ldg.n
    if not do_outproj:
        b.op("gpsimd", lambda e, v=stq.n: e.wait_ge(stq.h, v))
        return dict(stq=stq, lds=lds, ldg=ldg, ldig=ldig)
    v_o_all = b.ch["vector"].n
    ev_fin = b.ch["vector"].n
    x3sb = None
    if fx.fused and getattr(fx, "pre_c", None) and "o_x3" not in fx.force_out:
        x3sb = fx.pre_c["sx3"]
    x2ld = {}
    x2war = {}
    sc_hist = {}
    st_hist = {}
    tiles = [(qc, mch) for qc in range(NTC) for mch in range(NCH)]
    for pf in range(2):
        b.op("sync", (lambda e, pf=pf: e.dma_start(
            out=sx2t[pf][:, :],
            in_=r3(x2_in, T)[:, tiles[pf][1], tiles[pf][0] * TC:(tiles[pf][0] + 1) * TC])),
            waits=((lds, lds.n),), incs=((lds, 16),))
        x2ld[pf] = lds.n
    for idx, (qc, mch) in enumerate(tiles):
        buf = idx % 3
        if idx + 2 < NCH * NTC:
            nqc, nmch = tiles[idx + 2]
            pw = ()
            if (idx - 1) in x2war:
                pw = ((b.ch["vector"], x2war[idx - 1]),)
            elif idx == 0:
                pw = ((tq, avn),)
            b.op("sync", (lambda e, nmch=nmch, nqc=nqc, nbuf=(idx + 2) % 3:
                          e.dma_start(out=sx2t[nbuf][:, :],
                                      in_=r3(x2_in, T)[:, nmch, nqc * TC:(nqc + 1) * TC])),
                 waits=pw, incs=((lds, 16),))
            x2ld[idx + 2] = lds.n
        ps = p_att[idx % 2]
        for ech in range(NCH):
            w = []
            if ech == 0:
                w.append((b.ch["vector"], sc_hist.get(idx - 2, v_o_all)))
                w.append((ldg, ldg_all))
            b.op("tensor", (lambda e, mch=mch, qc=qc, ech=ech, ps=ps: e.matmul(
                ps[:, 0:TC], souw[:, ech * D + mch * P: ech * D + (mch + 1) * P],
                so[:, ech * T + qc * TC: ech * T + qc * TC + TC],
                start=(ech == 0), stop=(ech == NCH - 1))),
                waits=tuple(w),
                incs=((tq, 1),) if ech == NCH - 1 else ())
        td = tq.n
        scw = [(tq, td), (lds, ld_all)]
        if x3sb is None and idx - 2 in st_hist:
            scw.append((stq, st_hist[idx - 2]))
        sc_hist[idx] = b.chain("vector", (lambda e, mch=mch, sbuf=idx % 2, ps=ps: e.tensor_scalar(
            sfin[sbuf][:, :], ps[:, 0:TC], soutb[:, mch:mch + 1], None, ALU.add)),
            waits=tuple(scw))
        if x3sb is not None:
            ev_fin = b.chain("vector", (lambda e, mch=mch, qc=qc, buf=buf,
                                        sbuf=idx % 2: e.tensor_tensor(
                r3(x3sb, T)[:, mch, qc * TC:(qc + 1) * TC],
                sfin[sbuf][:, :], sx2t[buf][:, :], ALU.add)),
                waits=((lds, x2ld[idx]),))
        else:
            ev_fin = b.chain("vector", (lambda e, buf=buf, sbuf=idx % 2: e.tensor_tensor(
                sfin[sbuf][:, :], sfin[sbuf][:, :], sx2t[buf][:, :], ALU.add)),
                waits=((lds, x2ld[idx]),))
            b.op("gpsimd", (lambda e, mch=mch, qc=qc, buf=buf: e.dma_start(
                out=r3(o_x3, T)[:, mch, qc * TC:(qc + 1) * TC], in_=sfin[buf][:, :])),
                waits=((b.ch["vector"], ev_fin), (b.ch["gpsimd"], b.ch["gpsimd"].n)),
                incs=((stq, 16),))
            st_hist[idx] = stq.n
        x2war[idx] = ev_fin
    b.op("gpsimd", lambda e, v=stq.n: e.wait_ge(stq.h, v))
    return dict(stq=stq, lds=lds, ldg=ldg, ldig=ldig, tq=tq, avn=avn)


def build_b():
    fx = Fx()
    emit_b(fx)
    return fx.b.finalize()


def emit_c(fx):
    nc, b = fx.nc, fx.b
    b.pfx = "c_"
    x3_in = fx.dram("o_x3", [P, NCH * T], F32, "in")
    kern_in = fx.dram("kern", [D, D], BF16, "win")
    sin_in = fx.dram("s_in", [1, 1], F32, "win")
    sk_in = fx.dram("s_k", [1, 1], F32, "win")
    l1b_in = fx.dram("l1bias", [P, NCH], BF16, "win")

    o_x4 = fx.dram("o_x4", [P, NCH * T], F32, "result")

    pre_c = getattr(fx, "pre_c", None)
    sx3 = pre_c["sx3"] if pre_c else b.sb("sx3", [P, NCH * T], F32, 0)
    sxq = b.sb("sxq", [P, NCH * T], FP8, 64 * KB)
    skq = b.sb("skq", [P, NCH * D], FP8, 80 * KB)
    skern = pre_c["skern"] if pre_c else b.sb("skern", [P, NCH * D], BF16, 88 * KB)
    sln4 = b.sb("sln4", [P, NCH * TC], BF16, 104 * KB)
    sy = [b.sb("sy0", [P, TC], BF16, 112 * KB),
          b.sb("sy1", [P, TC], BF16, 112 * KB + 1024)]
    sscr = pre_c["sscr"] if pre_c else b.sb("sscr", [1, 8], F32, 170 * KB)
    sl1b = pre_c["sl1b"] if pre_c else b.sb("sl1b", [P, NCH], BF16, 170 * KB + 64)
    scols = b.sb("scols", [P, 8], F32, 170 * KB + 128)
    sl1bf = b.sb("sl1bf", [P, NCH], F32, 170 * KB + 192)
    sqt = b.sb("sqtC", [P, NCH * TC], F32, 114 * KB)
    env = std_env(b, nc, 130 * KB, fx, pre=pre_c)
    tq, lds, ldg, stq = env["tq"], env["lds"], env["ldg"], env["stq"]

    if not pre_c:
        b.dma("sync", lambda e: e.dma_start(out=sx3[:, :], in_=x3_in), lds)
        b.op("sync", lambda e: e.dma_start(out=r3(skern, D),
                                           in_=kern_in.rearrange("(c p) t -> p c t", p=P)),
             incs=((lds, 16),))
        b.dma("sync", lambda e: e.dma_start(out=sscr[0:1, 0:1], in_=sin_in), lds)
        b.dma("sync", lambda e: e.dma_start(out=sscr[0:1, 1:2], in_=sk_in), lds)
        b.dma("sync", lambda e: e.dma_start(out=sl1b[:, :], in_=l1b_in), lds)
    ld0 = lds.n

    b.chain("vector", lambda e: e.tensor_mul(
        sscr[0:1, 2:3], sscr[0:1, 0:1], sscr[0:1, 1:2]), waits=((lds, ld0),))
    b.chain("vector", lambda e: e.reciprocal(sscr[0:1, 3:4], sscr[0:1, 0:1]))
    vsc = b.chain("vector", lambda e: e.reciprocal(sscr[0:1, 4:5], sscr[0:1, 1:2]))
    b.op("tensor", lambda e: e.matmul(env["pk0"][:, 0:5], env["ones_row"][0:1, :],
                                      sscr[0:1, 0:5], start=True, stop=True),
         waits=((b.ch["vector"], vsc),), incs=((tq, 1),))
    tsc = tq.n
    vcols = b.chain("vector", lambda e: e.tensor_copy(scols[:, 0:5], env["pk0"][:, 0:5]),
                    waits=((tq, tsc),))
    b.chain("vector", lambda e: e.tensor_copy(sl1bf[:, :], sl1b[:, :]))
    vkq = b.chain("vector", lambda e: e.tensor_scalar(
        skq[:, :], skern[:, :], scols[:, 4:5], None, ALU.mult))

    for tc in range(NTC):
        tbc = emit_ln_tc(b, env, slc3(sx3, T, tc))
        b.chain("vector", lambda e, tc=tc: e.tensor_tensor(
            r3(sqt, TC), slc3(sx3, T, tc), bc3(env["pbc_m"]), ALU.subtract),
            waits=((tq, tbc),))
        b.chain("vector", lambda e: e.tensor_tensor(
            r3(sln4, TC), r3(sqt, TC), bc3(env["pbc_r"]), ALU.mult))
        b.chain("vector", lambda e, tc=tc: e.tensor_scalar(
            slc3(sxq, T, tc), r3(sln4, TC), scols[:, 3:4], None, ALU.mult))
    vq_all = b.ch["vector"].n

    scv_hist = {}
    for ci, (och, tcq) in enumerate([(o_, t_) for o_ in range(NCH)
                                     for t_ in range(NTC)]):
        key = ci % 2
        ps = env["pk0"] if key == 0 else env["pk1"]
        for ich in range(NCH):
            w = []
            if ich == 0:
                w.append((b.ch["vector"],
                          scv_hist.get(ci - 2, max(vq_all, vkq))))
            b.op("tensor",
                 (lambda e, och=och, tcq=tcq, ich=ich, ps=ps:
                  e.matmul(ps[:, :],
                           skq[:, ich * D + och * P: ich * D + (och + 1) * P],
                           sxq[:, ich * T + tcq * TC: ich * T + tcq * TC + TC],
                           start=(ich == 0), stop=(ich == NCH - 1))),
                 waits=tuple(w),
                 incs=((tq, 1),) if ich == NCH - 1 else ())
        td = tq.n
        scv_hist[ci] = b.chain("vector", (lambda e, ps=ps, key=key: e.tensor_scalar(
            sy[key][:, :], ps[:, :], scols[:, 2:3], None, ALU.mult)),
            waits=((tq, td), (b.ch["vector"], vcols)))
        b.chain("vector", (lambda e, och=och, key=key: e.tensor_scalar(
            sy[key][:, :], sy[key][:, :], sl1bf[:, och:och + 1], None, ALU.add)))
        tile_ev = b.chain("vector", (lambda e, och=och, tcq=tcq, key=key: e.tensor_tensor(
            sx3[:, och * T + tcq * TC: och * T + tcq * TC + TC],
            sx3[:, och * T + tcq * TC: och * T + tcq * TC + TC],
            sy[key][:, :], ALU.add)))
        b.op("gpsimd", (lambda e, och=och, tcq=tcq: e.dma_start(
            out=o_x4[:, och * T + tcq * TC: och * T + tcq * TC + TC],
            in_=sx3[:, och * T + tcq * TC: och * T + tcq * TC + TC])),
            waits=((b.ch["vector"], tile_ev), (b.ch["gpsimd"], b.ch["gpsimd"].n)),
            incs=((stq, 16),))
    b.op("gpsimd", lambda e, v=stq.n: e.wait_ge(stq.h, v))
    return dict(stq=stq)


def build_c():
    fx = Fx()
    emit_c(fx)
    return fx.b.finalize()


def build_full(upto="c", force_out=(), b_pairs=NCH, b_outproj=True, b_loads=5,
               b_gmode="normal"):
    fx = Fx(fused=True)
    fx.b_gmode = b_gmode
    fx.force_out = set(force_out)
    fx.b_pairs = b_pairs
    fx.b_outproj = b_outproj
    fx.b_loads = b_loads
    nc, b = fx.nc, fx.b
    fx.ag_in = nc.dram_tensor("ag_in", [2 * P, AGW], BF16, kind="Internal").ap()
    ag_out = nc.dram_tensor("ag_out", [8 * 2 * P, AGW], BF16,
                            kind="Internal", addr_space="Shared").ap()
    rs_in = nc.dram_tensor("rs_in", [8 * NCH, P], F32, kind="Internal").ap()
    rs_out = nc.dram_tensor("rs_out", [NCH, P], F32, kind="Internal").ap()
    mask_in = fx.dram("hostmask", [1, 8], F32, "win")
    idn_in = fx.dram("idn", [P, P], F32, "win")
    idx_ln = fx.dram("agidx_ln", [P, 1], I32, "win")
    idx_vx = fx.dram("agidx_vx", [P, 1], I32, "win")

    b.pfx = "g_"
    G2 = 64 * KB
    sidn = b.sb("sidn", [P, P], F32, G2)
    smaskr = b.sb("smaskr", [P, NCH], F32, G2 + 512)
    smt = b.sb("smt", [P, 8 * NCH], F32, G2 + 576)
    smtT = b.sb("smtT", [8 * NCH, P], F32, G2 + 832)
    srsb = b.sb("srsb", [NCH, P], F32, G2 + 1344)
    GT = 210560
    spref_f = b.sb("spref_f", [P, NCH], F32, GT)
    sidx1 = b.sb("sidx1", [P, 1], I32, GT + 64)
    sidx2 = b.sb("sidx2", [P, 1], I32, GT + 128)
    glq = b.sem("glq")
    glg = b.sem("glg")
    ccq = b.sem("ccq")
    fx.ext_spref = spref_f

    b.dma("sync", lambda e: e.dma_start(out=sidx1[:, :], in_=idx_ln), glq)
    b.dma("sync", lambda e: e.dma_start(out=sidx2[:, :], in_=idx_vx), glq)

    a1 = emit_a1(fx)
    b.pfx = "g_"
    stot = a1["stot"]
    tq1 = a1["env"]["tq"]
    b.dma("sync", lambda e: e.dma_start(out=sidn[:, :], in_=idn_in), glq,
          waits=((a1["stq"], a1["stq"].n),))
    idn_ld = glq.n
    b.dma("sync", lambda e: e.dma_start(
        out=smaskr[0:1, :], in_=mask_in), glq)
    mask_ld = glq.n
    b.op("tensor", lambda e: e.matmul(fx.PD[:, 0:NCH], a1["env"]["ones_row"][0:1, :],
                                      smaskr[0:1, 0:NCH], start=True, stop=True),
         waits=((glq, mask_ld),), incs=((tq1, 1),))
    tmr = tq1.n
    vmr = b.chain("vector", lambda e: e.tensor_copy(smaskr[:, :], fx.PD[:, 0:NCH]),
                  waits=((tq1, tmr),))
    for s_ in range(NCH):
        b.chain("vector", lambda e, s_=s_: e.tensor_tensor(
            smt[:, s_ * NCH:(s_ + 1) * NCH], stot[:, :],
            smaskr[:, s_:s_ + 1].broadcast_to([P, NCH]), ALU.mult))
    vmt = b.ch["vector"].n
    b.op("tensor", lambda e: e.transpose(fx.PD[0:8 * NCH, TC:TC + P], smt[:, :], sidn[:, :]),
         waits=((b.ch["vector"], vmt), (glq, idn_ld)), incs=((tq1, 1),))
    tmt = tq1.n
    vmtT = b.chain("vector", lambda e: e.tensor_copy(smtT[:, :], fx.PD[0:8 * NCH, TC:TC + P]),
                   waits=((tq1, tmt),))
    b.dma("gpsimd", lambda e: e.dma_start(out=rs_in, in_=smtT[:, :]), glg,
          waits=((b.ch["vector"], vmtT), (b.ch["gpsimd"], b.ch["gpsimd"].n)))
    rsw = glg.n
    b.op("gpsimd", lambda e: e.collective_compute(
        "ReduceScatter", ALU.add, replica_groups=[list(range(8))],
        ins=[rs_in], outs=[rs_out]),
        waits=((glg, rsw),), incs=((ccq, 1),))
    b.dma("gpsimd", lambda e: e.dma_start(out=srsb[:, :], in_=rs_out), glg,
          waits=((ccq, 1),))
    rsrd = glg.n
    b.op("tensor", lambda e: e.transpose(fx.PD[0:P, TC:TC + NCH], srsb[:, :],
                                         sidn[0:NCH, 0:NCH]),
         waits=((glg, rsrd),), incs=((tq1, 1),))
    tpr = tq1.n
    vpr = b.chain("vector", lambda e: e.tensor_copy(spref_f[:, :], fx.PD[0:P, TC:TC + NCH]),
                  waits=((tq1, tpr),))
    b.pfx = "a2_"
    pre_sx = b.sb("sx", [P, NCH * T], F32, 0)
    pre_sfkv = b.sb("sfkv", [P, NCH * T], BF16, 96 * KB)
    pre_lds = b.sem("lds")
    b.dma("sync", lambda e: e.dma_start(
        out=pre_sx[:, :], in_=fx.dram("o_xmid", [P, NCH * T], F32, "in")),
        pre_lds, waits=((a1["stq"], a1["stq"].n),))
    b.dma("sync", lambda e: e.dma_start(
        out=pre_sfkv[:, :], in_=fx.dram("o_fkv", [P, NCH * T], BF16, "in")),
        pre_lds)
    fx.pre = {"sx": pre_sx, "sfkv": pre_sfkv, "lds": pre_lds}
    b.pfx = "g_"
    if upto == "a1":
        return fx.b.finalize()
    emit_barrier(fx, dma_sems=(a1["lds"], a1["ldg"], a1["stq"], glq, glg))

    a2 = emit_a2(fx)
    b.pfx = "g_"
    b.op("gpsimd", lambda e: e.collective_compute(
        "AllGather", ALU.bypass, replica_groups=[list(range(8))],
        ins=[fx.ag_in], outs=[ag_out]),
        waits=((a2["stq"], a2["stq"].n), (b.ch["gpsimd"], b.ch["gpsimd"].n)),
        incs=((ccq, 1),))
    agq = ccq
    fx.ag = (agq, ag_out, sidx1, sidx2, a2["stq"])
    if upto == "a2":
        return fx.b.finalize()
    emit_barrier(fx, dma_sems=(a2["lds"], a2["ldg"], a2["stq"], glq, glg))
    if upto == "bar2":
        return fx.b.finalize()

    b.pfx = "c_"
    pc_sx3 = b.sb("sx3", [P, NCH * T], F32, 0)
    pc_skern = b.sb("skern", [P, NCH * D], BF16, 88 * KB)
    pc_sscr = b.sb("sscr", [1, 8], F32, 170 * KB)
    pc_sl1b = b.sb("sl1b", [P, NCH], BF16, 170 * KB + 64)
    pc_lds = b.sem("lds")
    fx.pre_c = {"sx3": pc_sx3, "skern": pc_skern, "sscr": pc_sscr,
                "sl1b": pc_sl1b, "lds": pc_lds}
    b.pfx = "g_"
    bb = emit_b(fx)
    b.pfx = "c_"
    b.op("sync", lambda e: e.dma_start(
        out=r3(pc_skern, D),
        in_=fx.dram("kern", [D, D], BF16, "win").rearrange("(c p) t -> p c t", p=P)),
        waits=((bb["tq"], bb["avn"]), (pc_lds, pc_lds.n)), incs=((pc_lds, 16),))
    b.dma("sync", lambda e: e.dma_start(
        out=pc_sscr[0:1, 0:1], in_=fx.dram("s_in", [1, 1], F32, "win")), pc_lds)
    b.dma("sync", lambda e: e.dma_start(
        out=pc_sscr[0:1, 1:2], in_=fx.dram("s_k", [1, 1], F32, "win")), pc_lds)
    b.dma("sync", lambda e: e.dma_start(
        out=pc_sl1b[:, :], in_=fx.dram("l1bias", [P, NCH], BF16, "win")), pc_lds)
    b.pfx = "g_"
    if upto == "b":
        return fx.b.finalize()
    emit_barrier(fx, dma_sems=(bb["lds"], bb["ldg"], bb["stq"], bb["ldig"]))

    emit_c(fx)
    return fx.b.finalize()


FUSED = True
_NC_CACHE = {}


def _get_nc(name):
    if name not in _NC_CACHE:
        _NC_CACHE[name] = {"a1": build_a1, "a2": build_a2, "b": build_b,
                           "c": build_c, "full": build_full}[name]()
    return _NC_CACHE[name]


def _host_arrays(inputs):
    x = np.asarray(inputs["L0_attn_x"])
    Cb, IRb = get_consts()
    h = {}
    h["Cb"] = np.asarray(Cb)
    h["IRb"] = np.asarray(IRb)
    h["Wq"] = np.asarray(inputs["L0_attn_Wq"])
    h["Wk"] = np.asarray(inputs["L0_attn_Wk"])
    h["Wv"] = np.asarray(inputs["L0_attn_Wv"])
    h["ternT"] = np.ascontiguousarray(np.asarray(inputs["L0_ffn_ternary_weight"]).T)
    h["l0bias"] = vcol(np.asarray(inputs["L0_ffn_bias"]))
    h["ascale"] = np.asarray(inputs["L0_ffn_act_scale"], np.float32).reshape(1, 1)
    watt = np.asarray(inputs["L1_attn_w_att_val"])
    ws = np.zeros((P, NCH * 64), np.float32)
    for j in range(NCH):
        ws[0:64, j * 64:(j + 1) * 64] = watt[2 * j]
        ws[64:128, j * 64:(j + 1) * 64] = watt[2 * j + 1]
    h["watt"] = ws
    h["wvT"] = np.ascontiguousarray(np.asarray(inputs["L1_attn_wv_w"]).T)
    h["wvb"] = np.asarray(inputs["L1_attn_wv_b"]).reshape(1, D)
    h["outwT"] = np.ascontiguousarray(np.asarray(inputs["L1_attn_out_w"]).T)
    h["outb"] = vcol(np.asarray(inputs["L1_attn_out_b"]))
    h["kern"] = np.asarray(inputs["L1_ffn_kernel"])
    h["s_in"] = np.asarray(inputs["L1_ffn_input_scale"], np.float32).reshape(1, 1)
    h["s_k"] = np.asarray(inputs["L1_ffn_kernel_scale"], np.float32).reshape(1, 1)
    h["l1bias"] = np.asarray(inputs["L1_ffn_bias"]).reshape(NCH, P).T.copy()
    xT = []
    for c in range(8):
        bb, hh = c // 2, c % 2
        xT.append(np.ascontiguousarray(x[bb, hh * T:(hh + 1) * T, :].T))
    return h, xT


def kernel(**inputs):
    h, xT = _host_arrays(inputs)
    cores = list(range(8))
    if FUSED:
        idn = np.eye(P, dtype=np.float32)
        in_maps = []
        for c in cores:
            mask = np.zeros((1, 8), np.float32)
            if c % 2 == 0:
                mask[0, c + 1] = 1.0
            pr = c ^ 1
            m = dict(h)
            m["xT"] = xT[c]
            m["hostmask"] = mask
            m["idn"] = idn
            m["agidx_ln"] = (pr * 2 * P + np.arange(P)).astype(np.int32).reshape(P, 1)
            m["agidx_vx"] = (pr * 2 * P + P + np.arange(P)).astype(np.int32).reshape(P, 1)
            in_maps.append(m)
        res = run_bass_kernel_spmd(_get_nc("full"), in_maps, core_ids=cores).results
        out = np.empty((4, 2 * T, D), np.float32)
        for c in cores:
            bb, hh = c // 2, c % 2
            out[bb, hh * T:(hh + 1) * T, :] = fm_inv(np.asarray(res[c]["o_x4"]), T).T
        return out

    in1 = [{"xT": xT[c], "Wk": h["Wk"], "Wv": h["Wv"], "Cb": h["Cb"]} for c in cores]
    r1 = run_bass_kernel_spmd(_get_nc("a1"), in1, core_ids=cores).results
    in2 = []
    for c in cores:
        pref = (np.zeros((P, NCH), np.float32) if c % 2 == 0
                else np.asarray(r1[c - 1]["o_tot"], np.float32))
        in2.append({"o_fkv": r1[c]["o_fkv"], "o_ln1": r1[c]["o_ln1"],
                    "o_xmid": r1[c]["o_xmid"], "prefix": pref,
                    "Wq": h["Wq"], "Cb": h["Cb"], "IRb": h["IRb"],
                    "ternT": h["ternT"], "l0bias": h["l0bias"],
                    "ascale": h["ascale"], "wvT": h["wvT"], "wvb": h["wvb"]})
    r2 = run_bass_kernel_spmd(_get_nc("a2"), in2, core_ids=cores).results
    in3 = []
    for c in cores:
        p = c ^ 1
        lnf = np.concatenate([np.asarray(r2[c]["o_ln3"]),
                              np.asarray(r2[p]["o_ln3"])], axis=1)
        vx2 = np.stack([np.asarray(r2[c]["o_vx"]),
                        np.asarray(r2[p]["o_vx"])])
        vxf = np.ones((2, P, T // P, 16, GW), vx2.dtype)
        vxf[..., 0:64] = vx2.reshape(2, P, T // P, 16, 64)
        vxf = vxf.transpose(1, 0, 2, 3, 4).reshape(P, 2 * AGW)
        in3.append({"ln3full": np.ascontiguousarray(lnf),
                    "vxfull": np.ascontiguousarray(vxf),
                    "o_x2": r2[c]["o_x2"], "watt": h["watt"],
                    "outwT": h["outwT"], "outb": h["outb"]})
    r3_ = run_bass_kernel_spmd(_get_nc("b"), in3, core_ids=cores).results
    in4 = [{"o_x3": r3_[c]["o_x3"], "kern": h["kern"], "s_in": h["s_in"],
            "s_k": h["s_k"], "l1bias": h["l1bias"]} for c in cores]
    r4 = run_bass_kernel_spmd(_get_nc("c"), in4, core_ids=cores).results
    out = np.empty((4, 2 * T, D), np.float32)
    for c in cores:
        bb, hh = c // 2, c % 2
        out[bb, hh * T:(hh + 1) * T, :] = fm_inv(np.asarray(r4[c]["o_x4"]), T).T
    return out


# version 18
# speedup vs baseline: 1.0280x; 1.0280x over previous
import numpy as np
import ml_dtypes
import concourse.bass as bass
import concourse.mybir as mybir
from concourse.bass_utils import run_bass_kernel_spmd
from concourse.mybir import AluOpType as ALU, ActivationFunctionType as AF
from concourse.bass_core_rust import IndirectOffsetOnAxis

F32 = mybir.dt.float32
BF16 = mybir.dt.bfloat16
I32 = mybir.dt.int32
FP8 = mybir.dt.float8e4

P = 128
D = 1024
NCH = D // P
TC = 512
T = 2048
NTC = T // TC
S = 2 * T
MAGIC = 12582912.0
NBF = ml_dtypes.bfloat16
X = mybir.AxisListType.X
KB = 1024
GW = 65
VB = 16 * GW
AGW = (T // P) * VB


def _dft_consts():
    e = np.arange(D)[:, None].astype(np.float64)
    C = np.zeros((D, D), np.float64)
    f_re = np.arange(512)[None, :].astype(np.float64)
    C[:, 0:512] = np.cos(2 * np.pi * e * f_re / D)
    C[:, 512] = np.cos(np.pi * e[:, 0])
    f_im = np.arange(1, 512)[None, :].astype(np.float64)
    C[:, 513:] = -np.sin(2 * np.pi * e * f_im / D)
    d = np.arange(D)[None, :].astype(np.float64)
    IR = np.zeros((D, D), np.float64)
    IR[0, :] = 1.0 / D
    IR[512, :] = np.cos(np.pi * d[0]) / D
    fr = np.arange(1, 512)[:, None].astype(np.float64)
    IR[1:512, :] = 2 * np.cos(2 * np.pi * d * fr / D) / D
    IR[513:, :] = -2 * np.sin(2 * np.pi * d * fr / D) / D
    return C.astype(np.float32), IR.astype(np.float32)


_CONSTS = None


def get_consts():
    global _CONSTS
    if _CONSTS is None:
        C, IR = _dft_consts()
        _CONSTS = (C.astype(NBF), IR.astype(NBF))
    return _CONSTS


def fm(a):
    Dd, Tt = a.shape
    n = Dd // P
    return np.ascontiguousarray(a.reshape(n, P, Tt).transpose(1, 0, 2).reshape(P, n * Tt))


def fm_inv(a, Tt):
    n = a.shape[1] // Tt
    return np.ascontiguousarray(a.reshape(P, n, Tt).transpose(1, 0, 2).reshape(n * P, Tt))


def vcol(a):
    return np.ascontiguousarray(np.asarray(a).reshape(NCH, P).T)


def tmaj(a):
    Tt, Dd = a.shape
    n = Tt // P
    return np.ascontiguousarray(a.reshape(n, P, Dd).transpose(1, 0, 2).reshape(P, n * Dd))


class Cnt:
    __slots__ = ("h", "n")

    def __init__(self, h):
        self.h = h
        self.n = 0


class Bld:
    def __init__(self, nc):
        self.nc = nc
        self.q = {e: [] for e in ("tensor", "vector", "scalar", "sync", "gpsimd")}
        self._si = 0
        self.ch = {e: self.sem("ch_" + e) for e in ("vector", "scalar", "gpsimd")}

    pfx = ""

    def sem(self, name=None):
        self._si += 1
        return Cnt(self.nc.alloc_semaphore(self.pfx + (name or ("s%d" % self._si))))

    def sb(self, name, shape, dtype, off):
        return self.nc.alloc_sbuf_tensor_at(self.pfx + name, shape, dtype,
                                            offset=off + 17408)

    def op(self, eng, fn, waits=(), incs=()):
        waits = tuple(waits) + tuple((c, c.n) for c, k in incs if k == 16)
        for c, k in incs:
            c.n += k
        W = tuple((c.h, v) for c, v in waits)
        I = tuple((c.h, k) for c, k in incs)

        def run(e):
            for h, v in W:
                e.wait_ge(h, v)
            ins = fn(e)
            if ins is not None:
                for h, k in I:
                    ins = ins.then_inc(h, k)
        self.q[eng].append(run)

    def chain(self, eng, fn, waits=()):
        c = self.ch[eng]
        self.op(eng, fn, waits=tuple(waits) + ((c, c.n),), incs=((c, 1),))
        return c.n

    def dma(self, eng, fn, sem, waits=()):
        self.op(eng, fn, waits=tuple(waits) + ((sem, sem.n),), incs=((sem, 16),))
        return sem.n

    def vw(self):
        return (self.ch["vector"], self.ch["vector"].n)

    def finalize(self):
        nc = self.nc
        q = self.q
        with nc.Block() as block:
            @block.tensor
            def _(e):
                for f in q["tensor"]:
                    f(e)

            @block.vector
            def _(e):
                for f in q["vector"]:
                    f(e)

            @block.scalar
            def _(e):
                for f in q["scalar"]:
                    f(e)

            @block.sync
            def _(e):
                for f in q["sync"]:
                    f(e)

            @block.gpsimd
            def _(e):
                for f in q["gpsimd"]:
                    f(e)
        return nc


def r3(t, inner):
    return t[:, :].rearrange("p (c t) -> p c t", t=inner)


def slc3(t, total, tc):
    return r3(t, total)[:, :, tc * TC:(tc + 1) * TC]


def bc3(ps):
    return ps[:, :].rearrange("p (u t) -> p u t", u=1).broadcast_to([P, NCH, TC])


class Fx:

    def __init__(self, fused=False):
        self.nc = bass.Bass("TRN2", target_bir_lowering=False)
        self.b = Bld(self.nc)
        self.fused = fused
        self._dram = {}
        nc = self.nc
        self.PA = nc.alloc_psum_tensor("PA", [P, 2 * TC], F32)[:, :]
        self.PB = nc.alloc_psum_tensor("PB", [P, 2 * TC], F32)[:, :]
        self.PC = nc.alloc_psum_tensor("PC", [P, 2 * TC], F32)[:, :]
        self.PD = nc.alloc_psum_tensor("PD", [P, 2 * TC], F32)[:, :]
        self.ext_spref = None
        self.ag = None
        self.force_out = set()

    def dram(self, name, shape, dtype, role):
        if name in self._dram:
            return self._dram[name]
        if name in self.force_out:
            kind = "ExternalOutput"
        elif role == "win" or (role == "in" and not self.fused):
            kind = "ExternalInput"
        elif role == "result" or (role == "out" and not self.fused):
            kind = "ExternalOutput"
        else:
            kind = "Internal"
        ap = self.nc.dram_tensor(name, shape, dtype, kind=kind).ap()
        self._dram[name] = ap
        return ap


_BARRIER_N = [0]


def emit_barrier(fx, dma_sems=()):
    b = fx.b
    _BARRIER_N[0] += 1
    ph = b.sem("ph%d" % _BARRIER_N[0])
    engs = ("tensor", "vector", "scalar", "sync", "gpsimd")
    for eng in engs:
        w = tuple((s, s.n) for s in dma_sems) if eng == "sync" else ()
        b.op(eng, lambda e: e.nop(), waits=w, incs=((ph, 1),))
    for eng in engs:
        b.op(eng, lambda e, v=len(engs): e.wait_ge(ph.h, v))


def std_env(b, nc, o, fx, pre=None):
    env = {}
    env["sxb"] = b.sb("sxb", [P, NCH * TC], BF16, o); o += NCH * TC * 2
    env["sxx"] = b.sb("sxx", [P, NCH * TC], BF16, o); o += NCH * TC * 2
    env["m_row"] = b.sb("m_row", [1, TC], F32, o); o += TC * 4
    env["r_row"] = b.sb("r_row", [1, TC], F32, o); o += TC * 4
    env["ones_bf"] = b.sb("ones_bf", [P, 1], BF16, o); o += 32
    env["ones_row"] = b.sb("ones_row", [1, P], F32, o); o += P * 4
    env["eps_c"] = b.sb("eps_c", [1, 1], F32, o); o += 32
    env["end"] = o
    env["psum_sum"] = fx.PA[:, 0:TC]
    env["psum_sq"] = fx.PA[:, TC:2 * TC]
    env["pbc_m"] = fx.PB[:, 0:TC]
    env["pbc_r"] = fx.PB[:, TC:2 * TC]
    env["pk0"] = fx.PC[:, 0:TC]
    env["pk1"] = fx.PC[:, TC:2 * TC]
    env["tq"] = b.sem("tq")
    env["lds"] = (pre["lds"] if pre else b.sem("lds"))
    env["ldg"] = b.sem("ldg")
    env["stq"] = b.sem("stq")
    b.chain("vector", lambda e: e.memset(env["ones_bf"][:, :], 1.0))
    b.chain("vector", lambda e: e.memset(env["ones_row"][:, :], 1.0))
    b.chain("vector", lambda e: e.memset(env["eps_c"][:, :], 1e-5))
    return env


def emit_ln_tc(b, env, src3, scale_ap=None):
    tq = env["tq"]
    sxb, sxx = env["sxb"], env["sxx"]
    vb = b.chain("vector", lambda e: e.tensor_copy(r3(sxb, TC), src3))
    vx = b.chain("vector", lambda e: e.tensor_mul(r3(sxx, TC), r3(sxb, TC), r3(sxb, TC)))
    for ch in range(NCH):
        b.op("tensor",
             (lambda e, ch=ch: e.matmul(env["psum_sum"][0:1, :], env["ones_bf"][:, :],
                                        sxb[:, ch * TC:(ch + 1) * TC],
                                        start=(ch == 0), stop=(ch == NCH - 1))),
             waits=((b.ch["vector"], vb),) if ch == 0 else ())
    for ch in range(NCH):
        b.op("tensor",
             (lambda e, ch=ch: e.matmul(env["psum_sq"][0:1, :], env["ones_bf"][:, :],
                                        sxx[:, ch * TC:(ch + 1) * TC],
                                        start=(ch == 0), stop=(ch == NCH - 1))),
             waits=((b.ch["vector"], vx),) if ch == 0 else (),
             incs=((tq, 1),) if ch == NCH - 1 else ())
    td = tq.n
    m, r = env["m_row"], env["r_row"]
    b.chain("vector", lambda e: e.tensor_scalar(m[0:1, :], env["psum_sum"][0:1, :],
                                                1.0 / 1024.0, None, ALU.mult),
            waits=((tq, td),))
    b.chain("vector", lambda e: e.tensor_mul(r[0:1, :], m[0:1, :], m[0:1, :]))
    vv = b.chain("vector", lambda e: e.scalar_tensor_tensor(
        r[0:1, :], env["psum_sq"][0:1, :], 1.0 / 1024.0, r[0:1, :],
        ALU.mult, ALU.subtract))
    sv = b.chain("scalar", lambda e: e.activation(r[0:1, :], r[0:1, :], AF.Sqrt,
                                                  bias=env["eps_c"][0:1, :]),
                 waits=((b.ch["vector"], vv),))
    vr = b.chain("vector", lambda e: e.reciprocal(r[0:1, :], r[0:1, :]),
                 waits=((b.ch["scalar"], sv),))
    if scale_ap is not None:
        vr = b.chain("vector", lambda e: e.tensor_scalar(
            r[0:1, :], r[0:1, :], scale_ap, None, ALU.mult))
    b.op("tensor", lambda e: e.matmul(env["pbc_m"][:, :], env["ones_row"][0:1, :],
                                      m[0:1, :], start=True, stop=True),
         waits=((b.ch["vector"], vr),))
    b.op("tensor", lambda e: e.matmul(env["pbc_r"][:, :], env["ones_row"][0:1, :],
                                      r[0:1, :], start=True, stop=True),
         incs=((tq, 1),))
    return tq.n


def emit_spectrum_norm(b, env, seva, sp0, sp1, stq2, stq3):
    H = 4 * TC
    re = seva[:, 0:H]
    im = seva[:, H:2 * H]
    for (blk, sp) in ((0, sp0), (1, sp1)):
        srcrow = seva[0:1, blk * H: blk * H + TC]
        vc = b.chain("vector", lambda e, srcrow=srcrow, sp=sp: e.tensor_copy(sp[0:1, 0:TC], srcrow))
        sa = b.chain("scalar", lambda e, srcrow=srcrow, sp=sp: e.activation(
            sp[0:1, TC:2 * TC], srcrow, AF.Abs), waits=((b.ch["vector"], vc),))
        b.chain("vector", lambda e, sp=sp: e.tensor_scalar(
            sp[0:1, TC:2 * TC], sp[0:1, TC:2 * TC], 1e-8, None, ALU.add),
            waits=((b.ch["scalar"], sa),))
        b.chain("vector", lambda e, sp=sp: e.reciprocal(sp[0:1, TC:2 * TC], sp[0:1, TC:2 * TC]))
        b.chain("vector", lambda e, sp=sp: e.tensor_mul(
            sp[0:1, 0:TC], sp[0:1, 0:TC], sp[0:1, TC:2 * TC]))
    s1 = b.chain("scalar", lambda e: e.activation(stq2[:, :], re, AF.Square),
                 waits=(b.vw(),))
    s2 = b.chain("scalar", lambda e: e.activation(stq3[:, :], im, AF.Square))
    b.chain("vector", lambda e: e.tensor_add(stq2[:, :], stq2[:, :], stq3[:, :]),
            waits=((b.ch["scalar"], s2),))
    s3 = b.chain("scalar", lambda e: e.activation(stq2[:, :], stq2[:, :], AF.Sqrt),
                 waits=(b.vw(),))
    b.chain("vector", lambda e: e.tensor_scalar(stq2[:, :], stq2[:, :], 1e-8, None, ALU.add),
            waits=((b.ch["scalar"], s3),))
    b.chain("vector", lambda e: e.reciprocal(stq2[:, :], stq2[:, :]))
    b.chain("vector", lambda e: e.tensor_mul(re, re, stq2[:, :]))
    b.chain("vector", lambda e: e.tensor_mul(im, im, stq2[:, :]))
    b.chain("vector", lambda e: e.tensor_copy(seva[0:1, 0:TC], sp0[0:1, 0:TC]))
    return b.chain("vector", lambda e: e.tensor_copy(seva[0:1, H:H + TC], sp1[0:1, 0:TC]))


def emit_cmul(b, are, aim, bre, bim, dre, dim, t1, conj_b, spec):
    b.chain("vector", lambda e: e.tensor_mul(t1, are, bre))
    b.chain("vector", lambda e: e.tensor_mul(dim, aim, bim))
    b.chain("vector", lambda e: e.tensor_tensor(dre, t1, dim,
                                                ALU.add if conj_b else ALU.subtract))
    b.chain("vector", lambda e: e.tensor_mul(t1, aim, bre))
    b.chain("vector", lambda e: e.tensor_mul(dim, are, bim))
    b.chain("vector", lambda e: e.tensor_tensor(dim, t1, dim,
                                                ALU.subtract if conj_b else ALU.add))
    a0, a5, b0, b5, d0, d5 = spec
    b.chain("vector", lambda e: e.tensor_mul(d0, a0, b0))
    return b.chain("vector", lambda e: e.tensor_mul(d5, a5, b5))


def emit_fuse(b, env, w_in, dst, war_tensor_cnt, swbuf, scb):
    tq, ldg = env["tq"], env["ldg"]
    b.op("gpsimd", lambda e: e.dma_start(
        out=r3(swbuf, D), in_=w_in.rearrange("(c p) t -> p c t", p=P)),
        waits=((tq, war_tensor_cnt), (b.ch["gpsimd"], b.ch["gpsimd"].n)),
        incs=((ldg, 16),))
    ld_target = ldg.n
    evac = {}
    for m in range(NCH):
        for half in range(2):
            key = (m * 2 + half) % 2
            ps = env["pk0"] if key == 0 else env["pk1"]
            for ech in range(NCH):
                w = []
                if m == 0 and half == 0 and ech == 0:
                    w += [(ldg, ld_target), (env["lds"], env["lds_need"]), b.vw()]
                if ech == 0 and key in evac:
                    w.append((b.ch["vector"], evac[key]))
                b.op("tensor",
                     (lambda e, m=m, half=half, ech=ech, ps=ps:
                      e.matmul(ps[:, :],
                               swbuf[:, ech * D + m * P: ech * D + (m + 1) * P],
                               scb[:, ech * D + half * TC: ech * D + half * TC + TC],
                               start=(ech == 0), stop=(ech == NCH - 1))),
                     waits=tuple(w),
                     incs=((tq, 1),) if ech == NCH - 1 else ())
            td = tq.n
            evac[key] = b.chain(
                "vector",
                (lambda e, m=m, half=half, ps=ps:
                 e.tensor_copy(dst[:, m * D + half * TC: m * D + half * TC + TC], ps[:, :])),
                waits=((tq, td),))
    return tq.n, b.ch["vector"].n


def emit_a1(fx):
    nc, b = fx.nc, fx.b
    b.pfx = "a1_"
    xin = fx.dram("xT", [D, T], F32, "win")
    wk_in = fx.dram("Wk", [D, D], F32, "win")
    wv_in = fx.dram("Wv", [D, D], F32, "win")
    c_in = fx.dram("Cb", [D, D], BF16, "win")

    o_fkv = fx.dram("o_fkv", [P, NCH * T], BF16, "out")
    o_ln1 = fx.dram("o_ln1", [P, NCH * T], BF16, "out")
    o_xmid = fx.dram("o_xmid", [P, NCH * T], F32, "out")
    o_tot = fx.dram("o_tot", [P, NCH], F32, "out")

    sfkv = b.sb("sfkv", [P, NCH * T], BF16, 0)
    sfusedA = b.sb("sfusedA", [P, NCH * D], BF16, 32 * KB)
    sfusedB = b.sb("sfusedB", [P, NCH * D], BF16, 48 * KB)
    scb = b.sb("scb", [P, NCH * D], BF16, 64 * KB)
    swbuf = b.sb("swbuf", [P, NCH * D], BF16, 80 * KB)
    sxt = [b.sb("sxt0", [P, NCH * TC], F32, 96 * KB),
           b.sb("sxt1", [P, NCH * TC], F32, 112 * KB)]
    slnt = b.sb("slnt", [P, NCH * TC], BF16, 128 * KB)
    seva = b.sb("seva", [P, NCH * TC], BF16, 136 * KB)
    sevb = b.sb("sevb", [P, NCH * TC], BF16, 144 * KB)
    stmp1 = b.sb("stmp1", [P, 4 * TC], F32, 152 * KB)
    stot = b.sb("stot", [P, NCH], F32, 160 * KB)
    sp0 = b.sb("sp0", [1, 2 * TC], F32, 161 * KB)
    sp1 = b.sb("sp1", [1, 2 * TC], F32, 165 * KB)
    stq2 = b.sb("stq2", [P, 4 * TC], F32, 169 * KB)
    stq3 = b.sb("stq3", [P, 4 * TC], F32, 177 * KB)
    env = std_env(b, nc, 185 * KB, fx)

    tq, lds, ldg, stq = env["tq"], env["lds"], env["ldg"], env["stq"]
    b.op("sync", lambda e: e.dma_start(out=r3(scb, D),
                                       in_=c_in.rearrange("(c p) t -> p c t", p=P)),
         incs=((lds, 16),))
    env["lds_need"] = 16
    tfK, vfK = emit_fuse(b, env, wk_in, sfusedA, 0, swbuf, scb)
    tfV, vfV = emit_fuse(b, env, wv_in, sfusedB, tfK, swbuf, scb)

    evac = {0: vfV, 1: vfV}
    xload = [0] * NTC
    tile_free_v = [0, 0]
    tile_free_s = [0, 0]
    for tc in range(min(2, NTC)):
        b.op("sync", lambda e, tc=tc: e.dma_start(
            out=r3(sxt[tc % 2], TC),
            in_=xin.rearrange("(c p) t -> p c t", p=P)[:, :, tc * TC:(tc + 1) * TC]),
            incs=((lds, 16),))
        xload[tc] = lds.n
    st_ln = 0
    for tc in range(NTC):
        c0 = tc * TC
        cur = sxt[tc % 2]
        b.op("vector", lambda e, v=xload[tc]: e.wait_ge(lds.h, v))
        b.op("vector", lambda e, v=st_ln: e.wait_ge(stq.h, v))
        tbc = emit_ln_tc(b, env, r3(cur, TC))
        b.chain("vector", lambda e, cur=cur: e.tensor_tensor(
            r3(slnt, TC), r3(cur, TC), bc3(env["pbc_m"]), ALU.subtract),
            waits=((tq, tbc),))
        vln = b.chain("vector", lambda e: e.tensor_tensor(
            r3(slnt, TC), r3(slnt, TC), bc3(env["pbc_r"]), ALU.mult))
        vres = b.chain("vector", lambda e, cur=cur: e.tensor_tensor(
            r3(cur, TC), r3(cur, TC), r3(slnt, TC), ALU.add))
        tile_free_v[tc % 2] = vres
        b.op("gpsimd", lambda e, tc=tc, cur=cur: e.dma_start(
            out=r3(o_xmid, T)[:, :, tc * TC:(tc + 1) * TC], in_=r3(cur, TC)),
            waits=((b.ch["vector"], vres), (b.ch["gpsimd"], b.ch["gpsimd"].n)),
            incs=((stq, 16),))
        tile_free_s[tc % 2] = stq.n
        b.op("gpsimd", lambda e, tc=tc: e.dma_start(
            out=r3(o_ln1, T)[:, :, tc * TC:(tc + 1) * TC], in_=r3(slnt, TC)),
            incs=((stq, 16),))
        st_ln = stq.n
        if tc + 2 < NTC:
            b.op("sync", lambda e, tc=tc: e.dma_start(
                out=r3(sxt[tc % 2], TC),
                in_=xin.rearrange("(c p) t -> p c t", p=P)[:, :, (tc + 2) * TC:(tc + 3) * TC]),
                waits=((b.ch["vector"], tile_free_v[tc % 2]), (stq, tile_free_s[tc % 2])),
                incs=((lds, 16),))
            xload[tc + 2] = lds.n
        for (dst, fmx) in ((seva, sfusedA), (sevb, sfusedB)):
            for fch in range(NCH):
                key = fch % 2
                ps = env["pk0"] if key == 0 else env["pk1"]
                for dch in range(NCH):
                    w = []
                    if dch == 0:
                        w.append((b.ch["vector"], max(vln, evac[key])))
                    b.op("tensor",
                         (lambda e, fch=fch, dch=dch, ps=ps, fmx=fmx:
                          e.matmul(ps[:, :],
                                   fmx[:, dch * D + fch * P: dch * D + (fch + 1) * P],
                                   slnt[:, dch * TC: (dch + 1) * TC],
                                   start=(dch == 0), stop=(dch == NCH - 1))),
                         waits=tuple(w),
                         incs=((tq, 1),) if dch == NCH - 1 else ())
                td = tq.n
                evac[key] = b.chain(
                    "vector",
                    (lambda e, dst=dst, fch=fch, ps=ps:
                     e.tensor_copy(dst[:, fch * TC:(fch + 1) * TC], ps[:, :])),
                    waits=((tq, td),))
        b.op("vector", lambda e, v=tq.n: e.wait_ge(tq.h, v))
        emit_spectrum_norm(b, env, seva, sp0, sp1, stq2, stq3)
        H = 4 * TC
        dre = r3(sfkv, T)[:, 0:4, c0:c0 + TC]
        dim = r3(sfkv, T)[:, 4:8, c0:c0 + TC]
        spec = (seva[0:1, 0:TC], seva[0:1, H:H + TC],
                sevb[0:1, 0:TC], sevb[0:1, H:H + TC],
                sfkv[0:1, c0:c0 + TC],
                sfkv[0:1, 4 * T + c0: 4 * T + c0 + TC])
        emit_cmul(b, r3(seva[:, 0:H], TC), r3(seva[:, H:2 * H], TC),
                  r3(sevb[:, 0:H], TC), r3(sevb[:, H:2 * H], TC),
                  dre, dim, r3(stmp1, TC), False, spec)
    for ch in range(NCH):
        b.chain("vector", lambda e, ch=ch: e.reduce_sum(
            stot[:, ch:ch + 1], sfkv[:, ch * T:(ch + 1) * T], X))
    vend = b.ch["vector"].n
    b.op("gpsimd", lambda e: e.dma_start(out=o_fkv, in_=sfkv[:, :]),
         waits=((b.ch["vector"], vend), (b.ch["gpsimd"], b.ch["gpsimd"].n)),
         incs=((stq, 16),))
    b.dma("gpsimd", lambda e: e.dma_start(out=o_tot, in_=stot[:, :]), stq)
    b.op("gpsimd", lambda e, v=stq.n: e.wait_ge(stq.h, v))
    return dict(stot=stot, env=env, stq=stq, lds=lds, ldg=ldg)


def build_a1():
    fx = Fx()
    emit_a1(fx)
    return fx.b.finalize()


def emit_a2(fx):
    nc, b = fx.nc, fx.b
    b.pfx = "a2_"
    fkv_in = fx.dram("o_fkv", [P, NCH * T], BF16, "in")
    ln1_in = fx.dram("o_ln1", [P, NCH * T], BF16, "in")
    xmid_in = fx.dram("o_xmid", [P, NCH * T], F32, "in")
    if not fx.fused:
        pref_in = fx.dram("prefix", [P, NCH], F32, "in")
    wq_in = fx.dram("Wq", [D, D], F32, "win")
    c_in = fx.dram("Cb", [D, D], BF16, "win")
    ir_in = fx.dram("IRb", [D, D], BF16, "win")
    tern_in = fx.dram("ternT", [D, D], I32, "win")
    l0b_in = fx.dram("l0bias", [P, NCH], F32, "win")
    asc_in = fx.dram("ascale", [1, 1], F32, "win")
    wvt_in = fx.dram("wvT", [D, D], F32, "win")
    wvb_in = fx.dram("wvb", [1, D], F32, "win")

    o_x2 = fx.dram("o_x2", [P, NCH * T], F32, "out")
    if fx.fused:
        o_ln3 = fx.ag_in[0:P, :]
        o_vx = None
    else:
        o_ln3 = fx.dram("o_ln3", [P, NCH * T], BF16, "out")
        o_vx = fx.dram("o_vx", [P, (T // P) * D], BF16, "out")

    pre = getattr(fx, "pre", None)
    sx = pre["sx"] if pre else b.sb("sx", [P, NCH * T], F32, 0)
    scum = b.sb("scum", [P, NCH * T], BF16, 64 * KB)
    sxdq = b.sb("sxdq", [P, NCH * T], BF16, 64 * KB)
    sfkv = pre["sfkv"] if pre else b.sb("sfkv", [P, NCH * T], BF16, 96 * KB)
    stq2 = b.sb("stq2", [P, 4 * TC], F32, 96 * KB)
    stmp1 = b.sb("stmp1", [P, 4 * TC], F32, 104 * KB)
    stq3 = b.sb("stq3", [P, 4 * TC], F32, 112 * KB)
    sp0 = b.sb("sp0", [1, 2 * TC], F32, 120 * KB)
    sp1 = b.sb("sp1", [1, 2 * TC], F32, 122 * KB)
    sqt = b.sb("sqt", [P, NCH * TC], F32, 96 * KB)
    sint = b.sb("sint", [P, NCH * TC], I32, 112 * KB)
    sfusedQ = b.sb("sfusedQ", [P, NCH * D], BF16, 128 * KB)
    swt = b.sb("swt", [P, NCH * D], BF16, 128 * KB)
    sln3 = b.sb("sln3", [P, NCH * T], BF16, 128 * KB)
    sir = b.sb("sir", [P, NCH * D], BF16, 144 * KB)
    scb = b.sb("scb", [P, NCH * D], BF16, 160 * KB)
    slnt = b.sb("slnt", [P, NCH * TC], BF16, 160 * KB)
    sfev = b.sb("sfev", [P, TC], F32, 160 * KB)
    swvt = b.sb("swvt", [P, NCH * D], BF16, 160 * KB)
    seva = b.sb("seva", [P, NCH * TC], BF16, 168 * KB)
    swbuf = b.sb("swbuf", [P, NCH * D], BF16, 176 * KB)
    sh = b.sb("sh", [P, NCH * TC], BF16, 176 * KB)
    svxt = b.sb("svxt", [P, VB], BF16, 197 * KB)
    swvbb = b.sb("swvbb", [1, D], BF16, 199 * KB + 64)
    onesr_bf = b.sb("onesr_bf", [1, P], BF16, 201 * KB + 128)
    spref = b.sb("spref", [P, NCH], F32, 201 * KB + 512)
    sl0b = b.sb("sl0b", [P, NCH], F32, 202 * KB)
    sasc = b.sb("sasc", [1, 4], F32, 202 * KB + 512)
    sscol = b.sb("sscol", [P, 1], F32, 203 * KB)
    env = std_env(b, nc, 176 * KB, fx, pre=pre)
    tq, lds, ldg, stq = env["tq"], env["lds"], env["ldg"], env["stq"]
    env["lds_need"] = 0

    if not pre:
        b.dma("sync", lambda e: e.dma_start(out=sx[:, :], in_=xmid_in), lds)
        b.dma("sync", lambda e: e.dma_start(out=sfkv[:, :], in_=fkv_in), lds)
    b.op("sync", lambda e: e.dma_start(out=r3(scb, D),
                                       in_=c_in.rearrange("(c p) t -> p c t", p=P)),
         incs=((lds, 16),))
    b.op("sync", lambda e: e.dma_start(out=r3(sir, D),
                                       in_=ir_in.rearrange("(c p) t -> p c t", p=P)),
         incs=((lds, 16),))
    if fx.fused:
        spref = fx.ext_spref
    else:
        b.dma("sync", lambda e: e.dma_start(out=spref[:, :], in_=pref_in), lds)
    b.dma("sync", lambda e: e.dma_start(out=sl0b[:, :], in_=l0b_in), lds)
    b.dma("sync", lambda e: e.dma_start(out=sasc[0:1, 0:1], in_=asc_in), lds)
    if not fx.fused:
        b.op("gpsimd", lambda e: e.dma_start(out=swvbb[:, :], in_=wvb_in),
             incs=((ldg, 16),))
    ldwvb = ldg.n
    ld_base = lds.n

    env["lds_need"] = 48
    tfQ, vfQ = emit_fuse(b, env, wq_in, sfusedQ, 0, swbuf, scb)

    b.op("vector", lambda e: e.wait_ge(lds.h, ld_base))
    scan_done = 0
    for ch in range(NCH):
        scan_done = b.chain("vector", lambda e, ch=ch: e.tensor_tensor_scan(
            scum[:, ch * T:(ch + 1) * T],
            env["ones_bf"][:, :].broadcast_to([P, T]),
            sfkv[:, ch * T:(ch + 1) * T],
            spref[:, ch:ch + 1], ALU.mult, ALU.add))

    b.chain("vector", lambda e: e.tensor_scalar(sasc[0:1, 0:1], sasc[0:1, 0:1],
                                                1e-5, None, ALU.max))
    va = b.chain("vector", lambda e: e.reciprocal(sasc[0:1, 1:2], sasc[0:1, 0:1]))
    b.op("tensor", lambda e: e.matmul(env["pk0"][:, 0:1], env["ones_row"][0:1, :],
                                      sasc[0:1, 0:1], start=True, stop=True),
         waits=((b.ch["vector"], va),), incs=((tq, 1),))
    tscol = tq.n
    b.chain("vector", lambda e: e.tensor_copy(sscol[:, :], env["pk0"][:, 0:1]),
            waits=((tq, tscol),))
    vscol = b.ch["vector"].n
    b.op("vector", lambda e, v=ldwvb: e.wait_ge(ldg.h, v))
    b.chain("vector", lambda e: e.memset(onesr_bf[:, :], 1.0))

    evac = {0: b.ch["vector"].n, 1: b.ch["vector"].n}
    slnt_ld = [0] * NTC
    b.op("sync", lambda e: e.dma_start(
        out=r3(slnt, TC), in_=r3(ln1_in, T)[:, :, 0:TC]),
        waits=((tq, tfQ),), incs=((lds, 16),))
    slnt_ld[0] = lds.n
    qproj_done = 0
    for tc in range(NTC):
        c0 = tc * TC
        b.op("vector", lambda e, v=slnt_ld[tc]: e.wait_ge(lds.h, v))
        vgate = b.chain("vector", lambda e: e.engine_nop())
        for fch in range(NCH):
            key = fch % 2
            ps = env["pk0"] if key == 0 else env["pk1"]
            for dch in range(NCH):
                w = []
                if dch == 0:
                    w.append((b.ch["vector"], max(vgate, evac[key], vfQ)))
                b.op("tensor",
                     (lambda e, fch=fch, dch=dch, ps=ps:
                      e.matmul(ps[:, :],
                               sfusedQ[:, dch * D + fch * P: dch * D + (fch + 1) * P],
                               slnt[:, dch * TC: (dch + 1) * TC],
                               start=(dch == 0), stop=(dch == NCH - 1))),
                     waits=tuple(w),
                     incs=((tq, 1),) if dch == NCH - 1 else ())
            td = tq.n
            evac[key] = b.chain(
                "vector",
                (lambda e, fch=fch, ps=ps:
                 e.tensor_copy(seva[:, fch * TC:(fch + 1) * TC], ps[:, :])),
                waits=((tq, td),))
        qproj_done = tq.n
        if tc + 1 < NTC:
            b.op("sync", lambda e, tc=tc: e.dma_start(
                out=r3(slnt, TC), in_=r3(ln1_in, T)[:, :, (tc + 1) * TC:(tc + 2) * TC]),
                waits=((tq, qproj_done),), incs=((lds, 16),))
            slnt_ld[tc + 1] = lds.n
        emit_spectrum_norm(b, env, seva, sp0, sp1, stq2, stq3)
        H = 4 * TC
        dre = r3(sh, TC)[:, 0:4, :]
        dim = r3(sh, TC)[:, 4:8, :]
        spec = (scum[0:1, c0:c0 + TC], scum[0:1, 4 * T + c0: 4 * T + c0 + TC],
                seva[0:1, 0:TC], seva[0:1, H:H + TC],
                sh[0:1, 0:TC], sh[0:1, 4 * TC: 5 * TC])
        cum_re = r3(scum, T)[:, 0:4, c0:c0 + TC]
        cum_im = r3(scum, T)[:, 4:8, c0:c0 + TC]
        emit_cmul(b, cum_re, cum_im,
                  r3(seva[:, 0:H], TC), r3(seva[:, H:2 * H], TC),
                  dre, dim, r3(stmp1, TC), True, spec)
        vh_done = b.ch["vector"].n
        iev = {0: 0, 1: 0}
        for dch in range(NCH):
            key = dch % 2
            ps = env["pk0"] if key == 0 else env["pk1"]
            for fc in range(NCH):
                w = []
                if fc == 0:
                    w.append((b.ch["vector"], max(vh_done, iev[key])))
                b.op("tensor",
                     (lambda e, dch=dch, fc=fc, ps=ps:
                      e.matmul(ps[:, :],
                               sir[:, fc * D + dch * P: fc * D + (dch + 1) * P],
                               sh[:, fc * TC:(fc + 1) * TC],
                               start=(fc == 0), stop=(fc == NCH - 1))),
                     waits=tuple(w),
                     incs=((tq, 1),) if fc == NCH - 1 else ())
            td = tq.n
            iev[key] = b.chain(
                "vector",
                (lambda e, dch=dch, ps=ps, c0=c0: e.tensor_tensor(
                    sx[:, dch * T + c0: dch * T + c0 + TC],
                    sx[:, dch * T + c0: dch * T + c0 + TC], ps[:, :], ALU.add)),
                waits=((tq, td),))
        evac[0] = max(evac[0], iev[0])
        evac[1] = max(evac[1], iev[1])

    for tc in range(NTC):
        c0 = tc * TC
        tbc = emit_ln_tc(b, env, slc3(sx, T, tc), scale_ap=sasc[0:1, 1:2])
        b.chain("vector", lambda e, tc=tc: e.tensor_tensor(
            r3(sqt, TC), slc3(sx, T, tc), bc3(env["pbc_m"]), ALU.subtract),
            waits=((tq, tbc),))
        b.chain("vector", lambda e: e.tensor_tensor(
            r3(sqt, TC), r3(sqt, TC), bc3(env["pbc_r"]), ALU.mult))
        b.chain("vector", lambda e: e.tensor_scalar(
            sqt[:, :], sqt[:, :], 127.0, -127.0, ALU.min, ALU.max))
        b.chain("vector", lambda e: e.tensor_scalar(
            sqt[:, :], sqt[:, :], MAGIC, MAGIC, ALU.add, ALU.subtract))
        b.chain("vector", lambda e: e.tensor_copy(sint[:, :], sqt[:, :]))
        b.chain("vector", lambda e, tc=tc: e.tensor_copy(
            slc3(sxdq, T, tc), r3(sint, TC)))

    b.op("gpsimd", lambda e: e.dma_start(
        out=r3(swt, D), in_=tern_in.rearrange("(c p) t -> p c t", p=P)),
        waits=((tq, qproj_done), (b.ch["gpsimd"], b.ch["gpsimd"].n)),
        incs=((ldg, 16),))
    vt = b.chain("vector", lambda e: e.tensor_scalar(
        swt[:, :], swt[:, :], 1.0, None, ALU.subtract),
        waits=((ldg, ldg.n),))
    ffn_ev = b.ch["vector"].n
    sc_ev = b.ch["vector"].n
    for och in range(NCH):
        for tcq in range(NTC):
            key = (och * NTC + tcq) % 2
            ps = env["pk0"] if key == 0 else env["pk1"]
            for ich in range(NCH):
                w = []
                if ich == 0:
                    w.append((b.ch["vector"], max(vt, ffn_ev, sc_ev)))
                b.op("tensor",
                     (lambda e, och=och, tcq=tcq, ich=ich, ps=ps:
                      e.matmul(ps[:, :],
                               swt[:, ich * D + och * P: ich * D + (och + 1) * P],
                               sxdq[:, ich * T + tcq * TC: ich * T + tcq * TC + TC],
                               start=(ich == 0), stop=(ich == NCH - 1))),
                     waits=tuple(w),
                     incs=((tq, 1),) if ich == NCH - 1 else ())
            td = tq.n
            sc_ev = b.chain("vector", lambda e, och=och, ps=ps: e.scalar_tensor_tensor(
                sfev[:, :], ps[:, :], sscol[:, 0:1],
                sl0b[:, och:och + 1].broadcast_to([P, TC]),
                ALU.mult, ALU.add), waits=((tq, td), (b.ch["vector"], vscol)))
            ffn_ev = b.chain("vector", lambda e, och=och, tcq=tcq: e.tensor_tensor(
                sx[:, och * T + tcq * TC: och * T + tcq * TC + TC],
                sx[:, och * T + tcq * TC: och * T + tcq * TC + TC],
                sfev[:, :], ALU.add))

    b.op("gpsimd", lambda e: e.dma_start(out=o_x2, in_=sx[:, :]),
         waits=((b.ch["vector"], b.ch["vector"].n), (b.ch["gpsimd"], b.ch["gpsimd"].n)),
         incs=((stq, 16),))

    if not fx.fused:
        b.op("gpsimd", lambda e: e.dma_start(
            out=r3(swvt, D), in_=wvt_in.rearrange("(c p) t -> p c t", p=P)),
            waits=((b.ch["scalar"], b.ch["scalar"].n), (b.ch["vector"], b.ch["vector"].n),
                   (tq, tq.n), (b.ch["gpsimd"], b.ch["gpsimd"].n)),
            incs=((ldg, 16),))
        ldwv = ldg.n

    for tc in range(NTC):
        tbc = emit_ln_tc(b, env, slc3(sx, T, tc))
        b.chain("vector", lambda e, tc=tc: e.tensor_tensor(
            r3(sqt, TC), slc3(sx, T, tc), bc3(env["pbc_m"]), ALU.subtract),
            waits=((tq, tbc),))
        b.chain("vector", lambda e, tc=tc: e.tensor_tensor(
            slc3(sln3, T, tc), r3(sqt, TC), bc3(env["pbc_r"]), ALU.mult))
    vln3 = b.ch["vector"].n
    b.op("gpsimd", lambda e: e.dma_start(out=o_ln3, in_=sln3[:, :]),
         waits=((b.ch["vector"], vln3), (b.ch["gpsimd"], b.ch["gpsimd"].n)),
         incs=((stq, 16),))

    vx_ev = b.ch["vector"].n
    for tk in range(0 if fx.fused else T // P):
        for half in range(2):
            key = half
            ps = env["pk0"] if key == 0 else env["pk1"]
            for dch in range(NCH):
                w = []
                if dch == 0:
                    w.append((b.ch["vector"], max(vln3, vx_ev)))
                    w.append((ldg, ldwv))
                b.op("tensor",
                     (lambda e, tk=tk, half=half, dch=dch, ps=ps:
                      e.matmul(ps[:, :],
                               sln3[:, dch * T + tk * P: dch * T + (tk + 1) * P],
                               swvt[:, dch * D + half * TC: dch * D + half * TC + TC],
                               start=(dch == 0), stop=False)),
                     waits=tuple(w))
            b.op("tensor",
                 (lambda e, half=half, ps=ps:
                  e.matmul(ps[:, :], onesr_bf[0:1, :],
                           swvbb[0:1, half * TC: half * TC + TC],
                           start=False, stop=True)),
                 incs=((tq, 1),))
            td = tq.n
            vx_ev = b.chain(
                "vector",
                (lambda e, half=half, ps=ps:
                 e.tensor_copy(
                     svxt[:, half * 8 * GW:(half + 1) * 8 * GW].rearrange(
                         "p (g f) -> p g f", f=GW)[:, :, 0:64],
                     ps[:, :].rearrange("p (g f) -> p g f", f=64))
                 if fx.fused else
                 e.tensor_copy(svxt[:, half * TC: half * TC + TC], ps[:, :])),
                waits=((tq, td),))
        b.op("gpsimd", (lambda e, tk=tk: e.dma_start(
            out=(o_vx[:, tk * VB:(tk + 1) * VB] if fx.fused
                else o_vx[:, tk * D:(tk + 1) * D]),
            in_=(svxt[:, 0:VB] if fx.fused else svxt[:, 0:D]))),
            waits=((b.ch["vector"], vx_ev), (b.ch["gpsimd"], b.ch["gpsimd"].n)),
            incs=((stq, 16),))
        b.op("vector", lambda e, v=stq.n: e.wait_ge(stq.h, v))

    b.op("gpsimd", lambda e, v=stq.n: e.wait_ge(stq.h, v))
    return dict(env=env, stq=stq, lds=lds, ldg=ldg)


def build_a2():
    fx = Fx()
    emit_a2(fx)
    return fx.b.finalize()


def emit_b(fx):
    nc, b = fx.nc, fx.b
    b.pfx = "b_"
    n_pairs = getattr(fx, "b_pairs", NCH)
    do_outproj = getattr(fx, "b_outproj", True)
    if not fx.fused:
        k_in = fx.dram("ln3full", [P, NCH * S], BF16, "in")
        v_in = fx.dram("vxfull", [P, 2 * AGW], BF16, "in")
    x2_in = fx.dram("o_x2", [P, NCH * T], F32, "in")
    watt_in = fx.dram("watt", [P, NCH * 64], F32, "win")
    ouw_in = fx.dram("outwT", [D, D], F32, "win")
    oub_in = fx.dram("outb", [P, NCH], F32, "win")

    o_x3 = fx.dram("o_x3", [P, NCH * T], F32, "out")

    sk = b.sb("sk", [P, 2 * NCH * T], BF16, 0)
    sv = b.sb("sv", [P, 2 * AGW], BF16, 66560)
    so = b.sb("so", [P, NCH * T], BF16, 133120)
    sxw = [b.sb("sxw0", [P, T], BF16, 165888),
           b.sb("sxw1", [P, T], BF16, 203264)]
    sp = [b.sb("sp_0", [P, 2 * TC], BF16, 169984),
          b.sb("sp_1", [P, 2 * TC], BF16, 172032)]
    swatt = b.sb("swatt", [P, NCH * 64], BF16, 174080)
    swattf = b.sb("swattf", [P, NCH * 64], F32, 174336)
    souw = b.sb("souw", [P, NCH * D], BF16, 174848)
    sx2t = [b.sb("sx2t0", [P, TC], F32, 191232),
            b.sb("sx2t1", [P, TC], F32, 193280),
            b.sb("sx2t2", [P, TC], F32, 195328)]
    ssr = b.sb("ssr", [1, 2 * TC], F32, 195328)
    sot = b.sb("sot", [P, TC], BF16, 199424)
    soutb = b.sb("soutb", [P, NCH], F32, 200448)
    sneg20 = b.sb("sneg20", [P, 1], F32, 200576)
    ones_row = b.sb("ones_rowB", [1, P], F32, 200640)
    sfin = [b.sb("sfin0", [P, TC], F32, 201216),
            b.sb("sfin1", [P, TC], F32, 207360)]

    p_att = [fx.PA, fx.PB]
    p_o = [(fx.PC[:, 0:TC], fx.PC[:, TC:2 * TC]),
           (fx.PD[:, 0:TC], fx.PD[:, TC:2 * TC])]

    tq = b.sem("tq")
    lds = b.sem("lds")
    ldg = b.sem("ldg")
    stq = b.sem("stq")
    ldig = b.sem("ldig")

    if fx.fused:
        agq, ag_out, idx_ln, a2_stq = fx.ag
        b.dma("sync", lambda e: e.dma_start(
            out=sk[:, 0:NCH * T], in_=fx.ag_in),
            sem=lds, waits=((a2_stq, a2_stq.n),))
        sk_own_ld = lds.n
        b.op("gpsimd", lambda e: e.indirect_dma_start(
            sk[:, NCH * T:2 * NCH * T],
            None, ag_out, IndirectOffsetOnAxis(idx_ln[:, :], 0)),
            waits=((agq, agq.n), (ldig, ldig.n), (b.ch["gpsimd"], b.ch["gpsimd"].n)),
            incs=((ldig, 16),))
        gath_done = ldig.n
        b.dma("sync", lambda e: e.dma_start(out=swattf[:, :], in_=watt_in), lds)
        b.dma("sync", lambda e: e.dma_start(out=soutb[:, :], in_=oub_in), lds)
        swvt2 = b.sb("swvtB", [P, NCH * D], BF16, 174848)
        swvbb2 = b.sb("swvbbB", [1, D], BF16, 191232)
        sonesr = b.sb("sonesrB", [1, P], BF16, 193280)
        b.op("gpsimd", lambda e: e.dma_start(
            out=r3(swvt2, D),
            in_=fx.dram("wvT", [D, D], F32, "win").rearrange("(c p) t -> p c t", p=P)),
            waits=((b.ch["gpsimd"], b.ch["gpsimd"].n),), incs=((ldg, 16),))
        ldwv2 = ldg.n
        b.op("gpsimd", lambda e: e.dma_start(
            out=swvbb2[:, :], in_=fx.dram("wvb", [1, D], F32, "win")),
            waits=((b.ch["gpsimd"], b.ch["gpsimd"].n),), incs=((ldg, 16),))
        wvb_ld = ldg.n
        b.chain("vector", lambda e: e.memset(
            sv[:, :].rearrange("p (g f) -> p g f", f=GW)[:, :, 64:65], 1.0))
        b.chain("vector", lambda e: e.memset(sonesr[:, :], 1.0))
    else:
        gath_done = 0
        b.dma("sync", lambda e: e.dma_start(out=sk[:, 0:NCH * S], in_=k_in), lds)
        sk_own_ld = lds.n
        b.dma("sync", lambda e: e.dma_start(out=sv[:, :], in_=v_in), lds)
        b.dma("sync", lambda e: e.dma_start(out=swattf[:, :], in_=watt_in), lds)
        b.dma("sync", lambda e: e.dma_start(out=soutb[:, :], in_=oub_in), lds)
    if not fx.fused:
        b.op("gpsimd", lambda e: e.dma_start(
            out=r3(souw, D), in_=ouw_in.rearrange("(c p) t -> p c t", p=P)),
            incs=((ldg, 16),))
    b.chain("vector", lambda e: e.memset(sneg20[:, :], -20.0))
    b.chain("vector", lambda e: e.memset(ones_row[:, :], 1.0))
    vw0 = b.chain("vector", lambda e: e.tensor_scalar(
        swatt[:, :], swattf[:, :], 0.125, None, ALU.mult),
        waits=((lds, lds.n),))

    vxev = {}
    pvx = [fx.PD[:, 0:TC], fx.PD[:, TC:2 * TC]]
    vx_evc = {0: 0, 1: 0}

    def emit_vx(half_src):
        src_base = half_src * NCH * T
        dst_base = half_src * AGW
        for tk in range(T // P):
            for half in range(2):
                ps = pvx[half]
                for dch in range(NCH):
                    w = []
                    if dch == 0:
                        w.append((b.ch["vector"], vx_evc[half]))
                        if tk == 0 and half == 0:
                            w.append((ldg, max(ldwv2, wvb_ld)))
                            w.append((lds, sk_own_ld))
                            if half_src == 1:
                                w.append((ldig, gath_done))
                    b.op("tensor",
                         (lambda e, tk=tk, half=half, dch=dch, ps=ps: e.matmul(
                             ps[:, :],
                             sk[:, src_base + dch * T + tk * P:
                                src_base + dch * T + (tk + 1) * P],
                             swvt2[:, dch * D + half * TC: dch * D + half * TC + TC],
                             start=(dch == 0), stop=False)),
                         waits=tuple(w))
                b.op("tensor",
                     (lambda e, half=half, ps=ps: e.matmul(
                         ps[:, :], sonesr[0:1, :],
                         swvbb2[0:1, half * TC: half * TC + TC],
                         start=False, stop=True)),
                     incs=((tq, 1),))
                td = tq.n
                vx_evc[half] = b.chain(
                    "vector",
                    (lambda e, tk=tk, half=half, ps=ps: e.tensor_copy(
                        sv[:, dst_base + tk * VB + half * 8 * GW:
                           dst_base + tk * VB + (half + 1) * 8 * GW].rearrange(
                               "p (g f) -> p g f", f=GW)[:, :, 0:64],
                        ps[:, :].rearrange("p (g f) -> p g f", f=64))),
                    waits=((tq, td),))
            vxev[half_src * (T // P) + tk] = max(vx_evc[0], vx_evc[1])
        return tq.n

    if fx.fused:
        emit_vx(0)

    ev_xw = 0
    ev_fin_par = {0: b.ch["vector"].n, 1: b.ch["vector"].n}
    exp_cnt = {}
    av_done = {}
    pending = None

    def emit_deferred():
        nonlocal pending
        pj, pqc, ppar, pev_s = pending
        pA, pB = p_o[ppar]
        b.op("tensor", (lambda e, pA=pA: e.matmul(
            pA[64:128, :], ones_row[0:1, 0:64], ssr[0:1, 0:TC],
            start=True, stop=True, tile_position=(0, 64))),
            waits=((b.ch["vector"], pev_s),))
        b.op("tensor", (lambda e, pB=pB: e.matmul(
            pB[64:128, :], ones_row[0:1, 0:64], ssr[0:1, TC:2 * TC],
            start=True, stop=True, tile_position=(0, 64))),
            incs=((tq, 1),))
        t_rs = tq.n
        b.chain("vector", (lambda e, pA=pA: e.tensor_copy(
            sot[0:64, :], pA[0:64, :])), waits=((tq, t_rs),))
        b.chain("vector", (lambda e, pB=pB: e.tensor_copy(
            sot[64:128, :], pB[0:64, :])))
        b.chain("vector", (lambda e, pj=pj, pqc=pqc, pA=pA: e.tensor_tensor(
            so[0:64, pj * T + pqc * TC: pj * T + pqc * TC + TC],
            sot[0:64, :], pA[64:128, :], ALU.mult)))
        ev_fin_par[ppar] = b.chain("vector",
            (lambda e, pj=pj, pqc=pqc, pB=pB: e.tensor_tensor(
                so[64:128, pj * T + pqc * TC: pj * T + pqc * TC + TC],
                sot[64:128, :], pB[64:128, :], ALU.mult)))
        pending = None

    def emit_xw_qb(jj, qb):
        nonlocal ev_xw
        tgt = p_att[0][:, 0:TC] if jj == 0 else p_o[0][0]
        w = []
        if jj == 0:
            w.append((lds, lds.n))
            w.append((b.ch["vector"], max(vw0, ev_xw)))
        elif qb == 0:
            w.append((b.ch["vector"], ev_fin_par[0]))
        else:
            w.append((b.ch["vector"], ev_xw))
        b.op("tensor", (lambda e, jj=jj, qb=qb, tgt=tgt: e.matmul(
            tgt[0:64, :], swatt[0:64, jj * 64:(jj + 1) * 64],
            sk[0:64, jj * T + qb * TC: jj * T + qb * TC + TC],
            start=True, stop=True, tile_position=(0, 0))),
            waits=tuple(w))
        b.op("tensor", (lambda e, jj=jj, qb=qb, tgt=tgt: e.matmul(
            tgt[64:128, :], swatt[64:128, jj * 64:(jj + 1) * 64],
            sk[64:128, jj * T + qb * TC: jj * T + qb * TC + TC],
            start=True, stop=True, tile_position=(64, 64))),
            incs=((tq, 1),))
        td = tq.n
        ev_xw = b.chain("vector", (lambda e, jj=jj, qb=qb, tgt=tgt: e.tensor_copy(
            sxw[jj % 2][:, qb * TC:(qb + 1) * TC], tgt[:, :])),
            waits=((tq, td),))
        if jj > 0:
            ev_fin_par[0] = ev_xw

    for j in range(n_pairs):
        if j == 0:
            for qb in range(NTC):
                emit_xw_qb(0, qb)
        for qc in range(NTC):
            NKB = S // P
            par = (j * NTC + qc) % 2
            pA, pB = p_o[par]

            def kcol(j, kb):
                if kb < NKB // 2:
                    return j * T + kb * P
                return NCH * T + j * T + (kb - NKB // 2) * P

            def svcol(j, kb, h):
                base = kb * VB if kb < NKB // 2 else AGW + (kb - NKB // 2) * VB
                return base + (2 * j + h) * GW

            att_done = {}

            def emit_att(kb, j=j, qc=qc):
                buf = kb % 2
                w = []
                if kb < 2:
                    w.append((b.ch["vector"], ev_xw))
                if fx.fused and kb == NKB // 2:
                    w.append((ldig, gath_done))
                if (j, qc, kb - 2) in exp_cnt:
                    w.append((b.ch["scalar"], exp_cnt[(j, qc, kb - 2)]))
                elif kb < 2 and (qc > 0 or j > 0):
                    prev = exp_cnt.get((j, qc - 1, NKB - 2 + buf)) if qc > 0 \
                        else exp_cnt.get((j - 1, NTC - 1, NKB - 2 + buf))
                    if prev is not None:
                        w.append((b.ch["scalar"], prev))
                kc = kcol(j, kb)
                b.op("tensor", (lambda e, kc=kc, qc=qc, buf=buf, j=j: e.matmul(
                    p_att[buf][:, 0:TC],
                    sk[0:64, kc: kc + P],
                    sxw[j % 2][0:64, qc * TC:(qc + 1) * TC],
                    start=True, stop=True, tile_position=(0, 0))),
                    waits=tuple(w))
                b.op("tensor", (lambda e, kc=kc, qc=qc, buf=buf, j=j: e.matmul(
                    p_att[buf][:, TC:2 * TC],
                    sk[64:128, kc: kc + P],
                    sxw[j % 2][64:128, qc * TC:(qc + 1) * TC],
                    start=True, stop=True, tile_position=(64, 0))),
                    incs=((tq, 1),))
                att_done[kb] = tq.n

            emit_att(0)
            emit_att(1)
            for kb in range(NKB):
                buf = kb % 2
                sw = [(tq, att_done[kb])]
                if (j, qc, kb - 2) in av_done:
                    sw.append((tq, av_done[(j, qc, kb - 2)]))
                exp_cnt[(j, qc, kb)] = b.chain("scalar", (lambda e, buf=buf: e.activation(
                    sp[buf][:, :], p_att[buf][:, :], AF.Exp, bias=sneg20[:, 0:1])),
                    waits=tuple(sw))
                if kb + 2 < NKB:
                    emit_att(kb + 2)
                if kb == 9 and pending is not None:
                    emit_deferred()
                if (qc == NTC - 1 and j + 1 < n_pairs
                        and kb in (13, 16, 19, 22)):
                    emit_xw_qb(j + 1, (kb - 13) // 3)
                if fx.fused and j == 0 and qc == 0 and kb == NKB // 2:
                    vx_par_done = emit_vx(1)
                    ev_fin_par[1] = max(vx_evc[0], vx_evc[1])
                    b.op("gpsimd", (lambda e, v=vx_par_done: e.dma_start(
                        out=r3(souw, D),
                        in_=ouw_in.rearrange("(c p) t -> p c t", p=P))),
                        waits=((tq, vx_par_done), (b.ch["gpsimd"], b.ch["gpsimd"].n)),
                        incs=((ldg, 16),))
                aw = [(b.ch["scalar"], exp_cnt[(j, qc, kb)])]
                if kb == 0:
                    aw.append((b.ch["vector"], ev_fin_par[par]))
                if fx.fused and j == 0 and qc == 0:
                    aw.append((b.ch["vector"], vxev[kb]))
                cA = svcol(j, kb, 0)
                cB = svcol(j, kb, 1)
                b.op("tensor", (lambda e, cA=cA, kb=kb, buf=buf, pA=pA: e.matmul(
                    pA[0:GW, :], sv[:, cA:cA + GW],
                    sp[buf][:, 0:TC],
                    start=(kb == 0), stop=(kb == NKB - 1), tile_position=(0, 0))),
                    waits=tuple(aw))
                b.op("tensor", (lambda e, cB=cB, kb=kb, buf=buf, pB=pB: e.matmul(
                    pB[0:GW, :], sv[:, cB:cB + GW],
                    sp[buf][:, TC:2 * TC],
                    start=(kb == 0), stop=(kb == NKB - 1), tile_position=(0, 0))),
                    incs=((tq, 1),))
                av_done[(j, qc, kb)] = tq.n
            tdav = av_done[(j, qc, NKB - 1)]
            b.chain("vector", (lambda e, pA=pA: e.tensor_copy(
                ssr[0:1, 0:TC], pA[64:65, :])), waits=((tq, tdav),))
            b.chain("vector", (lambda e, pB=pB: e.tensor_copy(
                ssr[0:1, TC:2 * TC], pB[64:65, :])))
            ev_s = b.chain("vector", lambda e: e.reciprocal(ssr[0:1, :], ssr[0:1, :]))
            pending = (j, qc, par, ev_s)
    emit_deferred()
    avn = tq.n

    ld_all = lds.n
    ldg_all = ldg.n
    if not do_outproj:
        b.op("gpsimd", lambda e, v=stq.n: e.wait_ge(stq.h, v))
        return dict(stq=stq, lds=lds, ldg=ldg, ldig=ldig)
    v_o_all = b.ch["vector"].n
    ev_fin = b.ch["vector"].n
    x3sb = None
    if fx.fused and getattr(fx, "pre_c", None) and "o_x3" not in fx.force_out:
        x3sb = fx.pre_c["sx3"]
    x2ld = {}
    x2war = {}
    sc_hist = {}
    st_hist = {}
    tiles = [(qc, mch) for qc in range(NTC) for mch in range(NCH)]
    for pf in range(2):
        b.op("sync", (lambda e, pf=pf: e.dma_start(
            out=sx2t[pf][:, :],
            in_=r3(x2_in, T)[:, tiles[pf][1], tiles[pf][0] * TC:(tiles[pf][0] + 1) * TC])),
            waits=((lds, lds.n), (tq, avn)), incs=((lds, 16),))
        x2ld[pf] = lds.n
    for idx, (qc, mch) in enumerate(tiles):
        buf = idx % 3
        if idx + 2 < NCH * NTC:
            nqc, nmch = tiles[idx + 2]
            pw = ()
            if (idx - 1) in x2war:
                pw = ((b.ch["vector"], x2war[idx - 1]),)
            elif idx == 0:
                pw = ((tq, avn),)
            b.op("sync", (lambda e, nmch=nmch, nqc=nqc, nbuf=(idx + 2) % 3:
                          e.dma_start(out=sx2t[nbuf][:, :],
                                      in_=r3(x2_in, T)[:, nmch, nqc * TC:(nqc + 1) * TC])),
                 waits=pw, incs=((lds, 16),))
            x2ld[idx + 2] = lds.n
        ps = p_att[idx % 2]
        for ech in range(NCH):
            w = []
            if ech == 0:
                w.append((b.ch["vector"], sc_hist.get(idx - 2, v_o_all)))
                w.append((ldg, ldg_all))
            b.op("tensor", (lambda e, mch=mch, qc=qc, ech=ech, ps=ps: e.matmul(
                ps[:, 0:TC], souw[:, ech * D + mch * P: ech * D + (mch + 1) * P],
                so[:, ech * T + qc * TC: ech * T + qc * TC + TC],
                start=(ech == 0), stop=(ech == NCH - 1))),
                waits=tuple(w),
                incs=((tq, 1),) if ech == NCH - 1 else ())
        td = tq.n
        scw = [(tq, td), (lds, ld_all)]
        if x3sb is None and idx - 2 in st_hist:
            scw.append((stq, st_hist[idx - 2]))
        sc_hist[idx] = b.chain("vector", (lambda e, mch=mch, sbuf=idx % 2, ps=ps: e.tensor_scalar(
            sfin[sbuf][:, :], ps[:, 0:TC], soutb[:, mch:mch + 1], None, ALU.add)),
            waits=tuple(scw))
        if x3sb is not None:
            ev_fin = b.chain("vector", (lambda e, mch=mch, qc=qc, buf=buf,
                                        sbuf=idx % 2: e.tensor_tensor(
                r3(x3sb, T)[:, mch, qc * TC:(qc + 1) * TC],
                sfin[sbuf][:, :], sx2t[buf][:, :], ALU.add)),
                waits=((lds, x2ld[idx]),))
        else:
            ev_fin = b.chain("vector", (lambda e, buf=buf, sbuf=idx % 2: e.tensor_tensor(
                sfin[sbuf][:, :], sfin[sbuf][:, :], sx2t[buf][:, :], ALU.add)),
                waits=((lds, x2ld[idx]),))
            b.op("gpsimd", (lambda e, mch=mch, qc=qc, buf=buf: e.dma_start(
                out=r3(o_x3, T)[:, mch, qc * TC:(qc + 1) * TC], in_=sfin[buf][:, :])),
                waits=((b.ch["vector"], ev_fin), (b.ch["gpsimd"], b.ch["gpsimd"].n)),
                incs=((stq, 16),))
            st_hist[idx] = stq.n
        x2war[idx] = ev_fin
    b.op("gpsimd", lambda e, v=stq.n: e.wait_ge(stq.h, v))
    return dict(stq=stq, lds=lds, ldg=ldg, ldig=ldig, tq=tq, avn=avn)


def build_b():
    fx = Fx()
    emit_b(fx)
    return fx.b.finalize()


def emit_c(fx):
    nc, b = fx.nc, fx.b
    b.pfx = "c_"
    x3_in = fx.dram("o_x3", [P, NCH * T], F32, "in")
    kern_in = fx.dram("kern", [D, D], BF16, "win")
    sin_in = fx.dram("s_in", [1, 1], F32, "win")
    sk_in = fx.dram("s_k", [1, 1], F32, "win")
    l1b_in = fx.dram("l1bias", [P, NCH], BF16, "win")

    o_x4 = fx.dram("o_x4", [P, NCH * T], F32, "result")

    pre_c = getattr(fx, "pre_c", None)
    sx3 = pre_c["sx3"] if pre_c else b.sb("sx3", [P, NCH * T], F32, 0)
    sxq = b.sb("sxq", [P, NCH * T], FP8, 64 * KB)
    skq = b.sb("skq", [P, NCH * D], FP8, 80 * KB)
    skern = pre_c["skern"] if pre_c else b.sb("skern", [P, NCH * D], BF16, 88 * KB)
    sln4 = b.sb("sln4", [P, NCH * TC], BF16, 104 * KB)
    sy = [b.sb("sy0", [P, TC], BF16, 112 * KB),
          b.sb("sy1", [P, TC], BF16, 112 * KB + 1024)]
    sscr = pre_c["sscr"] if pre_c else b.sb("sscr", [1, 8], F32, 170 * KB)
    sl1b = pre_c["sl1b"] if pre_c else b.sb("sl1b", [P, NCH], BF16, 170 * KB + 64)
    scols = b.sb("scols", [P, 8], F32, 170 * KB + 128)
    sl1bf = b.sb("sl1bf", [P, NCH], F32, 170 * KB + 192)
    sqt = b.sb("sqtC", [P, NCH * TC], F32, 114 * KB)
    env = std_env(b, nc, 130 * KB, fx, pre=pre_c)
    tq, lds, ldg, stq = env["tq"], env["lds"], env["ldg"], env["stq"]

    if not pre_c:
        b.dma("sync", lambda e: e.dma_start(out=sx3[:, :], in_=x3_in), lds)
        b.op("sync", lambda e: e.dma_start(out=r3(skern, D),
                                           in_=kern_in.rearrange("(c p) t -> p c t", p=P)),
             incs=((lds, 16),))
        b.dma("sync", lambda e: e.dma_start(out=sscr[0:1, 0:1], in_=sin_in), lds)
        b.dma("sync", lambda e: e.dma_start(out=sscr[0:1, 1:2], in_=sk_in), lds)
        b.dma("sync", lambda e: e.dma_start(out=sl1b[:, :], in_=l1b_in), lds)
    ld0 = lds.n

    b.chain("vector", lambda e: e.tensor_mul(
        sscr[0:1, 2:3], sscr[0:1, 0:1], sscr[0:1, 1:2]), waits=((lds, ld0),))
    b.chain("vector", lambda e: e.reciprocal(sscr[0:1, 3:4], sscr[0:1, 0:1]))
    vsc = b.chain("vector", lambda e: e.reciprocal(sscr[0:1, 4:5], sscr[0:1, 1:2]))
    b.op("tensor", lambda e: e.matmul(env["pk0"][:, 0:5], env["ones_row"][0:1, :],
                                      sscr[0:1, 0:5], start=True, stop=True),
         waits=((b.ch["vector"], vsc),), incs=((tq, 1),))
    tsc = tq.n
    vcols = b.chain("vector", lambda e: e.tensor_copy(scols[:, 0:5], env["pk0"][:, 0:5]),
                    waits=((tq, tsc),))
    b.chain("vector", lambda e: e.tensor_copy(sl1bf[:, :], sl1b[:, :]))
    vkq = b.chain("vector", lambda e: e.tensor_scalar(
        skq[:, :], skern[:, :], scols[:, 4:5], None, ALU.mult))

    for tc in range(NTC):
        tbc = emit_ln_tc(b, env, slc3(sx3, T, tc))
        b.chain("vector", lambda e, tc=tc: e.tensor_tensor(
            r3(sqt, TC), slc3(sx3, T, tc), bc3(env["pbc_m"]), ALU.subtract),
            waits=((tq, tbc),))
        b.chain("vector", lambda e: e.tensor_tensor(
            r3(sln4, TC), r3(sqt, TC), bc3(env["pbc_r"]), ALU.mult))
        b.chain("vector", lambda e, tc=tc: e.tensor_scalar(
            slc3(sxq, T, tc), r3(sln4, TC), scols[:, 3:4], None, ALU.mult))
    vq_all = b.ch["vector"].n

    scv_hist = {}
    for ci, (och, tcq) in enumerate([(o_, t_) for o_ in range(NCH)
                                     for t_ in range(NTC)]):
        key = ci % 2
        ps = env["pk0"] if key == 0 else env["pk1"]
        for ich in range(NCH):
            w = []
            if ich == 0:
                w.append((b.ch["vector"],
                          scv_hist.get(ci - 2, max(vq_all, vkq))))
            b.op("tensor",
                 (lambda e, och=och, tcq=tcq, ich=ich, ps=ps:
                  e.matmul(ps[:, :],
                           skq[:, ich * D + och * P: ich * D + (och + 1) * P],
                           sxq[:, ich * T + tcq * TC: ich * T + tcq * TC + TC],
                           start=(ich == 0), stop=(ich == NCH - 1))),
                 waits=tuple(w),
                 incs=((tq, 1),) if ich == NCH - 1 else ())
        td = tq.n
        scv_hist[ci] = b.chain("vector", (lambda e, ps=ps, key=key: e.tensor_scalar(
            sy[key][:, :], ps[:, :], scols[:, 2:3], None, ALU.mult)),
            waits=((tq, td), (b.ch["vector"], vcols)))
        b.chain("vector", (lambda e, och=och, key=key: e.tensor_scalar(
            sy[key][:, :], sy[key][:, :], sl1bf[:, och:och + 1], None, ALU.add)))
        tile_ev = b.chain("vector", (lambda e, och=och, tcq=tcq, key=key: e.tensor_tensor(
            sx3[:, och * T + tcq * TC: och * T + tcq * TC + TC],
            sx3[:, och * T + tcq * TC: och * T + tcq * TC + TC],
            sy[key][:, :], ALU.add)))
        b.op("gpsimd", (lambda e, och=och, tcq=tcq: e.dma_start(
            out=o_x4[:, och * T + tcq * TC: och * T + tcq * TC + TC],
            in_=sx3[:, och * T + tcq * TC: och * T + tcq * TC + TC])),
            waits=((b.ch["vector"], tile_ev), (b.ch["gpsimd"], b.ch["gpsimd"].n)),
            incs=((stq, 16),))
    b.op("gpsimd", lambda e, v=stq.n: e.wait_ge(stq.h, v))
    return dict(stq=stq)


def build_c():
    fx = Fx()
    emit_c(fx)
    return fx.b.finalize()


def build_full(upto="c", force_out=(), b_pairs=NCH, b_outproj=True, b_loads=5,
               b_gmode="normal"):
    fx = Fx(fused=True)
    fx.b_gmode = b_gmode
    fx.force_out = set(force_out)
    fx.b_pairs = b_pairs
    fx.b_outproj = b_outproj
    fx.b_loads = b_loads
    nc, b = fx.nc, fx.b
    fx.ag_in = nc.dram_tensor("ag_in", [P, NCH * T], BF16, kind="Internal").ap()
    ag_out = nc.dram_tensor("ag_out", [8 * P, NCH * T], BF16,
                            kind="Internal", addr_space="Shared").ap()
    rs_in = nc.dram_tensor("rs_in", [8 * NCH, P], F32, kind="Internal").ap()
    rs_out = nc.dram_tensor("rs_out", [NCH, P], F32, kind="Internal").ap()
    mask_in = fx.dram("hostmask", [1, 8], F32, "win")
    idn_in = fx.dram("idn", [P, P], F32, "win")
    idx_ln = fx.dram("agidx_ln", [P, 1], I32, "win")

    b.pfx = "g_"
    G2 = 64 * KB
    sidn = b.sb("sidn", [P, P], F32, G2)
    smaskr = b.sb("smaskr", [P, NCH], F32, G2 + 512)
    smt = b.sb("smt", [P, 8 * NCH], F32, G2 + 576)
    smtT = b.sb("smtT", [8 * NCH, P], F32, G2 + 832)
    srsb = b.sb("srsb", [NCH, P], F32, G2 + 1344)
    GT = 210560
    spref_f = b.sb("spref_f", [P, NCH], F32, GT)
    sidx1 = b.sb("sidx1", [P, 1], I32, GT + 64)

    glq = b.sem("glq")
    glg = b.sem("glg")
    ccq = b.sem("ccq")
    fx.ext_spref = spref_f

    b.dma("sync", lambda e: e.dma_start(out=sidx1[:, :], in_=idx_ln), glq)

    a1 = emit_a1(fx)
    b.pfx = "g_"
    stot = a1["stot"]
    tq1 = a1["env"]["tq"]
    b.dma("sync", lambda e: e.dma_start(out=sidn[:, :], in_=idn_in), glq,
          waits=((a1["stq"], a1["stq"].n),))
    idn_ld = glq.n
    b.dma("sync", lambda e: e.dma_start(
        out=smaskr[0:1, :], in_=mask_in), glq)
    mask_ld = glq.n
    b.op("tensor", lambda e: e.matmul(fx.PD[:, 0:NCH], a1["env"]["ones_row"][0:1, :],
                                      smaskr[0:1, 0:NCH], start=True, stop=True),
         waits=((glq, mask_ld),), incs=((tq1, 1),))
    tmr = tq1.n
    vmr = b.chain("vector", lambda e: e.tensor_copy(smaskr[:, :], fx.PD[:, 0:NCH]),
                  waits=((tq1, tmr),))
    for s_ in range(NCH):
        b.chain("vector", lambda e, s_=s_: e.tensor_tensor(
            smt[:, s_ * NCH:(s_ + 1) * NCH], stot[:, :],
            smaskr[:, s_:s_ + 1].broadcast_to([P, NCH]), ALU.mult))
    vmt = b.ch["vector"].n
    b.op("tensor", lambda e: e.transpose(fx.PD[0:8 * NCH, TC:TC + P], smt[:, :], sidn[:, :]),
         waits=((b.ch["vector"], vmt), (glq, idn_ld)), incs=((tq1, 1),))
    tmt = tq1.n
    vmtT = b.chain("vector", lambda e: e.tensor_copy(smtT[:, :], fx.PD[0:8 * NCH, TC:TC + P]),
                   waits=((tq1, tmt),))
    b.dma("gpsimd", lambda e: e.dma_start(out=rs_in, in_=smtT[:, :]), glg,
          waits=((b.ch["vector"], vmtT), (b.ch["gpsimd"], b.ch["gpsimd"].n)))
    rsw = glg.n
    b.op("gpsimd", lambda e: e.collective_compute(
        "ReduceScatter", ALU.add, replica_groups=[list(range(8))],
        ins=[rs_in], outs=[rs_out]),
        waits=((glg, rsw),), incs=((ccq, 1),))
    b.dma("gpsimd", lambda e: e.dma_start(out=srsb[:, :], in_=rs_out), glg,
          waits=((ccq, 1),))
    rsrd = glg.n
    b.op("tensor", lambda e: e.transpose(fx.PD[0:P, TC:TC + NCH], srsb[:, :],
                                         sidn[0:NCH, 0:NCH]),
         waits=((glg, rsrd),), incs=((tq1, 1),))
    tpr = tq1.n
    vpr = b.chain("vector", lambda e: e.tensor_copy(spref_f[:, :], fx.PD[0:P, TC:TC + NCH]),
                  waits=((tq1, tpr),))
    b.pfx = "a2_"
    pre_sx = b.sb("sx", [P, NCH * T], F32, 0)
    pre_sfkv = b.sb("sfkv", [P, NCH * T], BF16, 96 * KB)
    pre_lds = b.sem("lds")
    b.dma("sync", lambda e: e.dma_start(
        out=pre_sx[:, :], in_=fx.dram("o_xmid", [P, NCH * T], F32, "in")),
        pre_lds, waits=((a1["stq"], a1["stq"].n),))
    b.dma("sync", lambda e: e.dma_start(
        out=pre_sfkv[:, :], in_=fx.dram("o_fkv", [P, NCH * T], BF16, "in")),
        pre_lds)
    fx.pre = {"sx": pre_sx, "sfkv": pre_sfkv, "lds": pre_lds}
    b.pfx = "g_"
    if upto == "a1":
        return fx.b.finalize()
    emit_barrier(fx, dma_sems=(a1["lds"], a1["ldg"], a1["stq"], glq, glg))

    a2 = emit_a2(fx)
    b.pfx = "g_"
    b.op("gpsimd", lambda e: e.collective_compute(
        "AllGather", ALU.bypass, replica_groups=[list(range(8))],
        ins=[fx.ag_in], outs=[ag_out]),
        waits=((a2["stq"], a2["stq"].n), (b.ch["gpsimd"], b.ch["gpsimd"].n)),
        incs=((ccq, 1),))
    agq = ccq
    fx.ag = (agq, ag_out, sidx1, a2["stq"])
    if upto == "a2":
        return fx.b.finalize()
    emit_barrier(fx, dma_sems=(a2["lds"], a2["ldg"], a2["stq"], glq, glg))
    if upto == "bar2":
        return fx.b.finalize()

    b.pfx = "c_"
    pc_sx3 = b.sb("sx3", [P, NCH * T], F32, 0)
    pc_skern = b.sb("skern", [P, NCH * D], BF16, 88 * KB)
    pc_sscr = b.sb("sscr", [1, 8], F32, 170 * KB)
    pc_sl1b = b.sb("sl1b", [P, NCH], BF16, 170 * KB + 64)
    pc_lds = b.sem("lds")
    fx.pre_c = {"sx3": pc_sx3, "skern": pc_skern, "sscr": pc_sscr,
                "sl1b": pc_sl1b, "lds": pc_lds}
    b.pfx = "g_"
    bb = emit_b(fx)
    b.pfx = "c_"
    b.op("sync", lambda e: e.dma_start(
        out=r3(pc_skern, D),
        in_=fx.dram("kern", [D, D], BF16, "win").rearrange("(c p) t -> p c t", p=P)),
        waits=((bb["tq"], bb["avn"]), (pc_lds, pc_lds.n)), incs=((pc_lds, 16),))
    b.dma("sync", lambda e: e.dma_start(
        out=pc_sscr[0:1, 0:1], in_=fx.dram("s_in", [1, 1], F32, "win")), pc_lds)
    b.dma("sync", lambda e: e.dma_start(
        out=pc_sscr[0:1, 1:2], in_=fx.dram("s_k", [1, 1], F32, "win")), pc_lds)
    b.dma("sync", lambda e: e.dma_start(
        out=pc_sl1b[:, :], in_=fx.dram("l1bias", [P, NCH], BF16, "win")), pc_lds)
    b.pfx = "g_"
    if upto == "b":
        return fx.b.finalize()
    emit_barrier(fx, dma_sems=(bb["lds"], bb["ldg"], bb["stq"], bb["ldig"]))

    emit_c(fx)
    return fx.b.finalize()


FUSED = True
_NC_CACHE = {}


def _get_nc(name):
    if name not in _NC_CACHE:
        _NC_CACHE[name] = {"a1": build_a1, "a2": build_a2, "b": build_b,
                           "c": build_c, "full": build_full}[name]()
    return _NC_CACHE[name]


def _host_arrays(inputs):
    x = np.asarray(inputs["L0_attn_x"])
    Cb, IRb = get_consts()
    h = {}
    h["Cb"] = np.asarray(Cb)
    h["IRb"] = np.asarray(IRb)
    h["Wq"] = np.asarray(inputs["L0_attn_Wq"])
    h["Wk"] = np.asarray(inputs["L0_attn_Wk"])
    h["Wv"] = np.asarray(inputs["L0_attn_Wv"])
    h["ternT"] = np.ascontiguousarray(np.asarray(inputs["L0_ffn_ternary_weight"]).T)
    h["l0bias"] = vcol(np.asarray(inputs["L0_ffn_bias"]))
    h["ascale"] = np.asarray(inputs["L0_ffn_act_scale"], np.float32).reshape(1, 1)
    watt = np.asarray(inputs["L1_attn_w_att_val"])
    ws = np.zeros((P, NCH * 64), np.float32)
    for j in range(NCH):
        ws[0:64, j * 64:(j + 1) * 64] = watt[2 * j]
        ws[64:128, j * 64:(j + 1) * 64] = watt[2 * j + 1]
    h["watt"] = ws
    h["wvT"] = np.ascontiguousarray(np.asarray(inputs["L1_attn_wv_w"]).T)
    h["wvb"] = np.asarray(inputs["L1_attn_wv_b"]).reshape(1, D)
    h["outwT"] = np.ascontiguousarray(np.asarray(inputs["L1_attn_out_w"]).T)
    h["outb"] = vcol(np.asarray(inputs["L1_attn_out_b"]))
    h["kern"] = np.asarray(inputs["L1_ffn_kernel"])
    h["s_in"] = np.asarray(inputs["L1_ffn_input_scale"], np.float32).reshape(1, 1)
    h["s_k"] = np.asarray(inputs["L1_ffn_kernel_scale"], np.float32).reshape(1, 1)
    h["l1bias"] = np.asarray(inputs["L1_ffn_bias"]).reshape(NCH, P).T.copy()
    xT = []
    for c in range(8):
        bb, hh = c // 2, c % 2
        xT.append(np.ascontiguousarray(x[bb, hh * T:(hh + 1) * T, :].T))
    return h, xT


def kernel(**inputs):
    h, xT = _host_arrays(inputs)
    cores = list(range(8))
    if FUSED:
        idn = np.eye(P, dtype=np.float32)
        in_maps = []
        for c in cores:
            mask = np.zeros((1, 8), np.float32)
            if c % 2 == 0:
                mask[0, c + 1] = 1.0
            pr = c ^ 1
            m = dict(h)
            m["xT"] = xT[c]
            m["hostmask"] = mask
            m["idn"] = idn
            m["agidx_ln"] = (pr * P + np.arange(P)).astype(np.int32).reshape(P, 1)
            in_maps.append(m)
        res = run_bass_kernel_spmd(_get_nc("full"), in_maps, core_ids=cores).results
        out = np.empty((4, 2 * T, D), np.float32)
        for c in cores:
            bb, hh = c // 2, c % 2
            out[bb, hh * T:(hh + 1) * T, :] = fm_inv(np.asarray(res[c]["o_x4"]), T).T
        return out

    in1 = [{"xT": xT[c], "Wk": h["Wk"], "Wv": h["Wv"], "Cb": h["Cb"]} for c in cores]
    r1 = run_bass_kernel_spmd(_get_nc("a1"), in1, core_ids=cores).results
    in2 = []
    for c in cores:
        pref = (np.zeros((P, NCH), np.float32) if c % 2 == 0
                else np.asarray(r1[c - 1]["o_tot"], np.float32))
        in2.append({"o_fkv": r1[c]["o_fkv"], "o_ln1": r1[c]["o_ln1"],
                    "o_xmid": r1[c]["o_xmid"], "prefix": pref,
                    "Wq": h["Wq"], "Cb": h["Cb"], "IRb": h["IRb"],
                    "ternT": h["ternT"], "l0bias": h["l0bias"],
                    "ascale": h["ascale"], "wvT": h["wvT"], "wvb": h["wvb"]})
    r2 = run_bass_kernel_spmd(_get_nc("a2"), in2, core_ids=cores).results
    in3 = []
    for c in cores:
        p = c ^ 1
        lnf = np.concatenate([np.asarray(r2[c]["o_ln3"]),
                              np.asarray(r2[p]["o_ln3"])], axis=1)
        vx2 = np.stack([np.asarray(r2[c]["o_vx"]),
                        np.asarray(r2[p]["o_vx"])])
        vxf = np.ones((2, P, T // P, 16, GW), vx2.dtype)
        vxf[..., 0:64] = vx2.reshape(2, P, T // P, 16, 64)
        vxf = vxf.transpose(1, 0, 2, 3, 4).reshape(P, 2 * AGW)
        in3.append({"ln3full": np.ascontiguousarray(lnf),
                    "vxfull": np.ascontiguousarray(vxf),
                    "o_x2": r2[c]["o_x2"], "watt": h["watt"],
                    "outwT": h["outwT"], "outb": h["outb"]})
    r3_ = run_bass_kernel_spmd(_get_nc("b"), in3, core_ids=cores).results
    in4 = [{"o_x3": r3_[c]["o_x3"], "kern": h["kern"], "s_in": h["s_in"],
            "s_k": h["s_k"], "l1bias": h["l1bias"]} for c in cores]
    r4 = run_bass_kernel_spmd(_get_nc("c"), in4, core_ids=cores).results
    out = np.empty((4, 2 * T, D), np.float32)
    for c in cores:
        bb, hh = c // 2, c % 2
        out[bb, hh * T:(hh + 1) * T, :] = fm_inv(np.asarray(r4[c]["o_x4"]), T).T
    return out


# version 21
# speedup vs baseline: 1.0591x; 1.0303x over previous
import numpy as np
import ml_dtypes
import concourse.bass as bass
import concourse.mybir as mybir
from concourse.bass_utils import run_bass_kernel_spmd
from concourse.mybir import AluOpType as ALU, ActivationFunctionType as AF
from concourse.bass_core_rust import IndirectOffsetOnAxis

F32 = mybir.dt.float32
BF16 = mybir.dt.bfloat16
I32 = mybir.dt.int32
FP8 = mybir.dt.float8e4

P = 128
D = 1024
NCH = D // P
TC = 512
T = 2048
NTC = T // TC
S = 2 * T
MAGIC = 12582912.0
NBF = ml_dtypes.bfloat16
X = mybir.AxisListType.X
KB = 1024
GW = 65
VB = 16 * GW
AGW = (T // P) * VB


def _dft_consts():
    e = np.arange(D)[:, None].astype(np.float64)
    C = np.zeros((D, D), np.float64)
    f_re = np.arange(512)[None, :].astype(np.float64)
    C[:, 0:512] = np.cos(2 * np.pi * e * f_re / D)
    C[:, 512] = np.cos(np.pi * e[:, 0])
    f_im = np.arange(1, 512)[None, :].astype(np.float64)
    C[:, 513:] = -np.sin(2 * np.pi * e * f_im / D)
    d = np.arange(D)[None, :].astype(np.float64)
    IR = np.zeros((D, D), np.float64)
    IR[0, :] = 1.0 / D
    IR[512, :] = np.cos(np.pi * d[0]) / D
    fr = np.arange(1, 512)[:, None].astype(np.float64)
    IR[1:512, :] = 2 * np.cos(2 * np.pi * d * fr / D) / D
    IR[513:, :] = -2 * np.sin(2 * np.pi * d * fr / D) / D
    return C.astype(np.float32), IR.astype(np.float32)


_CONSTS = None


def get_consts():
    global _CONSTS
    if _CONSTS is None:
        C, IR = _dft_consts()
        _CONSTS = (C.astype(NBF), IR.astype(NBF))
    return _CONSTS


def fm(a):
    Dd, Tt = a.shape
    n = Dd // P
    return np.ascontiguousarray(a.reshape(n, P, Tt).transpose(1, 0, 2).reshape(P, n * Tt))


def fm_inv(a, Tt):
    n = a.shape[1] // Tt
    return np.ascontiguousarray(a.reshape(P, n, Tt).transpose(1, 0, 2).reshape(n * P, Tt))


def vcol(a):
    return np.ascontiguousarray(np.asarray(a).reshape(NCH, P).T)


def tmaj(a):
    Tt, Dd = a.shape
    n = Tt // P
    return np.ascontiguousarray(a.reshape(n, P, Dd).transpose(1, 0, 2).reshape(P, n * Dd))


class Cnt:
    __slots__ = ("h", "n")

    def __init__(self, h):
        self.h = h
        self.n = 0


class Bld:
    def __init__(self, nc):
        self.nc = nc
        self.q = {e: [] for e in ("tensor", "vector", "scalar", "sync", "gpsimd")}
        self._si = 0
        self.ch = {e: self.sem("ch_" + e) for e in ("vector", "scalar", "gpsimd")}

    pfx = ""

    def sem(self, name=None):
        self._si += 1
        return Cnt(self.nc.alloc_semaphore(self.pfx + (name or ("s%d" % self._si))))

    def sb(self, name, shape, dtype, off):
        return self.nc.alloc_sbuf_tensor_at(self.pfx + name, shape, dtype,
                                            offset=off + 17408)

    def op(self, eng, fn, waits=(), incs=()):
        waits = tuple(waits) + tuple((c, c.n) for c, k in incs if k == 16)
        for c, k in incs:
            c.n += k
        W = tuple((c.h, v) for c, v in waits)
        I = tuple((c.h, k) for c, k in incs)

        def run(e):
            for h, v in W:
                e.wait_ge(h, v)
            ins = fn(e)
            if ins is not None:
                for h, k in I:
                    ins = ins.then_inc(h, k)
        self.q[eng].append(run)

    def chain(self, eng, fn, waits=()):
        c = self.ch[eng]
        self.op(eng, fn, waits=tuple(waits) + ((c, c.n),), incs=((c, 1),))
        return c.n

    def dma(self, eng, fn, sem, waits=()):
        self.op(eng, fn, waits=tuple(waits) + ((sem, sem.n),), incs=((sem, 16),))
        return sem.n

    def vw(self):
        return (self.ch["vector"], self.ch["vector"].n)

    def finalize(self):
        nc = self.nc
        q = self.q
        with nc.Block() as block:
            @block.tensor
            def _(e):
                for f in q["tensor"]:
                    f(e)

            @block.vector
            def _(e):
                for f in q["vector"]:
                    f(e)

            @block.scalar
            def _(e):
                for f in q["scalar"]:
                    f(e)

            @block.sync
            def _(e):
                for f in q["sync"]:
                    f(e)

            @block.gpsimd
            def _(e):
                for f in q["gpsimd"]:
                    f(e)
        return nc


def r3(t, inner):
    return t[:, :].rearrange("p (c t) -> p c t", t=inner)


def slc3(t, total, tc):
    return r3(t, total)[:, :, tc * TC:(tc + 1) * TC]


def bc3(ps):
    return ps[:, :].rearrange("p (u t) -> p u t", u=1).broadcast_to([P, NCH, TC])


class Fx:

    def __init__(self, fused=False):
        self.nc = bass.Bass("TRN2", target_bir_lowering=False)
        self.b = Bld(self.nc)
        self.fused = fused
        self._dram = {}
        nc = self.nc
        self.PA = nc.alloc_psum_tensor("PA", [P, 2 * TC], F32)[:, :]
        self.PB = nc.alloc_psum_tensor("PB", [P, 2 * TC], F32)[:, :]
        self.PC = nc.alloc_psum_tensor("PC", [P, 2 * TC], F32)[:, :]
        self.PD = nc.alloc_psum_tensor("PD", [P, 2 * TC], F32)[:, :]
        self.ext_spref = None
        self.ag = None
        self.force_out = set()

    def dram(self, name, shape, dtype, role):
        if name in self._dram:
            return self._dram[name]
        if name in self.force_out:
            kind = "ExternalOutput"
        elif role == "win" or (role == "in" and not self.fused):
            kind = "ExternalInput"
        elif role == "result" or (role == "out" and not self.fused):
            kind = "ExternalOutput"
        else:
            kind = "Internal"
        ap = self.nc.dram_tensor(name, shape, dtype, kind=kind).ap()
        self._dram[name] = ap
        return ap


_BARRIER_N = [0]


def emit_barrier(fx, dma_sems=()):
    b = fx.b
    _BARRIER_N[0] += 1
    ph = b.sem("ph%d" % _BARRIER_N[0])
    engs = ("tensor", "vector", "scalar", "sync", "gpsimd")
    for eng in engs:
        w = tuple((s, s.n) for s in dma_sems) if eng == "sync" else ()
        b.op(eng, lambda e: e.nop(), waits=w, incs=((ph, 1),))
    for eng in engs:
        b.op(eng, lambda e, v=len(engs): e.wait_ge(ph.h, v))


def std_env(b, nc, o, fx, pre=None):
    env = {}
    env["sxb"] = b.sb("sxb", [P, NCH * TC], BF16, o); o += NCH * TC * 2
    env["sxx"] = b.sb("sxx", [P, NCH * TC], BF16, o); o += NCH * TC * 2
    env["m_row"] = b.sb("m_row", [1, TC], F32, o); o += TC * 4
    env["r_row"] = b.sb("r_row", [1, TC], F32, o); o += TC * 4
    env["ones_bf"] = b.sb("ones_bf", [P, 1], BF16, o); o += 32
    env["ones_row"] = b.sb("ones_row", [1, P], F32, o); o += P * 4
    env["eps_c"] = b.sb("eps_c", [1, 1], F32, o); o += 32
    env["end"] = o
    env["psum_sum"] = fx.PA[:, 0:TC]
    env["psum_sq"] = fx.PA[:, TC:2 * TC]
    env["pbc_m"] = fx.PB[:, 0:TC]
    env["pbc_r"] = fx.PB[:, TC:2 * TC]
    env["pk0"] = fx.PC[:, 0:TC]
    env["pk1"] = fx.PC[:, TC:2 * TC]
    env["tq"] = b.sem("tq")
    env["lds"] = (pre["lds"] if pre else b.sem("lds"))
    env["ldg"] = b.sem("ldg")
    env["stq"] = b.sem("stq")
    b.chain("vector", lambda e: e.memset(env["ones_bf"][:, :], 1.0))
    b.chain("vector", lambda e: e.memset(env["ones_row"][:, :], 1.0))
    b.chain("vector", lambda e: e.memset(env["eps_c"][:, :], 1e-5))
    return env


def emit_ln_tc(b, env, src3, scale_ap=None):
    tq = env["tq"]
    sxb, sxx = env["sxb"], env["sxx"]
    vb = b.chain("vector", lambda e: e.tensor_copy(r3(sxb, TC), src3))
    vx = b.chain("vector", lambda e: e.tensor_mul(r3(sxx, TC), r3(sxb, TC), r3(sxb, TC)))
    for ch in range(NCH):
        b.op("tensor",
             (lambda e, ch=ch: e.matmul(env["psum_sum"][0:1, :], env["ones_bf"][:, :],
                                        sxb[:, ch * TC:(ch + 1) * TC],
                                        start=(ch == 0), stop=(ch == NCH - 1))),
             waits=((b.ch["vector"], vb),) if ch == 0 else ())
    for ch in range(NCH):
        b.op("tensor",
             (lambda e, ch=ch: e.matmul(env["psum_sq"][0:1, :], env["ones_bf"][:, :],
                                        sxx[:, ch * TC:(ch + 1) * TC],
                                        start=(ch == 0), stop=(ch == NCH - 1))),
             waits=((b.ch["vector"], vx),) if ch == 0 else (),
             incs=((tq, 1),) if ch == NCH - 1 else ())
    td = tq.n
    m, r = env["m_row"], env["r_row"]
    b.chain("vector", lambda e: e.tensor_scalar(m[0:1, :], env["psum_sum"][0:1, :],
                                                1.0 / 1024.0, None, ALU.mult),
            waits=((tq, td),))
    b.chain("vector", lambda e: e.tensor_mul(r[0:1, :], m[0:1, :], m[0:1, :]))
    vv = b.chain("vector", lambda e: e.scalar_tensor_tensor(
        r[0:1, :], env["psum_sq"][0:1, :], 1.0 / 1024.0, r[0:1, :],
        ALU.mult, ALU.subtract))
    sv = b.chain("scalar", lambda e: e.activation(r[0:1, :], r[0:1, :], AF.Sqrt,
                                                  bias=env["eps_c"][0:1, :]),
                 waits=((b.ch["vector"], vv),))
    vr = b.chain("vector", lambda e: e.reciprocal(r[0:1, :], r[0:1, :]),
                 waits=((b.ch["scalar"], sv),))
    if scale_ap is not None:
        vr = b.chain("vector", lambda e: e.tensor_scalar(
            r[0:1, :], r[0:1, :], scale_ap, None, ALU.mult))
    b.op("tensor", lambda e: e.matmul(env["pbc_m"][:, :], env["ones_row"][0:1, :],
                                      m[0:1, :], start=True, stop=True),
         waits=((b.ch["vector"], vr),))
    b.op("tensor", lambda e: e.matmul(env["pbc_r"][:, :], env["ones_row"][0:1, :],
                                      r[0:1, :], start=True, stop=True),
         incs=((tq, 1),))
    return tq.n


def emit_spectrum_norm(b, env, seva, sp0, sp1, stq2, stq3):
    H = 4 * TC
    re = seva[:, 0:H]
    im = seva[:, H:2 * H]
    for (blk, sp) in ((0, sp0), (1, sp1)):
        srcrow = seva[0:1, blk * H: blk * H + TC]
        vc = b.chain("vector", lambda e, srcrow=srcrow, sp=sp: e.tensor_copy(sp[0:1, 0:TC], srcrow))
        sa = b.chain("scalar", lambda e, srcrow=srcrow, sp=sp: e.activation(
            sp[0:1, TC:2 * TC], srcrow, AF.Abs), waits=((b.ch["vector"], vc),))
        b.chain("vector", lambda e, sp=sp: e.tensor_scalar(
            sp[0:1, TC:2 * TC], sp[0:1, TC:2 * TC], 1e-8, None, ALU.add),
            waits=((b.ch["scalar"], sa),))
        b.chain("vector", lambda e, sp=sp: e.reciprocal(sp[0:1, TC:2 * TC], sp[0:1, TC:2 * TC]))
        b.chain("vector", lambda e, sp=sp: e.tensor_mul(
            sp[0:1, 0:TC], sp[0:1, 0:TC], sp[0:1, TC:2 * TC]))
    s1 = b.chain("scalar", lambda e: e.activation(stq2[:, :], re, AF.Square),
                 waits=(b.vw(),))
    s2 = b.chain("scalar", lambda e: e.activation(stq3[:, :], im, AF.Square))
    b.chain("vector", lambda e: e.tensor_add(stq2[:, :], stq2[:, :], stq3[:, :]),
            waits=((b.ch["scalar"], s2),))
    s3 = b.chain("scalar", lambda e: e.activation(stq2[:, :], stq2[:, :], AF.Sqrt),
                 waits=(b.vw(),))
    b.chain("vector", lambda e: e.tensor_scalar(stq2[:, :], stq2[:, :], 1e-8, None, ALU.add),
            waits=((b.ch["scalar"], s3),))
    b.chain("vector", lambda e: e.reciprocal(stq2[:, :], stq2[:, :]))
    b.chain("vector", lambda e: e.tensor_mul(re, re, stq2[:, :]))
    b.chain("vector", lambda e: e.tensor_mul(im, im, stq2[:, :]))
    b.chain("vector", lambda e: e.tensor_copy(seva[0:1, 0:TC], sp0[0:1, 0:TC]))
    return b.chain("vector", lambda e: e.tensor_copy(seva[0:1, H:H + TC], sp1[0:1, 0:TC]))


def emit_cmul(b, are, aim, bre, bim, dre, dim, t1, conj_b, spec):
    b.chain("vector", lambda e: e.tensor_mul(t1, are, bre))
    b.chain("vector", lambda e: e.tensor_mul(dim, aim, bim))
    b.chain("vector", lambda e: e.tensor_tensor(dre, t1, dim,
                                                ALU.add if conj_b else ALU.subtract))
    b.chain("vector", lambda e: e.tensor_mul(t1, aim, bre))
    b.chain("vector", lambda e: e.tensor_mul(dim, are, bim))
    b.chain("vector", lambda e: e.tensor_tensor(dim, t1, dim,
                                                ALU.subtract if conj_b else ALU.add))
    a0, a5, b0, b5, d0, d5 = spec
    b.chain("vector", lambda e: e.tensor_mul(d0, a0, b0))
    return b.chain("vector", lambda e: e.tensor_mul(d5, a5, b5))


def emit_fuse(b, env, w_in, dst, war_tensor_cnt, swbuf, scb):
    tq, ldg = env["tq"], env["ldg"]
    b.op("gpsimd", lambda e: e.dma_start(
        out=r3(swbuf, D), in_=w_in.rearrange("(c p) t -> p c t", p=P)),
        waits=((tq, war_tensor_cnt), (b.ch["gpsimd"], b.ch["gpsimd"].n)),
        incs=((ldg, 16),))
    ld_target = ldg.n
    evac = {}
    for m in range(NCH):
        for half in range(2):
            key = (m * 2 + half) % 2
            ps = env["pk0"] if key == 0 else env["pk1"]
            for ech in range(NCH):
                w = []
                if m == 0 and half == 0 and ech == 0:
                    w += [(ldg, ld_target), (env["lds"], env["lds_need"]), b.vw()]
                if ech == 0 and key in evac:
                    w.append((b.ch["vector"], evac[key]))
                b.op("tensor",
                     (lambda e, m=m, half=half, ech=ech, ps=ps:
                      e.matmul(ps[:, :],
                               swbuf[:, ech * D + m * P: ech * D + (m + 1) * P],
                               scb[:, ech * D + half * TC: ech * D + half * TC + TC],
                               start=(ech == 0), stop=(ech == NCH - 1))),
                     waits=tuple(w),
                     incs=((tq, 1),) if ech == NCH - 1 else ())
            td = tq.n
            evac[key] = b.chain(
                "vector",
                (lambda e, m=m, half=half, ps=ps:
                 e.tensor_copy(dst[:, m * D + half * TC: m * D + half * TC + TC], ps[:, :])),
                waits=((tq, td),))
    return tq.n, b.ch["vector"].n


def emit_a1(fx):
    nc, b = fx.nc, fx.b
    b.pfx = "a1_"
    xin = fx.dram("xT", [D, T], F32, "win")
    wk_in = fx.dram("Wk", [D, D], F32, "win")
    wv_in = fx.dram("Wv", [D, D], F32, "win")
    c_in = fx.dram("Cb", [D, D], BF16, "win")

    o_fkv = fx.dram("o_fkv", [P, NCH * T], BF16, "out")
    o_ln1 = fx.dram("o_ln1", [P, NCH * T], BF16, "out")
    o_xmid = fx.dram("o_xmid", [P, NCH * T], F32, "out")
    o_tot = fx.dram("o_tot", [P, NCH], F32, "out")

    sfkv = b.sb("sfkv", [P, NCH * T], BF16, 0)
    sfusedA = b.sb("sfusedA", [P, NCH * D], BF16, 32 * KB)
    sfusedB = b.sb("sfusedB", [P, NCH * D], BF16, 48 * KB)
    scb = b.sb("scb", [P, NCH * D], BF16, 64 * KB)
    swbuf = b.sb("swbuf", [P, NCH * D], BF16, 80 * KB)
    sxt = [b.sb("sxt0", [P, NCH * TC], F32, 96 * KB),
           b.sb("sxt1", [P, NCH * TC], F32, 112 * KB)]
    slnt = b.sb("slnt", [P, NCH * TC], BF16, 128 * KB)
    seva = b.sb("seva", [P, NCH * TC], BF16, 136 * KB)
    sevb = b.sb("sevb", [P, NCH * TC], BF16, 144 * KB)
    stmp1 = b.sb("stmp1", [P, 4 * TC], F32, 152 * KB)
    stot = b.sb("stot", [P, NCH], F32, 160 * KB)
    sp0 = b.sb("sp0", [1, 2 * TC], F32, 161 * KB)
    sp1 = b.sb("sp1", [1, 2 * TC], F32, 165 * KB)
    stq2 = b.sb("stq2", [P, 4 * TC], F32, 169 * KB)
    stq3 = b.sb("stq3", [P, 4 * TC], F32, 177 * KB)
    env = std_env(b, nc, 185 * KB, fx)

    tq, lds, ldg, stq = env["tq"], env["lds"], env["ldg"], env["stq"]
    b.op("sync", lambda e: e.dma_start(out=r3(scb, D),
                                       in_=c_in.rearrange("(c p) t -> p c t", p=P)),
         incs=((lds, 16),))
    env["lds_need"] = 16
    tfK, vfK = emit_fuse(b, env, wk_in, sfusedA, 0, swbuf, scb)
    tfV, vfV = emit_fuse(b, env, wv_in, sfusedB, tfK, swbuf, scb)

    evac = {0: vfV, 1: vfV}
    xload = [0] * NTC
    tile_free_v = [0, 0]
    tile_free_s = [0, 0]
    for tc in range(min(2, NTC)):
        b.op("sync", lambda e, tc=tc: e.dma_start(
            out=r3(sxt[tc % 2], TC),
            in_=xin.rearrange("(c p) t -> p c t", p=P)[:, :, tc * TC:(tc + 1) * TC]),
            incs=((lds, 16),))
        xload[tc] = lds.n
    st_ln = 0
    for tc in range(NTC):
        c0 = tc * TC
        cur = sxt[tc % 2]
        b.op("vector", lambda e, v=xload[tc]: e.wait_ge(lds.h, v))
        b.op("vector", lambda e, v=st_ln: e.wait_ge(stq.h, v))
        tbc = emit_ln_tc(b, env, r3(cur, TC))
        b.chain("vector", lambda e, cur=cur: e.tensor_tensor(
            r3(slnt, TC), r3(cur, TC), bc3(env["pbc_m"]), ALU.subtract),
            waits=((tq, tbc),))
        vln = b.chain("vector", lambda e: e.tensor_tensor(
            r3(slnt, TC), r3(slnt, TC), bc3(env["pbc_r"]), ALU.mult))
        vres = b.chain("vector", lambda e, cur=cur: e.tensor_tensor(
            r3(cur, TC), r3(cur, TC), r3(slnt, TC), ALU.add))
        tile_free_v[tc % 2] = vres
        b.op("gpsimd", lambda e, tc=tc, cur=cur: e.dma_start(
            out=r3(o_xmid, T)[:, :, tc * TC:(tc + 1) * TC], in_=r3(cur, TC)),
            waits=((b.ch["vector"], vres), (b.ch["gpsimd"], b.ch["gpsimd"].n)),
            incs=((stq, 16),))
        tile_free_s[tc % 2] = stq.n
        b.op("gpsimd", lambda e, tc=tc: e.dma_start(
            out=r3(o_ln1, T)[:, :, tc * TC:(tc + 1) * TC], in_=r3(slnt, TC)),
            incs=((stq, 16),))
        st_ln = stq.n
        if tc + 2 < NTC:
            b.op("sync", lambda e, tc=tc: e.dma_start(
                out=r3(sxt[tc % 2], TC),
                in_=xin.rearrange("(c p) t -> p c t", p=P)[:, :, (tc + 2) * TC:(tc + 3) * TC]),
                waits=((b.ch["vector"], tile_free_v[tc % 2]), (stq, tile_free_s[tc % 2])),
                incs=((lds, 16),))
            xload[tc + 2] = lds.n
        for (dst, fmx) in ((seva, sfusedA), (sevb, sfusedB)):
            for fch in range(NCH):
                key = fch % 2
                ps = env["pk0"] if key == 0 else env["pk1"]
                for dch in range(NCH):
                    w = []
                    if dch == 0:
                        w.append((b.ch["vector"], max(vln, evac[key])))
                    b.op("tensor",
                         (lambda e, fch=fch, dch=dch, ps=ps, fmx=fmx:
                          e.matmul(ps[:, :],
                                   fmx[:, dch * D + fch * P: dch * D + (fch + 1) * P],
                                   slnt[:, dch * TC: (dch + 1) * TC],
                                   start=(dch == 0), stop=(dch == NCH - 1))),
                         waits=tuple(w),
                         incs=((tq, 1),) if dch == NCH - 1 else ())
                td = tq.n
                evac[key] = b.chain(
                    "vector",
                    (lambda e, dst=dst, fch=fch, ps=ps:
                     e.tensor_copy(dst[:, fch * TC:(fch + 1) * TC], ps[:, :])),
                    waits=((tq, td),))
        b.op("vector", lambda e, v=tq.n: e.wait_ge(tq.h, v))
        emit_spectrum_norm(b, env, seva, sp0, sp1, stq2, stq3)
        H = 4 * TC
        dre = r3(sfkv, T)[:, 0:4, c0:c0 + TC]
        dim = r3(sfkv, T)[:, 4:8, c0:c0 + TC]
        spec = (seva[0:1, 0:TC], seva[0:1, H:H + TC],
                sevb[0:1, 0:TC], sevb[0:1, H:H + TC],
                sfkv[0:1, c0:c0 + TC],
                sfkv[0:1, 4 * T + c0: 4 * T + c0 + TC])
        emit_cmul(b, r3(seva[:, 0:H], TC), r3(seva[:, H:2 * H], TC),
                  r3(sevb[:, 0:H], TC), r3(sevb[:, H:2 * H], TC),
                  dre, dim, r3(stmp1, TC), False, spec)
    for ch in range(NCH):
        b.chain("vector", lambda e, ch=ch: e.reduce_sum(
            stot[:, ch:ch + 1], sfkv[:, ch * T:(ch + 1) * T], X))
    vend = b.ch["vector"].n
    b.op("gpsimd", lambda e: e.dma_start(out=o_fkv, in_=sfkv[:, :]),
         waits=((b.ch["vector"], vend), (b.ch["gpsimd"], b.ch["gpsimd"].n)),
         incs=((stq, 16),))
    b.dma("gpsimd", lambda e: e.dma_start(out=o_tot, in_=stot[:, :]), stq)
    b.op("gpsimd", lambda e, v=stq.n: e.wait_ge(stq.h, v))
    return dict(stot=stot, env=env, stq=stq, lds=lds, ldg=ldg)


def build_a1():
    fx = Fx()
    emit_a1(fx)
    return fx.b.finalize()


def emit_a2(fx):
    nc, b = fx.nc, fx.b
    b.pfx = "a2_"
    fkv_in = fx.dram("o_fkv", [P, NCH * T], BF16, "in")
    ln1_in = fx.dram("o_ln1", [P, NCH * T], BF16, "in")
    xmid_in = fx.dram("o_xmid", [P, NCH * T], F32, "in")
    if not fx.fused:
        pref_in = fx.dram("prefix", [P, NCH], F32, "in")
    wq_in = fx.dram("Wq", [D, D], F32, "win")
    c_in = fx.dram("Cb", [D, D], BF16, "win")
    ir_in = fx.dram("IRb", [D, D], BF16, "win")
    tern_in = fx.dram("ternT", [D, D], I32, "win")
    l0b_in = fx.dram("l0bias", [P, NCH], F32, "win")
    asc_in = fx.dram("ascale", [1, 1], F32, "win")
    wvt_in = fx.dram("wvT", [D, D], F32, "win")
    wvb_in = fx.dram("wvb", [1, D], F32, "win")

    o_x2 = fx.dram("o_x2", [P, NCH * T], F32, "out")
    if fx.fused:
        o_ln3 = fx.ag_in[0:P, :]
        o_vx = None
    else:
        o_ln3 = fx.dram("o_ln3", [P, NCH * T], BF16, "out")
        o_vx = fx.dram("o_vx", [P, (T // P) * D], BF16, "out")

    pre = getattr(fx, "pre", None)
    sx = pre["sx"] if pre else b.sb("sx", [P, NCH * T], F32, 0)
    scum = b.sb("scum", [P, NCH * T], BF16, 64 * KB)
    sxdq = b.sb("sxdq", [P, NCH * T], BF16, 64 * KB)
    sfkv = pre["sfkv"] if pre else b.sb("sfkv", [P, NCH * T], BF16, 96 * KB)
    stq2 = b.sb("stq2", [P, 4 * TC], F32, 96 * KB)
    stmp1 = b.sb("stmp1", [P, 4 * TC], F32, 104 * KB)
    stq3 = b.sb("stq3", [P, 4 * TC], F32, 112 * KB)
    sp0 = b.sb("sp0", [1, 2 * TC], F32, 120 * KB)
    sp1 = b.sb("sp1", [1, 2 * TC], F32, 122 * KB)
    sqt = b.sb("sqt", [P, NCH * TC], F32, 96 * KB)
    sint = b.sb("sint", [P, NCH * TC], I32, 112 * KB)
    sfusedQ = b.sb("sfusedQ", [P, NCH * D], BF16, 128 * KB)
    swt = b.sb("swt", [P, NCH * D], BF16, 128 * KB)
    sln3 = b.sb("sln3", [P, NCH * T], BF16, 128 * KB)
    sir = b.sb("sir", [P, NCH * D], BF16, 144 * KB)
    scb = b.sb("scb", [P, NCH * D], BF16, 160 * KB)
    slnt = b.sb("slnt", [P, NCH * TC], BF16, 160 * KB)
    sfev = b.sb("sfev", [P, TC], F32, 160 * KB)
    swvt = b.sb("swvt", [P, NCH * D], BF16, 160 * KB)
    seva = b.sb("seva", [P, NCH * TC], BF16, 168 * KB)
    swbuf = b.sb("swbuf", [P, NCH * D], BF16, 176 * KB)
    sh = b.sb("sh", [P, NCH * TC], BF16, 176 * KB)
    svxt = b.sb("svxt", [P, VB], BF16, 197 * KB)
    swvbb = b.sb("swvbb", [1, D], BF16, 199 * KB + 64)
    onesr_bf = b.sb("onesr_bf", [1, P], BF16, 201 * KB + 128)
    spref = b.sb("spref", [P, NCH], F32, 201 * KB + 512)
    sl0b = b.sb("sl0b", [P, NCH], F32, 202 * KB)
    sasc = b.sb("sasc", [1, 4], F32, 202 * KB + 512)
    sscol = b.sb("sscol", [P, 1], F32, 203 * KB)
    env = std_env(b, nc, 176 * KB, fx, pre=pre)
    tq, lds, ldg, stq = env["tq"], env["lds"], env["ldg"], env["stq"]
    env["lds_need"] = 0

    if not pre:
        b.dma("sync", lambda e: e.dma_start(out=sx[:, :], in_=xmid_in), lds)
        b.dma("sync", lambda e: e.dma_start(out=sfkv[:, :], in_=fkv_in), lds)
    b.op("sync", lambda e: e.dma_start(out=r3(scb, D),
                                       in_=c_in.rearrange("(c p) t -> p c t", p=P)),
         incs=((lds, 16),))
    b.op("sync", lambda e: e.dma_start(out=r3(sir, D),
                                       in_=ir_in.rearrange("(c p) t -> p c t", p=P)),
         incs=((lds, 16),))
    if fx.fused:
        spref = fx.ext_spref
    else:
        b.dma("sync", lambda e: e.dma_start(out=spref[:, :], in_=pref_in), lds)
    b.dma("sync", lambda e: e.dma_start(out=sl0b[:, :], in_=l0b_in), lds)
    b.dma("sync", lambda e: e.dma_start(out=sasc[0:1, 0:1], in_=asc_in), lds)
    if not fx.fused:
        b.op("gpsimd", lambda e: e.dma_start(out=swvbb[:, :], in_=wvb_in),
             incs=((ldg, 16),))
    ldwvb = ldg.n
    ld_base = lds.n

    env["lds_need"] = 48
    tfQ, vfQ = emit_fuse(b, env, wq_in, sfusedQ, 0, swbuf, scb)

    b.op("vector", lambda e: e.wait_ge(lds.h, ld_base))
    scan_done = 0
    for ch in range(NCH):
        scan_done = b.chain("vector", lambda e, ch=ch: e.tensor_tensor_scan(
            scum[:, ch * T:(ch + 1) * T],
            env["ones_bf"][:, :].broadcast_to([P, T]),
            sfkv[:, ch * T:(ch + 1) * T],
            spref[:, ch:ch + 1], ALU.mult, ALU.add))

    b.chain("vector", lambda e: e.tensor_scalar(sasc[0:1, 0:1], sasc[0:1, 0:1],
                                                1e-5, None, ALU.max))
    va = b.chain("vector", lambda e: e.reciprocal(sasc[0:1, 1:2], sasc[0:1, 0:1]))
    b.op("tensor", lambda e: e.matmul(env["pk0"][:, 0:1], env["ones_row"][0:1, :],
                                      sasc[0:1, 0:1], start=True, stop=True),
         waits=((b.ch["vector"], va),), incs=((tq, 1),))
    tscol = tq.n
    b.chain("vector", lambda e: e.tensor_copy(sscol[:, :], env["pk0"][:, 0:1]),
            waits=((tq, tscol),))
    vscol = b.ch["vector"].n
    b.op("vector", lambda e, v=ldwvb: e.wait_ge(ldg.h, v))
    b.chain("vector", lambda e: e.memset(onesr_bf[:, :], 1.0))

    evac = {0: b.ch["vector"].n, 1: b.ch["vector"].n}
    slnt_ld = [0] * NTC
    b.op("sync", lambda e: e.dma_start(
        out=r3(slnt, TC), in_=r3(ln1_in, T)[:, :, 0:TC]),
        waits=((tq, tfQ),), incs=((lds, 16),))
    slnt_ld[0] = lds.n
    qproj_done = 0
    for tc in range(NTC):
        c0 = tc * TC
        b.op("vector", lambda e, v=slnt_ld[tc]: e.wait_ge(lds.h, v))
        vgate = b.chain("vector", lambda e: e.engine_nop())
        for fch in range(NCH):
            key = fch % 2
            ps = env["pk0"] if key == 0 else env["pk1"]
            for dch in range(NCH):
                w = []
                if dch == 0:
                    w.append((b.ch["vector"], max(vgate, evac[key], vfQ)))
                b.op("tensor",
                     (lambda e, fch=fch, dch=dch, ps=ps:
                      e.matmul(ps[:, :],
                               sfusedQ[:, dch * D + fch * P: dch * D + (fch + 1) * P],
                               slnt[:, dch * TC: (dch + 1) * TC],
                               start=(dch == 0), stop=(dch == NCH - 1))),
                     waits=tuple(w),
                     incs=((tq, 1),) if dch == NCH - 1 else ())
            td = tq.n
            evac[key] = b.chain(
                "vector",
                (lambda e, fch=fch, ps=ps:
                 e.tensor_copy(seva[:, fch * TC:(fch + 1) * TC], ps[:, :])),
                waits=((tq, td),))
        qproj_done = tq.n
        if tc + 1 < NTC:
            b.op("sync", lambda e, tc=tc: e.dma_start(
                out=r3(slnt, TC), in_=r3(ln1_in, T)[:, :, (tc + 1) * TC:(tc + 2) * TC]),
                waits=((tq, qproj_done),), incs=((lds, 16),))
            slnt_ld[tc + 1] = lds.n
        emit_spectrum_norm(b, env, seva, sp0, sp1, stq2, stq3)
        H = 4 * TC
        dre = r3(sh, TC)[:, 0:4, :]
        dim = r3(sh, TC)[:, 4:8, :]
        spec = (scum[0:1, c0:c0 + TC], scum[0:1, 4 * T + c0: 4 * T + c0 + TC],
                seva[0:1, 0:TC], seva[0:1, H:H + TC],
                sh[0:1, 0:TC], sh[0:1, 4 * TC: 5 * TC])
        cum_re = r3(scum, T)[:, 0:4, c0:c0 + TC]
        cum_im = r3(scum, T)[:, 4:8, c0:c0 + TC]
        emit_cmul(b, cum_re, cum_im,
                  r3(seva[:, 0:H], TC), r3(seva[:, H:2 * H], TC),
                  dre, dim, r3(stmp1, TC), True, spec)
        vh_done = b.ch["vector"].n
        iev = {0: 0, 1: 0}
        for dch in range(NCH):
            key = dch % 2
            ps = env["pk0"] if key == 0 else env["pk1"]
            for fc in range(NCH):
                w = []
                if fc == 0:
                    w.append((b.ch["vector"], max(vh_done, iev[key])))
                b.op("tensor",
                     (lambda e, dch=dch, fc=fc, ps=ps:
                      e.matmul(ps[:, :],
                               sir[:, fc * D + dch * P: fc * D + (dch + 1) * P],
                               sh[:, fc * TC:(fc + 1) * TC],
                               start=(fc == 0), stop=(fc == NCH - 1))),
                     waits=tuple(w),
                     incs=((tq, 1),) if fc == NCH - 1 else ())
            td = tq.n
            iev[key] = b.chain(
                "vector",
                (lambda e, dch=dch, ps=ps, c0=c0: e.tensor_tensor(
                    sx[:, dch * T + c0: dch * T + c0 + TC],
                    sx[:, dch * T + c0: dch * T + c0 + TC], ps[:, :], ALU.add)),
                waits=((tq, td),))
        evac[0] = max(evac[0], iev[0])
        evac[1] = max(evac[1], iev[1])

    for tc in range(NTC):
        c0 = tc * TC
        tbc = emit_ln_tc(b, env, slc3(sx, T, tc), scale_ap=sasc[0:1, 1:2])
        b.chain("vector", lambda e, tc=tc: e.tensor_tensor(
            r3(sqt, TC), slc3(sx, T, tc), bc3(env["pbc_m"]), ALU.subtract),
            waits=((tq, tbc),))
        b.chain("vector", lambda e: e.tensor_tensor(
            r3(sqt, TC), r3(sqt, TC), bc3(env["pbc_r"]), ALU.mult))
        b.chain("vector", lambda e: e.tensor_scalar(
            sqt[:, :], sqt[:, :], 127.0, -127.0, ALU.min, ALU.max))
        b.chain("vector", lambda e: e.tensor_scalar(
            sqt[:, :], sqt[:, :], MAGIC, MAGIC, ALU.add, ALU.subtract))
        b.chain("vector", lambda e: e.tensor_copy(sint[:, :], sqt[:, :]))
        b.chain("vector", lambda e, tc=tc: e.tensor_copy(
            slc3(sxdq, T, tc), r3(sint, TC)))

    b.op("gpsimd", lambda e: e.dma_start(
        out=r3(swt, D), in_=tern_in.rearrange("(c p) t -> p c t", p=P)),
        waits=((tq, qproj_done), (b.ch["gpsimd"], b.ch["gpsimd"].n)),
        incs=((ldg, 16),))
    vt = b.chain("vector", lambda e: e.tensor_scalar(
        swt[:, :], swt[:, :], 1.0, None, ALU.subtract),
        waits=((ldg, ldg.n),))
    ffn_ev = b.ch["vector"].n
    sc_ev = b.ch["vector"].n
    for och in range(NCH):
        for tcq in range(NTC):
            key = (och * NTC + tcq) % 2
            ps = env["pk0"] if key == 0 else env["pk1"]
            for ich in range(NCH):
                w = []
                if ich == 0:
                    w.append((b.ch["vector"], max(vt, ffn_ev, sc_ev)))
                b.op("tensor",
                     (lambda e, och=och, tcq=tcq, ich=ich, ps=ps:
                      e.matmul(ps[:, :],
                               swt[:, ich * D + och * P: ich * D + (och + 1) * P],
                               sxdq[:, ich * T + tcq * TC: ich * T + tcq * TC + TC],
                               start=(ich == 0), stop=(ich == NCH - 1))),
                     waits=tuple(w),
                     incs=((tq, 1),) if ich == NCH - 1 else ())
            td = tq.n
            sc_ev = b.chain("vector", lambda e, och=och, ps=ps: e.scalar_tensor_tensor(
                sfev[:, :], ps[:, :], sscol[:, 0:1],
                sl0b[:, och:och + 1].broadcast_to([P, TC]),
                ALU.mult, ALU.add), waits=((tq, td), (b.ch["vector"], vscol)))
            ffn_ev = b.chain("vector", lambda e, och=och, tcq=tcq: e.tensor_tensor(
                sx[:, och * T + tcq * TC: och * T + tcq * TC + TC],
                sx[:, och * T + tcq * TC: och * T + tcq * TC + TC],
                sfev[:, :], ALU.add))

    b.op("gpsimd", lambda e: e.dma_start(out=o_x2, in_=sx[:, :]),
         waits=((b.ch["vector"], b.ch["vector"].n), (b.ch["gpsimd"], b.ch["gpsimd"].n)),
         incs=((stq, 16),))

    if not fx.fused:
        b.op("gpsimd", lambda e: e.dma_start(
            out=r3(swvt, D), in_=wvt_in.rearrange("(c p) t -> p c t", p=P)),
            waits=((b.ch["scalar"], b.ch["scalar"].n), (b.ch["vector"], b.ch["vector"].n),
                   (tq, tq.n), (b.ch["gpsimd"], b.ch["gpsimd"].n)),
            incs=((ldg, 16),))
        ldwv = ldg.n

    for tc in range(NTC):
        tbc = emit_ln_tc(b, env, slc3(sx, T, tc))
        b.chain("vector", lambda e, tc=tc: e.tensor_tensor(
            r3(sqt, TC), slc3(sx, T, tc), bc3(env["pbc_m"]), ALU.subtract),
            waits=((tq, tbc),))
        b.chain("vector", lambda e, tc=tc: e.tensor_tensor(
            slc3(sln3, T, tc), r3(sqt, TC), bc3(env["pbc_r"]), ALU.mult))
    vln3 = b.ch["vector"].n
    b.op("gpsimd", lambda e: e.dma_start(out=o_ln3, in_=sln3[:, :]),
         waits=((b.ch["vector"], vln3), (b.ch["gpsimd"], b.ch["gpsimd"].n)),
         incs=((stq, 16),))

    vx_ev = b.ch["vector"].n
    for tk in range(0 if fx.fused else T // P):
        for half in range(2):
            key = half
            ps = env["pk0"] if key == 0 else env["pk1"]
            for dch in range(NCH):
                w = []
                if dch == 0:
                    w.append((b.ch["vector"], max(vln3, vx_ev)))
                    w.append((ldg, ldwv))
                b.op("tensor",
                     (lambda e, tk=tk, half=half, dch=dch, ps=ps:
                      e.matmul(ps[:, :],
                               sln3[:, dch * T + tk * P: dch * T + (tk + 1) * P],
                               swvt[:, dch * D + half * TC: dch * D + half * TC + TC],
                               start=(dch == 0), stop=False)),
                     waits=tuple(w))
            b.op("tensor",
                 (lambda e, half=half, ps=ps:
                  e.matmul(ps[:, :], onesr_bf[0:1, :],
                           swvbb[0:1, half * TC: half * TC + TC],
                           start=False, stop=True)),
                 incs=((tq, 1),))
            td = tq.n
            vx_ev = b.chain(
                "vector",
                (lambda e, half=half, ps=ps:
                 e.tensor_copy(
                     svxt[:, half * 8 * GW:(half + 1) * 8 * GW].rearrange(
                         "p (g f) -> p g f", f=GW)[:, :, 0:64],
                     ps[:, :].rearrange("p (g f) -> p g f", f=64))
                 if fx.fused else
                 e.tensor_copy(svxt[:, half * TC: half * TC + TC], ps[:, :])),
                waits=((tq, td),))
        b.op("gpsimd", (lambda e, tk=tk: e.dma_start(
            out=(o_vx[:, tk * VB:(tk + 1) * VB] if fx.fused
                else o_vx[:, tk * D:(tk + 1) * D]),
            in_=(svxt[:, 0:VB] if fx.fused else svxt[:, 0:D]))),
            waits=((b.ch["vector"], vx_ev), (b.ch["gpsimd"], b.ch["gpsimd"].n)),
            incs=((stq, 16),))
        b.op("vector", lambda e, v=stq.n: e.wait_ge(stq.h, v))

    b.op("gpsimd", lambda e, v=stq.n: e.wait_ge(stq.h, v))
    return dict(env=env, stq=stq, lds=lds, ldg=ldg)


def build_a2():
    fx = Fx()
    emit_a2(fx)
    return fx.b.finalize()


def emit_b(fx):
    nc, b = fx.nc, fx.b
    b.pfx = "b_"
    n_pairs = getattr(fx, "b_pairs", NCH)
    do_outproj = getattr(fx, "b_outproj", True)
    if not fx.fused:
        k_in = fx.dram("ln3full", [P, NCH * S], BF16, "in")
        v_in = fx.dram("vxfull", [P, 2 * AGW], BF16, "in")
    x2_in = fx.dram("o_x2", [P, NCH * T], F32, "in")
    watt_in = fx.dram("watt", [P, NCH * 64], F32, "win")
    ouw_in = fx.dram("outwT", [D, D], F32, "win")
    oub_in = fx.dram("outb", [P, NCH], F32, "win")

    o_x3 = fx.dram("o_x3", [P, NCH * T], F32, "out")

    sk = b.sb("sk", [P, 2 * NCH * T], BF16, 0)
    sv = b.sb("sv", [P, 2 * AGW], BF16, 66560)
    so = b.sb("so", [P, NCH * T], BF16, 133120)
    sxw = [b.sb("sxw0", [P, T], BF16, 165888),
           b.sb("sxw1", [P, T], BF16, 203456)]
    sp = [b.sb("sp_0", [P, 2 * TC], BF16, 169984),
          b.sb("sp_1", [P, 2 * TC], BF16, 172032)]
    swatt = b.sb("swatt", [P, NCH * 64], BF16, 174080)
    swattf = b.sb("swattf", [P, NCH * 64], F32, 175104)
    souw = b.sb("souw", [P, NCH * D], BF16, 177152)
    sx2t = [b.sb("sx2t0", [P, TC], F32, 193536),
            b.sb("sx2t1", [P, TC], F32, 195584),
            b.sb("sx2t2", [P, TC], F32, 197632)]
    ssr = b.sb("ssr", [1, 2 * TC], F32, 197632)
    sot = b.sb("sot", [P, TC], BF16, 201728)
    soutb = b.sb("soutb", [P, NCH], F32, 202752)
    sneg20 = b.sb("sneg20", [P, 1], F32, 202816)
    ones_row = b.sb("ones_rowB", [1, P], F32, 202944)
    sfin = [b.sb("sfin0", [P, TC], F32, 207552),
            b.sb("sfin1", [P, TC], F32, 174080)]

    p_att = [fx.PA, fx.PB]
    p_o = [(fx.PC[:, 0:TC], fx.PC[:, TC:2 * TC]),
           (fx.PD[:, 0:TC], fx.PD[:, TC:2 * TC])]

    tq = b.sem("tq")
    lds = b.sem("lds")
    ldg = b.sem("ldg")
    stq = b.sem("stq")
    ldig = b.sem("ldig")

    if fx.fused:
        agq, ag_out, idx_ln, a2_stq = fx.ag
        b.dma("sync", lambda e: e.dma_start(
            out=sk[:, 0:NCH * T], in_=fx.ag_in),
            sem=lds, waits=((a2_stq, a2_stq.n),))
        sk_own_ld = lds.n
        b.dma("sync", lambda e: e.dma_start(out=swattf[:, :], in_=watt_in), lds)
        b.dma("sync", lambda e: e.dma_start(out=soutb[:, :], in_=oub_in), lds)
        swvt2 = b.sb("swvtB", [P, NCH * D], BF16, 177152)
        swvbb2 = b.sb("swvbbB", [1, D], BF16, 193536)
        sonesr = b.sb("sonesrB", [1, P], BF16, 195584)
        b.op("gpsimd", lambda e: e.dma_start(
            out=r3(swvt2, D),
            in_=fx.dram("wvT", [D, D], F32, "win").rearrange("(c p) t -> p c t", p=P)),
            waits=((b.ch["gpsimd"], b.ch["gpsimd"].n),), incs=((ldg, 16),))
        ldwv2 = ldg.n
        b.op("gpsimd", lambda e: e.dma_start(
            out=swvbb2[:, :], in_=fx.dram("wvb", [1, D], F32, "win")),
            waits=((b.ch["gpsimd"], b.ch["gpsimd"].n),), incs=((ldg, 16),))
        wvb_ld = ldg.n
        b.op("gpsimd", lambda e: e.indirect_dma_start(
            sk[:, NCH * T:2 * NCH * T],
            None, ag_out, IndirectOffsetOnAxis(idx_ln[:, :], 0)),
            waits=((agq, agq.n), (ldig, ldig.n), (b.ch["gpsimd"], b.ch["gpsimd"].n)),
            incs=((ldig, 16),))
        gath_done = ldig.n
        b.chain("vector", lambda e: e.memset(
            sv[:, :].rearrange("p (g f) -> p g f", f=GW)[:, :, 64:65], 1.0))
        b.chain("vector", lambda e: e.memset(sonesr[:, :], 1.0))
    else:
        gath_done = 0
        b.dma("sync", lambda e: e.dma_start(out=sk[:, 0:NCH * S], in_=k_in), lds)
        sk_own_ld = lds.n
        b.dma("sync", lambda e: e.dma_start(out=sv[:, :], in_=v_in), lds)
        b.dma("sync", lambda e: e.dma_start(out=swattf[:, :], in_=watt_in), lds)
        b.dma("sync", lambda e: e.dma_start(out=soutb[:, :], in_=oub_in), lds)
    if not fx.fused:
        b.op("gpsimd", lambda e: e.dma_start(
            out=r3(souw, D), in_=ouw_in.rearrange("(c p) t -> p c t", p=P)),
            incs=((ldg, 16),))
    b.chain("vector", lambda e: e.memset(sneg20[:, :], -20.0))
    b.chain("vector", lambda e: e.memset(ones_row[:, :], 1.0))
    vw0 = b.chain("vector", lambda e: e.tensor_scalar(
        swatt[:, :], swattf[:, :], 0.125, None, ALU.mult),
        waits=((lds, lds.n),))

    vxev = {}
    pvx = [fx.PD[:, 0:TC], fx.PD[:, TC:2 * TC]]
    vx_evc = {0: 0, 1: 0}

    def emit_vx(half_src):
        src_base = half_src * NCH * T
        dst_base = half_src * AGW
        vck0 = b.ch["vector"].n
        for tk in range(T // P):
            for half in range(2):
                ps = pvx[half]
                for dch in range(NCH):
                    w = []
                    if dch == 0:
                        w.append((b.ch["vector"], max(vx_evc[half], vck0)))
                        if tk == 0 and half == 0:
                            w.append((ldg, max(ldwv2, wvb_ld)))
                            w.append((lds, sk_own_ld))
                            if half_src == 1:
                                w.append((ldig, gath_done))
                    b.op("tensor",
                         (lambda e, tk=tk, half=half, dch=dch, ps=ps: e.matmul(
                             ps[:, :],
                             sk[:, src_base + dch * T + tk * P:
                                src_base + dch * T + (tk + 1) * P],
                             swvt2[:, dch * D + half * TC: dch * D + half * TC + TC],
                             start=(dch == 0), stop=False)),
                         waits=tuple(w))
                b.op("tensor",
                     (lambda e, half=half, ps=ps: e.matmul(
                         ps[:, :], sonesr[0:1, :],
                         swvbb2[0:1, half * TC: half * TC + TC],
                         start=False, stop=True)),
                     incs=((tq, 1),))
                td = tq.n
                vx_evc[half] = b.chain(
                    "vector",
                    (lambda e, tk=tk, half=half, ps=ps: e.tensor_copy(
                        sv[:, dst_base + tk * VB + half * 8 * GW:
                           dst_base + tk * VB + (half + 1) * 8 * GW].rearrange(
                               "p (g f) -> p g f", f=GW)[:, :, 0:64],
                        ps[:, :].rearrange("p (g f) -> p g f", f=64))),
                    waits=((tq, td),))
            vxev[half_src * (T // P) + tk] = max(vx_evc[0], vx_evc[1])
        return tq.n

    if fx.fused:
        emit_vx(0)

    ev_xw = 0
    ev_fin_par = {0: b.ch["vector"].n, 1: b.ch["vector"].n}
    exp_cnt = {}
    av_done = {}
    pending = None

    def emit_deferred():
        nonlocal pending
        pj, pqc, ppar, pev_s = pending
        pA, pB = p_o[ppar]
        b.op("tensor", (lambda e, pA=pA: e.matmul(
            pA[64:128, :], ones_row[0:1, 0:64], ssr[0:1, 0:TC],
            start=True, stop=True, tile_position=(0, 64))),
            waits=((b.ch["vector"], pev_s),))
        b.op("tensor", (lambda e, pB=pB: e.matmul(
            pB[64:128, :], ones_row[0:1, 0:64], ssr[0:1, TC:2 * TC],
            start=True, stop=True, tile_position=(0, 64))),
            incs=((tq, 1),))
        t_rs = tq.n
        b.chain("vector", (lambda e, pA=pA: e.tensor_copy(
            sot[0:64, :], pA[0:64, :])), waits=((tq, t_rs),))
        b.chain("vector", (lambda e, pB=pB: e.tensor_copy(
            sot[64:128, :], pB[0:64, :])))
        b.chain("vector", (lambda e, pj=pj, pqc=pqc, pA=pA: e.tensor_tensor(
            so[0:64, pj * T + pqc * TC: pj * T + pqc * TC + TC],
            sot[0:64, :], pA[64:128, :], ALU.mult)))
        ev_fin_par[ppar] = b.chain("vector",
            (lambda e, pj=pj, pqc=pqc, pB=pB: e.tensor_tensor(
                so[64:128, pj * T + pqc * TC: pj * T + pqc * TC + TC],
                sot[64:128, :], pB[64:128, :], ALU.mult)))
        pending = None

    def emit_xw_qb(jj, qb):
        nonlocal ev_xw
        tgt = p_att[0][:, 0:TC] if jj == 0 else p_o[0][0]
        w = []
        if jj == 0:
            w.append((lds, lds.n))
            w.append((b.ch["vector"], max(vw0, ev_xw)))
        elif qb == 0:
            w.append((b.ch["vector"], ev_fin_par[0]))
        else:
            w.append((b.ch["vector"], ev_xw))
        b.op("tensor", (lambda e, jj=jj, qb=qb, tgt=tgt: e.matmul(
            tgt[0:64, :], swatt[0:64, jj * 64:(jj + 1) * 64],
            sk[0:64, jj * T + qb * TC: jj * T + qb * TC + TC],
            start=True, stop=True, tile_position=(0, 0))),
            waits=tuple(w))
        b.op("tensor", (lambda e, jj=jj, qb=qb, tgt=tgt: e.matmul(
            tgt[64:128, :], swatt[64:128, jj * 64:(jj + 1) * 64],
            sk[64:128, jj * T + qb * TC: jj * T + qb * TC + TC],
            start=True, stop=True, tile_position=(64, 64))),
            incs=((tq, 1),))
        td = tq.n
        ev_xw = b.chain("vector", (lambda e, jj=jj, qb=qb, tgt=tgt: e.tensor_copy(
            sxw[jj % 2][:, qb * TC:(qb + 1) * TC], tgt[:, :])),
            waits=((tq, td),))
        if jj > 0:
            ev_fin_par[0] = ev_xw

    for j in range(n_pairs):
        if j == 0:
            for qb in range(NTC):
                emit_xw_qb(0, qb)
        for qc in range(NTC):
            NKB = S // P
            par = (j * NTC + qc) % 2
            pA, pB = p_o[par]

            def kcol(j, kb):
                if kb < NKB // 2:
                    return j * T + kb * P
                return NCH * T + j * T + (kb - NKB // 2) * P

            def svcol(j, kb, h):
                base = kb * VB if kb < NKB // 2 else AGW + (kb - NKB // 2) * VB
                return base + (2 * j + h) * GW

            att_done = {}

            def emit_att(kb, j=j, qc=qc):
                buf = kb % 2
                w = []
                if kb < 2:
                    w.append((b.ch["vector"], ev_xw))
                if fx.fused and kb == NKB // 2:
                    w.append((ldig, gath_done))
                if (j, qc, kb - 2) in exp_cnt:
                    w.append((b.ch["scalar"], exp_cnt[(j, qc, kb - 2)]))
                elif kb < 2 and (qc > 0 or j > 0):
                    prev = exp_cnt.get((j, qc - 1, NKB - 2 + buf)) if qc > 0 \
                        else exp_cnt.get((j - 1, NTC - 1, NKB - 2 + buf))
                    if prev is not None:
                        w.append((b.ch["scalar"], prev))
                kc = kcol(j, kb)
                b.op("tensor", (lambda e, kc=kc, qc=qc, buf=buf, j=j: e.matmul(
                    p_att[buf][:, 0:TC],
                    sk[0:64, kc: kc + P],
                    sxw[j % 2][0:64, qc * TC:(qc + 1) * TC],
                    start=True, stop=True, tile_position=(0, 0))),
                    waits=tuple(w))
                b.op("tensor", (lambda e, kc=kc, qc=qc, buf=buf, j=j: e.matmul(
                    p_att[buf][:, TC:2 * TC],
                    sk[64:128, kc: kc + P],
                    sxw[j % 2][64:128, qc * TC:(qc + 1) * TC],
                    start=True, stop=True, tile_position=(64, 0))),
                    incs=((tq, 1),))
                att_done[kb] = tq.n

            emit_att(0)
            emit_att(1)
            for kb in range(NKB):
                buf = kb % 2
                sw = [(tq, att_done[kb])]
                if (j, qc, kb - 2) in av_done:
                    sw.append((tq, av_done[(j, qc, kb - 2)]))
                exp_cnt[(j, qc, kb)] = b.chain("scalar", (lambda e, buf=buf: e.activation(
                    sp[buf][:, :], p_att[buf][:, :], AF.Exp, bias=sneg20[:, 0:1])),
                    waits=tuple(sw))
                if kb + 2 < NKB:
                    emit_att(kb + 2)
                if kb == 9 and pending is not None:
                    emit_deferred()
                if (qc == NTC - 1 and j + 1 < n_pairs
                        and kb in (13, 16, 19, 22)):
                    emit_xw_qb(j + 1, (kb - 13) // 3)
                if fx.fused and j == 0 and qc == 0 and kb == NKB // 2:
                    vx_par_done = emit_vx(1)
                    ev_fin_par[1] = max(vx_evc[0], vx_evc[1])
                    b.op("gpsimd", (lambda e, v=vx_par_done: e.dma_start(
                        out=r3(souw, D),
                        in_=ouw_in.rearrange("(c p) t -> p c t", p=P))),
                        waits=((tq, vx_par_done), (b.ch["gpsimd"], b.ch["gpsimd"].n)),
                        incs=((ldg, 16),))
                aw = [(b.ch["scalar"], exp_cnt[(j, qc, kb)])]
                if kb == 0:
                    aw.append((b.ch["vector"], ev_fin_par[par]))
                if fx.fused and j == 0 and qc == 0:
                    aw.append((b.ch["vector"], vxev[kb]))
                cA = svcol(j, kb, 0)
                cB = svcol(j, kb, 1)
                b.op("tensor", (lambda e, cA=cA, kb=kb, buf=buf, pA=pA: e.matmul(
                    pA[0:GW, :], sv[:, cA:cA + GW],
                    sp[buf][:, 0:TC],
                    start=(kb == 0), stop=(kb == NKB - 1), tile_position=(0, 0))),
                    waits=tuple(aw))
                b.op("tensor", (lambda e, cB=cB, kb=kb, buf=buf, pB=pB: e.matmul(
                    pB[0:GW, :], sv[:, cB:cB + GW],
                    sp[buf][:, TC:2 * TC],
                    start=(kb == 0), stop=(kb == NKB - 1), tile_position=(0, 0))),
                    incs=((tq, 1),))
                av_done[(j, qc, kb)] = tq.n
            tdav = av_done[(j, qc, NKB - 1)]
            b.chain("vector", (lambda e, pA=pA: e.tensor_copy(
                ssr[0:1, 0:TC], pA[64:65, :])), waits=((tq, tdav),))
            b.chain("vector", (lambda e, pB=pB: e.tensor_copy(
                ssr[0:1, TC:2 * TC], pB[64:65, :])))
            ev_s = b.chain("vector", lambda e: e.reciprocal(ssr[0:1, :], ssr[0:1, :]))
            pending = (j, qc, par, ev_s)
    emit_deferred()
    avn = tq.n

    ld_all = lds.n
    ldg_all = ldg.n
    if not do_outproj:
        b.op("gpsimd", lambda e, v=stq.n: e.wait_ge(stq.h, v))
        return dict(stq=stq, lds=lds, ldg=ldg, ldig=ldig)
    v_o_all = b.ch["vector"].n
    ev_fin = b.ch["vector"].n
    x3sb = None
    if fx.fused and getattr(fx, "pre_c", None) and "o_x3" not in fx.force_out:
        x3sb = fx.pre_c["sx3"]
    x2ld = {}
    x2war = {}
    sc_hist = {}
    st_hist = {}
    tiles = [(qc, mch) for qc in range(NTC) for mch in range(NCH)]
    for pf in range(2):
        b.op("sync", (lambda e, pf=pf: e.dma_start(
            out=sx2t[pf][:, :],
            in_=r3(x2_in, T)[:, tiles[pf][1], tiles[pf][0] * TC:(tiles[pf][0] + 1) * TC])),
            waits=((lds, lds.n), (tq, avn)), incs=((lds, 16),))
        x2ld[pf] = lds.n
    for idx, (qc, mch) in enumerate(tiles):
        buf = idx % 3
        if idx + 2 < NCH * NTC:
            nqc, nmch = tiles[idx + 2]
            pw = ()
            if (idx - 1) in x2war:
                pw = ((b.ch["vector"], x2war[idx - 1]),)
            elif idx == 0:
                pw = ((tq, avn),)
            b.op("sync", (lambda e, nmch=nmch, nqc=nqc, nbuf=(idx + 2) % 3:
                          e.dma_start(out=sx2t[nbuf][:, :],
                                      in_=r3(x2_in, T)[:, nmch, nqc * TC:(nqc + 1) * TC])),
                 waits=pw, incs=((lds, 16),))
            x2ld[idx + 2] = lds.n
        ps = p_att[idx % 2]
        for ech in range(NCH):
            w = []
            if ech == 0:
                w.append((b.ch["vector"], sc_hist.get(idx - 2, v_o_all)))
                w.append((ldg, ldg_all))
            b.op("tensor", (lambda e, mch=mch, qc=qc, ech=ech, ps=ps: e.matmul(
                ps[:, 0:TC], souw[:, ech * D + mch * P: ech * D + (mch + 1) * P],
                so[:, ech * T + qc * TC: ech * T + qc * TC + TC],
                start=(ech == 0), stop=(ech == NCH - 1))),
                waits=tuple(w),
                incs=((tq, 1),) if ech == NCH - 1 else ())
        td = tq.n
        scw = [(tq, td), (lds, ld_all)]
        if x3sb is None and idx - 2 in st_hist:
            scw.append((stq, st_hist[idx - 2]))
        sc_hist[idx] = b.chain("vector", (lambda e, mch=mch, sbuf=idx % 2, ps=ps: e.tensor_scalar(
            sfin[sbuf][:, :], ps[:, 0:TC], soutb[:, mch:mch + 1], None, ALU.add)),
            waits=tuple(scw))
        if x3sb is not None:
            ev_fin = b.chain("vector", (lambda e, mch=mch, qc=qc, buf=buf,
                                        sbuf=idx % 2: e.tensor_tensor(
                r3(x3sb, T)[:, mch, qc * TC:(qc + 1) * TC],
                sfin[sbuf][:, :], sx2t[buf][:, :], ALU.add)),
                waits=((lds, x2ld[idx]),))
        else:
            ev_fin = b.chain("vector", (lambda e, buf=buf, sbuf=idx % 2: e.tensor_tensor(
                sfin[sbuf][:, :], sfin[sbuf][:, :], sx2t[buf][:, :], ALU.add)),
                waits=((lds, x2ld[idx]),))
            b.op("gpsimd", (lambda e, mch=mch, qc=qc, buf=buf: e.dma_start(
                out=r3(o_x3, T)[:, mch, qc * TC:(qc + 1) * TC], in_=sfin[buf][:, :])),
                waits=((b.ch["vector"], ev_fin), (b.ch["gpsimd"], b.ch["gpsimd"].n)),
                incs=((stq, 16),))
            st_hist[idx] = stq.n
        x2war[idx] = ev_fin
    b.op("gpsimd", lambda e, v=stq.n: e.wait_ge(stq.h, v))
    return dict(stq=stq, lds=lds, ldg=ldg, ldig=ldig, tq=tq, avn=avn)


def build_b():
    fx = Fx()
    emit_b(fx)
    return fx.b.finalize()


def emit_c(fx):
    nc, b = fx.nc, fx.b
    b.pfx = "c_"
    x3_in = fx.dram("o_x3", [P, NCH * T], F32, "in")
    kern_in = fx.dram("kern", [D, D], BF16, "win")
    sin_in = fx.dram("s_in", [1, 1], F32, "win")
    sk_in = fx.dram("s_k", [1, 1], F32, "win")
    l1b_in = fx.dram("l1bias", [P, NCH], BF16, "win")

    o_x4 = fx.dram("o_x4", [P, NCH * T], F32, "result")

    pre_c = getattr(fx, "pre_c", None)
    sx3 = pre_c["sx3"] if pre_c else b.sb("sx3", [P, NCH * T], F32, 0)
    sxq = b.sb("sxq", [P, NCH * T], FP8, 64 * KB)
    skq = b.sb("skq", [P, NCH * D], FP8, 80 * KB)
    skern = pre_c["skern"] if pre_c else b.sb("skern", [P, NCH * D], BF16, 88 * KB)
    sln4 = b.sb("sln4", [P, NCH * TC], BF16, 104 * KB)
    sy = [b.sb("sy0", [P, TC], BF16, 112 * KB),
          b.sb("sy1", [P, TC], BF16, 112 * KB + 1024)]
    sscr = pre_c["sscr"] if pre_c else b.sb("sscr", [1, 8], F32, 169984)
    sl1b = pre_c["sl1b"] if pre_c else b.sb("sl1b", [P, NCH], BF16, 170048)
    scols = b.sb("scols", [P, 8], F32, 170112)
    sl1bf = b.sb("sl1bf", [P, NCH], F32, 170176)
    sqt = b.sb("sqtC", [P, NCH * TC], F32, 114 * KB)
    env = std_env(b, nc, 130 * KB, fx, pre=pre_c)
    tq, lds, ldg, stq = env["tq"], env["lds"], env["ldg"], env["stq"]

    if not pre_c:
        b.dma("sync", lambda e: e.dma_start(out=sx3[:, :], in_=x3_in), lds)
        b.op("sync", lambda e: e.dma_start(out=r3(skern, D),
                                           in_=kern_in.rearrange("(c p) t -> p c t", p=P)),
             incs=((lds, 16),))
        b.dma("sync", lambda e: e.dma_start(out=sscr[0:1, 0:1], in_=sin_in), lds)
        b.dma("sync", lambda e: e.dma_start(out=sscr[0:1, 1:2], in_=sk_in), lds)
        b.dma("sync", lambda e: e.dma_start(out=sl1b[:, :], in_=l1b_in), lds)
    ld0 = lds.n

    b.chain("vector", lambda e: e.tensor_mul(
        sscr[0:1, 2:3], sscr[0:1, 0:1], sscr[0:1, 1:2]), waits=((lds, ld0),))
    b.chain("vector", lambda e: e.reciprocal(sscr[0:1, 3:4], sscr[0:1, 0:1]))
    vsc = b.chain("vector", lambda e: e.reciprocal(sscr[0:1, 4:5], sscr[0:1, 1:2]))
    b.op("tensor", lambda e: e.matmul(env["pk0"][:, 0:5], env["ones_row"][0:1, :],
                                      sscr[0:1, 0:5], start=True, stop=True),
         waits=((b.ch["vector"], vsc),), incs=((tq, 1),))
    tsc = tq.n
    vcols = b.chain("vector", lambda e: e.tensor_copy(scols[:, 0:5], env["pk0"][:, 0:5]),
                    waits=((tq, tsc),))
    b.chain("vector", lambda e: e.tensor_copy(sl1bf[:, :], sl1b[:, :]))
    vkq = b.chain("vector", lambda e: e.tensor_scalar(
        skq[:, :], skern[:, :], scols[:, 4:5], None, ALU.mult))

    for tc in range(NTC):
        tbc = emit_ln_tc(b, env, slc3(sx3, T, tc))
        b.chain("vector", lambda e, tc=tc: e.tensor_tensor(
            r3(sqt, TC), slc3(sx3, T, tc), bc3(env["pbc_m"]), ALU.subtract),
            waits=((tq, tbc),))
        b.chain("vector", lambda e: e.tensor_tensor(
            r3(sln4, TC), r3(sqt, TC), bc3(env["pbc_r"]), ALU.mult))
        b.chain("vector", lambda e, tc=tc: e.tensor_scalar(
            slc3(sxq, T, tc), r3(sln4, TC), scols[:, 3:4], None, ALU.mult))
    vq_all = b.ch["vector"].n

    scv_hist = {}
    for ci, (och, tcq) in enumerate([(o_, t_) for o_ in range(NCH)
                                     for t_ in range(NTC)]):
        key = ci % 2
        ps = env["pk0"] if key == 0 else env["pk1"]
        for ich in range(NCH):
            w = []
            if ich == 0:
                w.append((b.ch["vector"],
                          scv_hist.get(ci - 2, max(vq_all, vkq))))
            b.op("tensor",
                 (lambda e, och=och, tcq=tcq, ich=ich, ps=ps:
                  e.matmul(ps[:, :],
                           skq[:, ich * D + och * P: ich * D + (och + 1) * P],
                           sxq[:, ich * T + tcq * TC: ich * T + tcq * TC + TC],
                           start=(ich == 0), stop=(ich == NCH - 1))),
                 waits=tuple(w),
                 incs=((tq, 1),) if ich == NCH - 1 else ())
        td = tq.n
        scv_hist[ci] = b.chain("vector", (lambda e, ps=ps, key=key: e.tensor_scalar(
            sy[key][:, :], ps[:, :], scols[:, 2:3], None, ALU.mult)),
            waits=((tq, td), (b.ch["vector"], vcols)))
        b.chain("vector", (lambda e, och=och, key=key: e.tensor_scalar(
            sy[key][:, :], sy[key][:, :], sl1bf[:, och:och + 1], None, ALU.add)))
        tile_ev = b.chain("vector", (lambda e, och=och, tcq=tcq, key=key: e.tensor_tensor(
            sx3[:, och * T + tcq * TC: och * T + tcq * TC + TC],
            sx3[:, och * T + tcq * TC: och * T + tcq * TC + TC],
            sy[key][:, :], ALU.add)))
        b.op("gpsimd", (lambda e, och=och, tcq=tcq: e.dma_start(
            out=o_x4[:, och * T + tcq * TC: och * T + tcq * TC + TC],
            in_=sx3[:, och * T + tcq * TC: och * T + tcq * TC + TC])),
            waits=((b.ch["vector"], tile_ev), (b.ch["gpsimd"], b.ch["gpsimd"].n)),
            incs=((stq, 16),))
    b.op("gpsimd", lambda e, v=stq.n: e.wait_ge(stq.h, v))
    return dict(stq=stq)


def build_c():
    fx = Fx()
    emit_c(fx)
    return fx.b.finalize()


def build_full(upto="c", force_out=(), b_pairs=NCH, b_outproj=True, b_loads=5,
               b_gmode="normal"):
    fx = Fx(fused=True)
    fx.b_gmode = b_gmode
    fx.force_out = set(force_out)
    fx.b_pairs = b_pairs
    fx.b_outproj = b_outproj
    fx.b_loads = b_loads
    nc, b = fx.nc, fx.b
    fx.ag_in = nc.dram_tensor("ag_in", [P, NCH * T], BF16, kind="Internal").ap()
    ag_out = nc.dram_tensor("ag_out", [8 * P, NCH * T], BF16,
                            kind="Internal", addr_space="Shared").ap()
    rs_in = nc.dram_tensor("rs_in", [8 * NCH, P], F32, kind="Internal").ap()
    rs_out = nc.dram_tensor("rs_out", [NCH, P], F32, kind="Internal").ap()
    mask_in = fx.dram("hostmask", [1, 8], F32, "win")
    idn_in = fx.dram("idn", [P, P], F32, "win")
    idx_ln = fx.dram("agidx_ln", [P, 1], I32, "win")

    b.pfx = "g_"
    G2 = 64 * KB
    sidn = b.sb("sidn", [P, P], F32, G2)
    smaskr = b.sb("smaskr", [P, NCH], F32, G2 + 512)
    smt = b.sb("smt", [P, 8 * NCH], F32, G2 + 576)
    smtT = b.sb("smtT", [8 * NCH, P], F32, G2 + 832)
    srsb = b.sb("srsb", [NCH, P], F32, G2 + 1344)
    GT = 210560
    spref_f = b.sb("spref_f", [P, NCH], F32, GT)
    sidx1 = b.sb("sidx1", [P, 1], I32, GT + 64)

    glq = b.sem("glq")
    glg = b.sem("glg")
    ccq = b.sem("ccq")
    fx.ext_spref = spref_f

    b.dma("sync", lambda e: e.dma_start(out=sidx1[:, :], in_=idx_ln), glq)

    a1 = emit_a1(fx)
    b.pfx = "g_"
    stot = a1["stot"]
    tq1 = a1["env"]["tq"]
    b.dma("sync", lambda e: e.dma_start(out=sidn[:, :], in_=idn_in), glq,
          waits=((a1["stq"], a1["stq"].n),))
    idn_ld = glq.n
    b.dma("sync", lambda e: e.dma_start(
        out=smaskr[0:1, :], in_=mask_in), glq)
    mask_ld = glq.n
    b.op("tensor", lambda e: e.matmul(fx.PD[:, 0:NCH], a1["env"]["ones_row"][0:1, :],
                                      smaskr[0:1, 0:NCH], start=True, stop=True),
         waits=((glq, mask_ld),), incs=((tq1, 1),))
    tmr = tq1.n
    vmr = b.chain("vector", lambda e: e.tensor_copy(smaskr[:, :], fx.PD[:, 0:NCH]),
                  waits=((tq1, tmr),))
    for s_ in range(NCH):
        b.chain("vector", lambda e, s_=s_: e.tensor_tensor(
            smt[:, s_ * NCH:(s_ + 1) * NCH], stot[:, :],
            smaskr[:, s_:s_ + 1].broadcast_to([P, NCH]), ALU.mult))
    vmt = b.ch["vector"].n
    b.op("tensor", lambda e: e.transpose(fx.PD[0:8 * NCH, TC:TC + P], smt[:, :], sidn[:, :]),
         waits=((b.ch["vector"], vmt), (glq, idn_ld)), incs=((tq1, 1),))
    tmt = tq1.n
    vmtT = b.chain("vector", lambda e: e.tensor_copy(smtT[:, :], fx.PD[0:8 * NCH, TC:TC + P]),
                   waits=((tq1, tmt),))
    b.dma("gpsimd", lambda e: e.dma_start(out=rs_in, in_=smtT[:, :]), glg,
          waits=((b.ch["vector"], vmtT), (b.ch["gpsimd"], b.ch["gpsimd"].n)))
    rsw = glg.n
    b.op("gpsimd", lambda e: e.collective_compute(
        "ReduceScatter", ALU.add, replica_groups=[list(range(8))],
        ins=[rs_in], outs=[rs_out]),
        waits=((glg, rsw),), incs=((ccq, 1),))
    b.dma("gpsimd", lambda e: e.dma_start(out=srsb[:, :], in_=rs_out), glg,
          waits=((ccq, 1),))
    rsrd = glg.n
    b.op("tensor", lambda e: e.transpose(fx.PD[0:P, TC:TC + NCH], srsb[:, :],
                                         sidn[0:NCH, 0:NCH]),
         waits=((glg, rsrd),), incs=((tq1, 1),))
    tpr = tq1.n
    vpr = b.chain("vector", lambda e: e.tensor_copy(spref_f[:, :], fx.PD[0:P, TC:TC + NCH]),
                  waits=((tq1, tpr),))
    b.pfx = "a2_"
    pre_sx = b.sb("sx", [P, NCH * T], F32, 0)
    pre_sfkv = b.sb("sfkv", [P, NCH * T], BF16, 96 * KB)
    pre_lds = b.sem("lds")
    b.dma("sync", lambda e: e.dma_start(
        out=pre_sx[:, :], in_=fx.dram("o_xmid", [P, NCH * T], F32, "in")),
        pre_lds, waits=((a1["stq"], a1["stq"].n),))
    b.dma("sync", lambda e: e.dma_start(
        out=pre_sfkv[:, :], in_=fx.dram("o_fkv", [P, NCH * T], BF16, "in")),
        pre_lds)
    fx.pre = {"sx": pre_sx, "sfkv": pre_sfkv, "lds": pre_lds}
    b.pfx = "g_"
    if upto == "a1":
        return fx.b.finalize()
    emit_barrier(fx, dma_sems=(a1["lds"], a1["ldg"], a1["stq"], glq, glg))

    a2 = emit_a2(fx)
    b.pfx = "g_"
    b.op("gpsimd", lambda e: e.collective_compute(
        "AllGather", ALU.bypass, replica_groups=[list(range(8))],
        ins=[fx.ag_in], outs=[ag_out]),
        waits=((a2["stq"], a2["stq"].n), (b.ch["gpsimd"], b.ch["gpsimd"].n)),
        incs=((ccq, 1),))
    agq = ccq
    fx.ag = (agq, ag_out, sidx1, a2["stq"])
    if upto == "a2":
        return fx.b.finalize()
    emit_barrier(fx, dma_sems=(a2["lds"], a2["ldg"], a2["stq"], glq, glg))
    if upto == "bar2":
        return fx.b.finalize()

    b.pfx = "c_"
    pc_sx3 = b.sb("sx3", [P, NCH * T], F32, 0)
    pc_skern = b.sb("skern", [P, NCH * D], BF16, 88 * KB)
    pc_sscr = b.sb("sscr", [1, 8], F32, 169984)
    pc_sl1b = b.sb("sl1b", [P, NCH], BF16, 170048)
    pc_lds = b.sem("lds")
    fx.pre_c = {"sx3": pc_sx3, "skern": pc_skern, "sscr": pc_sscr,
                "sl1b": pc_sl1b, "lds": pc_lds}
    b.pfx = "g_"
    bb = emit_b(fx)
    b.pfx = "c_"
    b.op("sync", lambda e: e.dma_start(
        out=r3(pc_skern, D),
        in_=fx.dram("kern", [D, D], BF16, "win").rearrange("(c p) t -> p c t", p=P)),
        waits=((bb["tq"], bb["avn"]), (pc_lds, pc_lds.n)), incs=((pc_lds, 16),))
    b.dma("sync", lambda e: e.dma_start(
        out=pc_sscr[0:1, 0:1], in_=fx.dram("s_in", [1, 1], F32, "win")), pc_lds)
    b.dma("sync", lambda e: e.dma_start(
        out=pc_sscr[0:1, 1:2], in_=fx.dram("s_k", [1, 1], F32, "win")), pc_lds)
    b.dma("sync", lambda e: e.dma_start(
        out=pc_sl1b[:, :], in_=fx.dram("l1bias", [P, NCH], BF16, "win")), pc_lds)
    b.pfx = "g_"
    if upto == "b":
        return fx.b.finalize()
    emit_barrier(fx, dma_sems=(bb["lds"], bb["ldg"], bb["stq"], bb["ldig"]))

    emit_c(fx)
    return fx.b.finalize()


FUSED = True
_NC_CACHE = {}


def _get_nc(name):
    if name not in _NC_CACHE:
        _NC_CACHE[name] = {"a1": build_a1, "a2": build_a2, "b": build_b,
                           "c": build_c, "full": build_full}[name]()
    return _NC_CACHE[name]


def _host_arrays(inputs):
    x = np.asarray(inputs["L0_attn_x"])
    Cb, IRb = get_consts()
    h = {}
    h["Cb"] = np.asarray(Cb)
    h["IRb"] = np.asarray(IRb)
    h["Wq"] = np.asarray(inputs["L0_attn_Wq"])
    h["Wk"] = np.asarray(inputs["L0_attn_Wk"])
    h["Wv"] = np.asarray(inputs["L0_attn_Wv"])
    h["ternT"] = np.ascontiguousarray(np.asarray(inputs["L0_ffn_ternary_weight"]).T)
    h["l0bias"] = vcol(np.asarray(inputs["L0_ffn_bias"]))
    h["ascale"] = np.asarray(inputs["L0_ffn_act_scale"], np.float32).reshape(1, 1)
    watt = np.asarray(inputs["L1_attn_w_att_val"])
    ws = np.zeros((P, NCH * 64), np.float32)
    for j in range(NCH):
        ws[0:64, j * 64:(j + 1) * 64] = watt[2 * j]
        ws[64:128, j * 64:(j + 1) * 64] = watt[2 * j + 1]
    h["watt"] = ws
    h["wvT"] = np.ascontiguousarray(np.asarray(inputs["L1_attn_wv_w"]).T)
    h["wvb"] = np.asarray(inputs["L1_attn_wv_b"]).reshape(1, D)
    h["outwT"] = np.ascontiguousarray(np.asarray(inputs["L1_attn_out_w"]).T)
    h["outb"] = vcol(np.asarray(inputs["L1_attn_out_b"]))
    h["kern"] = np.asarray(inputs["L1_ffn_kernel"])
    h["s_in"] = np.asarray(inputs["L1_ffn_input_scale"], np.float32).reshape(1, 1)
    h["s_k"] = np.asarray(inputs["L1_ffn_kernel_scale"], np.float32).reshape(1, 1)
    h["l1bias"] = np.asarray(inputs["L1_ffn_bias"]).reshape(NCH, P).T.copy()
    xT = []
    for c in range(8):
        bb, hh = c // 2, c % 2
        xT.append(np.ascontiguousarray(x[bb, hh * T:(hh + 1) * T, :].T))
    return h, xT


def kernel(**inputs):
    h, xT = _host_arrays(inputs)
    cores = list(range(8))
    if FUSED:
        idn = np.eye(P, dtype=np.float32)
        in_maps = []
        for c in cores:
            mask = np.zeros((1, 8), np.float32)
            if c % 2 == 0:
                mask[0, c + 1] = 1.0
            pr = c ^ 1
            m = dict(h)
            m["xT"] = xT[c]
            m["hostmask"] = mask
            m["idn"] = idn
            m["agidx_ln"] = (pr * P + np.arange(P)).astype(np.int32).reshape(P, 1)
            in_maps.append(m)
        res = run_bass_kernel_spmd(_get_nc("full"), in_maps, core_ids=cores).results
        out = np.empty((4, 2 * T, D), np.float32)
        for c in cores:
            bb, hh = c // 2, c % 2
            out[bb, hh * T:(hh + 1) * T, :] = fm_inv(np.asarray(res[c]["o_x4"]), T).T
        return out

    in1 = [{"xT": xT[c], "Wk": h["Wk"], "Wv": h["Wv"], "Cb": h["Cb"]} for c in cores]
    r1 = run_bass_kernel_spmd(_get_nc("a1"), in1, core_ids=cores).results
    in2 = []
    for c in cores:
        pref = (np.zeros((P, NCH), np.float32) if c % 2 == 0
                else np.asarray(r1[c - 1]["o_tot"], np.float32))
        in2.append({"o_fkv": r1[c]["o_fkv"], "o_ln1": r1[c]["o_ln1"],
                    "o_xmid": r1[c]["o_xmid"], "prefix": pref,
                    "Wq": h["Wq"], "Cb": h["Cb"], "IRb": h["IRb"],
                    "ternT": h["ternT"], "l0bias": h["l0bias"],
                    "ascale": h["ascale"], "wvT": h["wvT"], "wvb": h["wvb"]})
    r2 = run_bass_kernel_spmd(_get_nc("a2"), in2, core_ids=cores).results
    in3 = []
    for c in cores:
        p = c ^ 1
        lnf = np.concatenate([np.asarray(r2[c]["o_ln3"]),
                              np.asarray(r2[p]["o_ln3"])], axis=1)
        vx2 = np.stack([np.asarray(r2[c]["o_vx"]),
                        np.asarray(r2[p]["o_vx"])])
        vxf = np.ones((2, P, T // P, 16, GW), vx2.dtype)
        vxf[..., 0:64] = vx2.reshape(2, P, T // P, 16, 64)
        vxf = vxf.transpose(1, 0, 2, 3, 4).reshape(P, 2 * AGW)
        in3.append({"ln3full": np.ascontiguousarray(lnf),
                    "vxfull": np.ascontiguousarray(vxf),
                    "o_x2": r2[c]["o_x2"], "watt": h["watt"],
                    "outwT": h["outwT"], "outb": h["outb"]})
    r3_ = run_bass_kernel_spmd(_get_nc("b"), in3, core_ids=cores).results
    in4 = [{"o_x3": r3_[c]["o_x3"], "kern": h["kern"], "s_in": h["s_in"],
            "s_k": h["s_k"], "l1bias": h["l1bias"]} for c in cores]
    r4 = run_bass_kernel_spmd(_get_nc("c"), in4, core_ids=cores).results
    out = np.empty((4, 2 * T, D), np.float32)
    for c in cores:
        bb, hh = c // 2, c % 2
        out[bb, hh * T:(hh + 1) * T, :] = fm_inv(np.asarray(r4[c]["o_x4"]), T).T
    return out
